# version 07
# baseline (speedup 1.0000x reference)
import math

import numpy as np
import ml_dtypes

import concourse.bass as bass
import concourse.mybir as mybir
from concourse.bass_utils import run_bass_kernel_spmd
from concourse.dve_op import DveOp
from concourse.dve_spec import Spec, Src0, C0, C1, C2, C3, lower as dve_lower, sq
from concourse.dve_table_gen import dve_ver_for
import concourse.dve_ops as dve_ops_mod

F32 = mybir.dt.float32
BF16 = mybir.dt.bfloat16
A = mybir.AluOpType
AF = mybir.ActivationFunctionType
AX = mybir.AxisListType.X
BF = ml_dtypes.bfloat16

B_, SEQ, E = 8, 4096, 1024
ROWS = B_ * SEQ
TPC = SEQ // 128
STILES = ROWS // 128
I_FFN = 2816
HALFPI = float(np.float32(math.pi / 2.0))
SIXTH = float(np.float32(1.0 / 6.0))
N_CORES = 8


def _register_op(name, spec, reference):
    for op_ in dve_ops_mod.OPS:
        if op_.name == name:
            return op_
    ver = dve_ver_for("TRN2")
    sha = dve_lower(spec, ver=ver, perf_en=False, name=name).sha(ver)
    op_ = DveOp(name, spec, subdim=False, uops_sha={ver: sha}, reference=reference)
    dve_ops_mod.OPS.append(op_)
    row = max(dve_ops_mod._SUB_OPCODE_FOR_NAME.values()) + 1
    assert row < 0x20, "custom-DVE opcode rows exhausted"
    dve_ops_mod._SUB_OPCODE_FOR_NAME[name] = row
    dve_ops_mod.CUSTOM_DVE_REFS[name] = reference
    return op_


def _fp4a_ref(in0, in1, s0, s1, imm2):
    y = in0 * in0
    return (y > s0) * 1.0 + (y > s1) + (y > imm2) + (y > in1)


def _fp4b_ref(in0, in1, s0, s1, imm2):
    y = in0 * in0
    return 2.0 * ((y > s0) * 1.0 + (y > s1) + 2.0 * (y > imm2))


_y = sq(Src0)
FP4A = _register_op(
    "ANT_FP4A", Spec(body=(_y > C0) + (_y > C1) + ((_y > C2) + (_y > C3))), _fp4a_ref
)
_c7 = _y > C2
_inner = ((_y > C0) + (_y > C1)) + (_c7 + _c7)
FP4B = _register_op("ANT_FP4B", Spec(body=_inner + _inner), _fp4b_ref)


def _build_consts():
    k = np.arange(513, dtype=np.float64)
    n = np.arange(1024, dtype=np.float64)
    ang = 2.0 * np.pi * np.outer(k, n) / 1024.0
    Acp = np.zeros((640, 1024), dtype=np.float64)
    Asp = np.zeros((640, 1024), dtype=np.float64)
    Acp[:513] = np.cos(ang)
    Asp[:513] = np.sin(ang)
    ac_ch = np.ascontiguousarray(Acp.reshape(5, 128, 1024).transpose(1, 0, 2)).astype(np.float32)
    as_ch = np.ascontiguousarray(Asp.reshape(5, 128, 1024).transpose(1, 0, 2)).astype(np.float32)
    Ac_hi = ac_ch.astype(BF)
    As_hi = as_ch.astype(BF)
    AcT = np.ascontiguousarray(Acp.T.reshape(8, 128, 640).transpose(1, 0, 2)).astype(np.float32)
    AsT = np.ascontiguousarray(Asp.T.reshape(8, 128, 640).transpose(1, 0, 2)).astype(np.float32)
    AcT_hi = AcT.astype(BF)
    AsT_hi = AsT.astype(BF)
    AcT_lo = (AcT - AcT_hi.astype(np.float32)).astype(BF)
    AsT_lo = (AsT - AsT_hi.astype(np.float32)).astype(BF)
    cw = np.zeros(640, dtype=np.float64)
    cw[:513] = 2.0 / 1024.0
    cw[0] = 1.0 / 1024.0
    cw[512] = 1.0 / 1024.0
    cconst = np.ascontiguousarray(cw.reshape(5, 128).T.astype(np.float32))
    kk = np.fft.fftfreq(1024).astype(np.float32)[:513].copy()
    kk[512] = np.float32(-0.5)
    ka = (np.abs(kk) + np.float32(1e-8)).astype(np.float32)
    ka2 = (ka * ka).astype(np.float32)
    kap = np.zeros(640, dtype=np.float32)
    ka2p = np.zeros(640, dtype=np.float32)
    kap[:513] = ka
    ka2p[:513] = ka2
    ka_ch = np.ascontiguousarray(kap.reshape(5, 128).T)
    ka2_ch = np.ascontiguousarray(ka2p.reshape(5, 128).T)
    lx = np.log(np.array([2.0, 4.0, 8.0, 16.0, 32.0], dtype=np.float32)).astype(np.float64)
    den = 5.0 * np.sum(lx * lx) - np.sum(lx) ** 2
    acoef = np.ascontiguousarray((-(5.0 * lx - np.sum(lx)) / den).astype(np.float32)[None, :])
    ident = np.eye(128, dtype=np.float32).astype(BF)
    _A = np.array(
        [[1,1,1,0,1,1,1,0,0,0,1,1],[1,1,0,1,1,1,0,0,0,1,1,0],[1,0,1,1,1,0,0,0,1,1,0,1],
         [0,1,1,1,0,0,0,1,1,0,1,1],[0,0,0,0,1,1,1,1,1,1,1,1],[1,1,1,1,0,0,0,0,0,0,0,0],
         [0,0,0,0,1,1,1,1,0,0,0,0],[1,1,0,0,1,1,0,0,1,1,0,0],[1,0,1,0,1,0,1,0,1,0,1,0],
         [0,1,0,1,0,1,0,1,0,1,0,1],[1,0,0,1,1,0,0,1,1,0,0,1],[0,1,1,0,0,1,1,0,0,1,1,0]],
        dtype=np.float32)
    H = np.concatenate([_A.T, np.eye(12, dtype=np.float32)], axis=1)
    HT = np.ascontiguousarray(H.T).astype(BF)
    Ac_lo = (ac_ch - Ac_hi.astype(np.float32)).astype(BF)
    As_lo = (as_ch - As_hi.astype(np.float32)).astype(BF)
    return dict(cAchi=Ac_hi, cAshi=As_hi, cAclo=Ac_lo, cAslo=As_lo,
                cAcThi=AcT_hi, cAcTlo=AcT_lo, cAsThi=AsT_hi, cAsTlo=AsT_lo,
                ccw=cconst, cka=ka_ch, cka2=ka2_ch, cacoef=acoef,
                cident=ident, cHT=HT)


_CONSTS = _build_consts()
_NC_CACHE = {}


def _v3(ap, n=16):
    return ap.rearrange("p (s n) -> p s n", n=n)


def _b16(small_ap, S):
    return small_ap.rearrange("p (s o) -> p s o", o=1).broadcast_to([128, S, 16])


def build_program():
    nc = bass.Bass("TRN2", target_bir_lowering=False, debug=False)

    xb = nc.dram_tensor("xb", [SEQ, E], F32, kind="ExternalInput").ap()
    xfull = nc.dram_tensor("xfull", [ROWS, E], F32, kind="ExternalInput").ap()
    wg = nc.dram_tensor("wg", [I_FFN, E], F32, kind="ExternalInput").ap()
    wu = nc.dram_tensor("wu", [I_FFN, E], F32, kind="ExternalInput").ap()
    wd = nc.dram_tensor("wd", [E, I_FFN], F32, kind="ExternalInput").ap()
    wel = nc.dram_tensor("wel", [E, 24], F32, kind="ExternalInput").ap()
    bel = nc.dram_tensor("bel", [1, 24], F32, kind="ExternalInput").ap()
    wle = nc.dram_tensor("wle", [24, E], F32, kind="ExternalInput").ap()
    ble = nc.dram_tensor("ble", [1, E], F32, kind="ExternalInput").ap()
    yout = nc.dram_tensor("yout", [SEQ, E], F32, kind="ExternalOutput").ap()
    ci = {}
    for name, arr in _CONSTS.items():
        dt = BF16 if arr.dtype == BF else F32
        ci[name] = nc.dram_tensor(name, list(arr.shape), dt, kind="ExternalInput").ap()

    xqT_d = nc.dram_tensor("xqT_d", [TPC, 128, 8, 128], BF16, kind="Internal").ap()
    xhT_d = nc.dram_tensor("xhT_d", [TPC, 128, 8, 128], BF16, kind="Internal").ap()
    xlT_d = nc.dram_tensor("xlT_d", [TPC, 128, 8, 128], BF16, kind="Internal").ap()
    g_d = nc.dram_tensor("g_d", [TPC, 128, E], F32, kind="Internal").ap()
    vec_d = nc.dram_tensor("vec_d", [4, 128], F32, kind="Internal").ap()

    from contextlib import ExitStack
    es = ExitStack()
    with es:
        WGb = es.enter_context(nc.sbuf_tensor("WGb", [128, 8, I_FFN], BF16))
        WUb = es.enter_context(nc.sbuf_tensor("WUb", [128, 8, I_FFN], BF16))
        WDb = es.enter_context(nc.sbuf_tensor("WDb", [128, 22, E], BF16))
        Chi = es.enter_context(nc.sbuf_tensor("Chi", [128, 8, E], BF16))
        Mhi = es.enter_context(nc.sbuf_tensor("Mhi", [128, 8, 24], BF16))
        Mlo = es.enter_context(nc.sbuf_tensor("Mlo", [128, 8, 24], BF16))
        AF_ = es.enter_context(nc.sbuf_tensor("AF_", [128, 6144], F32))
        ABF = es.enter_context(nc.sbuf_tensor("ABF", [128, 13824], BF16))
        ident = es.enter_context(nc.sbuf_tensor("ident", [128, 128], BF16))
        HTs = es.enter_context(nc.sbuf_tensor("HTs", [24, 12], BF16))
        negb = es.enter_context(nc.sbuf_tensor("negb", [128, 64], F32))
        c3buf = es.enter_context(nc.sbuf_tensor("c3buf", [128, 1], F32))
        sc1 = es.enter_context(nc.sbuf_tensor("sc1", [128, 4], F32))
        cstb = es.enter_context(nc.sbuf_tensor("cstb", [128, 8], F32))
        dsm = es.enter_context(nc.sbuf_tensor("dsm", [1, 16], F32))
        dly = es.enter_context(nc.sbuf_tensor("dly", [1, 8], F32))
        dly2 = es.enter_context(nc.sbuf_tensor("dly2", [1, 8], F32))
        dcnt = es.enter_context(nc.sbuf_tensor("dcnt", [1, 8], F32))
        dll = es.enter_context(nc.sbuf_tensor("dll", [1, 8], F32))
        dslp = es.enter_context(nc.sbuf_tensor("dslp", [1, 8], F32))
        acf = es.enter_context(nc.sbuf_tensor("acf", [1, 5], F32))
        Db = es.enter_context(nc.sbuf_tensor("Db", [128, 1], F32))
        alf0 = es.enter_context(nc.sbuf_tensor("alf0", [128, 1], F32))
        gtm = es.enter_context(nc.sbuf_tensor("gtm", [128, 1], F32))
        alf = es.enter_context(nc.sbuf_tensor("alf", [128, 1], F32))
        bet = es.enter_context(nc.sbuf_tensor("bet", [128, 1], F32))
        gam = es.enter_context(nc.sbuf_tensor("gam", [128, 1], F32))
        kas = es.enter_context(nc.sbuf_tensor("kas", [128, 5], F32))
        ka2s = es.enter_context(nc.sbuf_tensor("ka2s", [128, 5], F32))
        cws = es.enter_context(nc.sbuf_tensor("cws", [128, 5], F32))
        tmp5 = es.enter_context(nc.sbuf_tensor("tmp5", [128, 5], F32))
        t5c = es.enter_context(nc.sbuf_tensor("t5c", [128, 5], F32))
        ub5 = es.enter_context(nc.sbuf_tensor("ub5", [128, 5], F32))
        fr1 = es.enter_context(nc.sbuf_tensor("fr1", [128, 5], F32))
        fr2 = es.enter_context(nc.sbuf_tensor("fr2", [128, 5], F32))
        cfr = es.enter_context(nc.sbuf_tensor("cfr", [128, 5], F32))
        WelS = es.enter_context(nc.sbuf_tensor("WelS", [128, 8, 24], F32))
        WelH = es.enter_context(nc.sbuf_tensor("WelH", [128, 8, 24], BF16))
        WelL = es.enter_context(nc.sbuf_tensor("WelL", [128, 8, 24], BF16))
        AWc = es.enter_context(nc.sbuf_tensor("AWc", [128, 5, 64], F32))
        AWs = es.enter_context(nc.sbuf_tensor("AWs", [128, 5, 64], F32))
        Rh = es.enter_context(nc.sbuf_tensor("Rh", [128, 5, 32], BF16))
        Rl = es.enter_context(nc.sbuf_tensor("Rl", [128, 5, 32], BF16))
        Mf = es.enter_context(nc.sbuf_tensor("Mf", [128, 8, 64], F32))
        bits = es.enter_context(nc.sbuf_tensor("bits", [128, 24], BF16))
        corr = es.enter_context(nc.sbuf_tensor("corr", [128, 24], BF16))
        pg0 = es.enter_context(nc.psum_tensor("pg0", [128, 512], F32))
        pg1 = es.enter_context(nc.psum_tensor("pg1", [128, 512], F32))
        py0 = es.enter_context(nc.psum_tensor("py0", [128, 512], F32))
        py1 = es.enter_context(nc.psum_tensor("py1", [128, 512], F32))
        pdn = es.enter_context(nc.psum_tensor("pdn", [128, 1024], F32))
        pT = es.enter_context(nc.psum_tensor("pT", [128, 1024], BF16))
        sLd = es.enter_context(nc.semaphore("sLd"))
        sSt = es.enter_context(nc.semaphore("sSt"))
        sV = es.enter_context(nc.semaphore("sV"))
        sA = es.enter_context(nc.semaphore("sA"))
        sT = es.enter_context(nc.semaphore("sT"))
        sSy = es.enter_context(nc.semaphore("sSy"))
        sG = es.enter_context(nc.semaphore("sG"))
        block = es.enter_context(nc.Block())
        XT0 = AF_[:, 0:1024]
        XT1 = AF_[:, 1024:2048]
        QW = AF_[:, 2048:3072]
        SIL = AF_[:, 3072:3584]
        HB = AF_[:, 3584:4096]
        SMV3 = AF_[:, 4096:4608].rearrange("p (t o) -> p t o", o=2)
        SMN = AF_[:, 4608:4864]
        SMX = AF_[:, 4864:5120]
        XNF = AF_[:, 5120:6144]
        BLEB = AF_[:, 5120:6144]
        SSB = AF_[:, 4096:5120]
        OSB = AF_[:, 3072:4096]
        NEGM = AF_[:, 2048:2304]
        MSQ = AF_[:, 2304:2560]
        VARE = AF_[:, 2560:2816]
        SQV = AF_[:, 2816:3072]
        R0G = AF_[:, 3072:3328]
        RRG = AF_[:, 3328:3584]
        BVG = AF_[:, 3584:3840]
        RSTDG = AF_[:, 3840:4096]
        OWNSUM = AF_[:, 5120:5184]
        OWNSQ = AF_[:, 5184:5248]
        NEGM64 = AF_[:, 3072:3136]
        RSTD64 = AF_[:, 3136:3200]
        MSQ64 = AF_[:, 5248:5312]
        VARE64 = AF_[:, 5312:5376]
        SQ64 = AF_[:, 5376:5440]
        R064 = AF_[:, 5440:5504]
        RR64 = AF_[:, 5504:5568]
        B64 = AF_[:, 5568:5632]
        MfC = AF_[:, 2048:2240].rearrange("p (c n) -> p c n", n=24)
        Rf = AF_[:, 2048:2208].rearrange("p (c n) -> p c n", n=32)
        qm = AF_[:, 4096:4160]
        qsb = AF_[:, 4160:4224]
        qrb = AF_[:, 4224:4288]
        qs2 = AF_[:, 4288:4352]
        XROW = AF_[0:1, 0:1024]
        XNROW = AF_[0:1, 1024:2048]
        ZROW = AF_[0:1, 3072:4096]
        MROW = AF_[0:1, 0:1024]
        MNR = AF_[0:1, 2048:2176]
        MXR = AF_[0:1, 2176:2304]

        S1 = ABF[:, 0:5120].rearrange("p (c n) -> p c n", n=1024)
        S2 = ABF[:, 5120:10240].rearrange("p (c n) -> p c n", n=1024)
        S1f = ABF[:, 0:5120]
        S2f = ABF[:, 5120:10240]
        AcT_hi_v = S1f.rearrange("p (c n) -> p c n", n=640)
        AcT_lo_v = S2f.rearrange("p (c n) -> p c n", n=640)
        QGA = ABF[:, 0:1024]
        QGB = ABF[:, 1024:2048]
        QSG = ABF[:, 2048:3072]
        QOUT = ABF[:, 3072:5888]
        EV1 = ABF[:, 5888:6912]
        EV2 = ABF[:, 6912:7936]
        TSTG = ABF[:, 7936:8960]
        GSTGF = AF_[:, 5120:6144]
        HQT = ABF[:, 8960:11776].rearrange("p (c n) -> p c n", n=128)
        XQT0 = ABF[:, 11776:12800].rearrange("p (c n) -> p c n", n=128)
        XQT1 = ABF[:, 12800:13824].rearrange("p (c n) -> p c n", n=128)
        XHT = ABF[:, 5888:6912].rearrange("p (c n) -> p c n", n=128)
        XLT = ABF[:, 6912:7936].rearrange("p (c n) -> p c n", n=128)
        GTF = AF_[:, 2048:3072]
        GW = AF_[:, 1024:1088]
        GU = AF_[:, 1088:1152]
        GA = AF_[:, 1152:1216]
        GB = AF_[:, 1024:1088]
        GF0 = AF_[:, 1088:1152]
        GF = AF_[:, 1152:1216]
        BITSF = AF_[:, 1216:1280]
        GD = AF_[:, 1024:1088]
        GE = AF_[:, 1088:1152]
        CORRF = AF_[:, 1152:1216]
        synpm = AF_[:, 1280:1344]
        maskb = AF_[:, 1344:1408]
        WleB = ABF[0:24, 8960:9984]
        bitsT = ABF[0:24, 9984:10112]
        corrT = ABF[0:24, 10112:10240]
        NB05 = cstb[:, 0:1]
        NB5 = cstb[:, 1:2]
        HPIN = cstb[:, 2:3]
        HPIP = cstb[:, 3:4]

        pAW = pg0[:, 0:24]
        pAW64 = pg0[:, 0:64]
        latP = pg0[:, 0:24]
        synP = py0[:, 0:12]
        psA = pT[0:24, 0:128]
        psB = pT[0:24, 128:256]

        Q = {"sy": [], "v": [], "sa": [], "te": [], "g": []}
        Cn = {"ld": 0, "st": 0, "v": 0, "a": 0, "t": 0, "sy": 0, "g": 0}
        SEM = {"ld": sLd, "st": sSt, "v": sV, "a": sA, "t": sT, "sy": sSy, "g": sG}

        def em(eng, fn):
            Q[eng].append(fn)

        def dma_in(dst, src):
            Cn["ld"] += 16
            tgt = Cn["ld"]
            em("sy", lambda e, d=dst, s=src: e.dma_start(out=d, in_=s).then_inc(sLd, 16))
            return tgt

        def wait(eng, sem_key, val):
            em(eng, lambda e, k=sem_key, vv=val: e.wait_ge(SEM[k], vv))

        def op(eng, fn, inc_key=None):
            if inc_key is None:
                em(eng, fn)
                return None
            Cn[inc_key] += 1
            tgt = Cn[inc_key]

            def _f(e, f=fn, k=inc_key):
                f(e).then_inc(SEM[k], 1)

            em(eng, _f)
            return tgt

        def dma_out(eng_key, dst, src):
            Cn["st"] += 16
            tgt = Cn["st"]
            em(eng_key, lambda e, d=dst, s=src: e.dma_start(out=d, in_=s).then_inc(sSt, 16))
            return tgt

        def barrier():
            dp = {
                "sy": lambda e: e.nop(),
                "v": lambda e: e.engine_nop(),
                "sa": lambda e: e.copy(cstb[:, 7:8], c3buf[:]),
                "te": lambda e: e.ldweights(ident[:]),
                "g": lambda e: e.engine_nop(),
            }
            marks = {}
            for ek, sk in (("sy", "sy"), ("v", "v"), ("sa", "a"), ("te", "t"), ("g", "g")):
                Cn[sk] += 1
                marks[ek] = Cn[sk]
                em(ek, lambda e, k=sk, f=dp[ek]: f(e).then_inc(SEM[k], 1))
            for ek in ("sy", "v", "sa", "te", "g"):
                for ok, sk in (("sy", "sy"), ("v", "v"), ("sa", "a"), ("te", "t"), ("g", "g")):
                    if ok != ek:
                        wait(ek, sk, marks[ok])

        def emit_quant(src, dst, W):
            S = W // 16

            def _q(v, s=src, d=dst, W=W, S=S):
                v.tensor_reduce(qm[:, 0:S], _v3(s), AX, A.max, apply_absolute_value=True)
                v.tensor_copy(sc1[:, 3:4], c3buf[:])
                v.tensor_scalar(qsb[:, 0:64], qm[:, 0:64], SIXTH, 1e-12, A.mult, A.max)
                v.reciprocal(qrb[:, 0:64], qsb[:, 0:64])
                v.tensor_scalar(qs2[:, 0:64], qsb[:, 0:64], 0.5, None, A.mult)
                v.tensor_tensor(_v3(QW[:, :W]), _v3(s), _b16(qrb[:, 0:S], S), A.mult)
                v._custom_dve(FP4A, out=QGA[:, :W], in0=QW[:, :W], in1=c3buf[:],
                              s0=0.0625, s1=0.5625, imm2=1.5625)
                v._custom_dve(FP4B, out=QGB[:, :W], in0=QW[:, :W],
                              s0=6.25, s1=12.25, imm2=25.0)
                v.tensor_tensor(QGA[:, :W], QGA[:, :W], QGB[:, :W], A.add)
                v.tensor_tensor(QGA[:, :W], QGA[:, :W], QSG[:, :W], A.mult)
                return v.tensor_tensor(_v3(d), _v3(QGA[:, :W]), _b16(qs2[:, 0:S], S), A.mult)

            return _q

        for dst, key in ((ident[:], "cident"), (HTs[:], "cHT"), (acf[:], "cacoef"),
                         (kas[:], "cka"), (ka2s[:], "cka2"), (cws[:], "ccw")):
            dma_in(dst, ci[key])
        dma_in(WelS[:], wel.rearrange("(c p) e -> p c e", p=128))
        t_bel = dma_in(negb[:, 0:24], bel.partition_broadcast(128))
        t_act1 = dma_in(AcT_hi_v, ci["cAcThi"])
        t_act2 = dma_in(AcT_lo_v, ci["cAcTlo"])
        consts_ld = Cn["ld"]

        wait("v", "ld", consts_ld)
        op("v", lambda v: v.memset(c3buf[:], 3.0625))
        wait("g", "ld", t_bel)

        def _gsetup(g):
            g.memset(negb[:, 24:64], 0.0)
            g.tensor_scalar(negb[:, 0:24], negb[:, 0:24], -1.0, None, A.mult)
            g.memset(cstb[:, 0:1], -0.5)
            g.memset(cstb[:, 1:2], -5.0)
            g.memset(cstb[:, 2:3], -HALFPI)
            return g.memset(cstb[:, 3:4], HALFPI)

        g_setup = op("g", _gsetup, "g")
        op("v", lambda v: v.tensor_copy(WelH[:], WelS[:]))
        v_wel = op("v", lambda v: v.scalar_tensor_tensor(
            WelL[:].rearrange("p c n -> p (c n)"),
            WelH[:].rearrange("p c n -> p (c n)"), -1.0,
            WelS[:].rearrange("p c n -> p (c n)"), A.mult, A.add), "v")

        wait("te", "v", v_wel)
        wait("te", "ld", t_act2)
        aw_last_t = None
        for nm, AW in (("c", AWc), ("s", AWs)):
            if nm == "s":
                wait("sy", "t", aw_last_t)
                t_ast1 = dma_in(AcT_hi_v, ci["cAsThi"])
                t_ast2 = dma_in(AcT_lo_v, ci["cAsTlo"])
                wait("te", "ld", t_ast2)
            for kc in range(5):
                ksl = slice(kc * 128, (kc + 1) * 128)
                mms = []
                for cc in range(8):
                    mms.append((AcT_hi_v[:, cc, ksl], WelH[:, cc, :]))
                    mms.append((AcT_hi_v[:, cc, ksl], WelL[:, cc, :]))
                    mms.append((AcT_lo_v[:, cc, ksl], WelH[:, cc, :]))
                t_mark = None
                for i, (lh, rh) in enumerate(mms):
                    def _mm(te, lh=lh, rh=rh, st=(i == 0), sp=(i == len(mms) - 1)):
                        return te.matmul(pAW, lh, rh, start=st, stop=sp)
                    if i < len(mms) - 1:
                        op("te", _mm)
                    else:
                        t_mark = op("te", _mm, "t")
                aw_last_t = t_mark
                wait("sa", "t", t_mark)
                a_mark = op("sa", lambda s, d=AW[:, kc, :]: s.copy(d, pAW64), "a")
                wait("te", "a", a_mark)

        op("v", lambda v: v.memset(OWNSUM[:, 0:32], 1.0))
        op("v", lambda v: v.memset(OWNSQ[:, 0:32], 1.0))
        v_stats_base = Cn["v"]
        NT_ALL = STILES + TPC
        for t in range(NT_ALL):
            buf = XT0 if t % 2 == 0 else XT1
            if t >= 2:
                wait("sy", "v", v_stats_base + (t - 1))
            if t < STILES:
                t_ld = dma_in(buf, xfull[t * 128:(t + 1) * 128, :])
            else:
                tt = t - STILES
                t_ld = dma_in(buf, xb[tt * 128:(tt + 1) * 128, :])
            wait("sa", "ld", t_ld)
            if t >= 1:
                wait("sa", "v", v_stats_base + t)
            a_sq0 = op("sa", lambda s, b=buf: s.activation(QW, b, AF.Square), "a")
            wait("v", "ld", t_ld)
            wait("v", "a", a_sq0)

            if t < STILES:
                def _st(v, b=buf, t=t):
                    v.tensor_reduce(SMV3[:, t, 0:1], b, AX, A.add)
                    v.tensor_reduce(SMV3[:, t, 1:2], QW, AX, A.add)
                    v.tensor_reduce(SMN[:, t:t + 1], b, AX, A.min)
                    v.reduce_max(SMX[:, t:t + 1], b, AX)
                    return v.tensor_copy(sc1[:, 3:4], c3buf[:])
            else:
                def _st(v, b=buf, tt=t - STILES):
                    v.tensor_reduce(OWNSUM[:, tt + 32:tt + 33], b, AX, A.add)
                    v.tensor_reduce(OWNSQ[:, tt + 32:tt + 33], QW, AX, A.add)
                    return v.tensor_copy(sc1[:, 3:4], c3buf[:])

            op("v", _st, "v")

        def _wp1(v):
            v.tensor_scalar(NEGM.rearrange("p (a b) -> p a b", b=1),
                            SMV3[:, :, 0:1], -1.0 / 1024.0, None, A.mult)
            return v.tensor_copy(sc1[:, 3:4], c3buf[:])

        v_wp1 = op("v", _wp1, "v")
        wait("sa", "v", v_wp1)
        a_wp1 = op("sa", lambda s: s.activation(MSQ, NEGM, AF.Square), "a")
        wait("v", "a", a_wp1)

        def _wp2(v):
            v.scalar_tensor_tensor(VARE.rearrange("p (a b) -> p a b", b=1),
                                   SMV3[:, :, 1:2], 1.0 / 1024.0,
                                   MSQ.rearrange("p (a b) -> p a b", b=1),
                                   A.mult, A.subtract)
            return v.tensor_scalar(VARE, VARE, 1e-5, None, A.add)

        v_wp2 = op("v", _wp2, "v")
        wait("sa", "v", v_wp2)
        a_wp2 = op("sa", lambda s: s.activation(SQV, VARE, AF.Sqrt), "a")
        wait("v", "a", a_wp2)
        v_wp3 = op("v", lambda v: v.reciprocal(R0G, SQV), "v")
        wait("sa", "v", v_wp3)
        a_wp3 = op("sa", lambda s: s.activation(RRG, R0G, AF.Square), "a")
        wait("v", "a", a_wp3)

        def _wp4(v):
            v.tensor_tensor(BVG, RRG, VARE, A.mult)
            v.tensor_scalar(BVG, BVG, -0.5, 1.5, A.mult, A.add)
            v.tensor_tensor(RSTDG, R0G, BVG, A.mult)
            v.tensor_tensor(SMN, SMN, NEGM, A.add)
            v.tensor_tensor(SMN, SMN, RSTDG, A.mult)
            v.tensor_tensor(SMX, SMX, NEGM, A.add)
            v.tensor_tensor(SMX, SMX, RSTDG, A.mult)
            v.tensor_reduce(sc1[:, 0:1], SMN, AX, A.min)
            v.reduce_max(sc1[:, 1:2], SMX, AX)
            return v.tensor_copy(sc1[:, 3:4], c3buf[:])

        v_wp = op("v", _wp4, "v")

        wait("g", "v", v_wp)
        wait("g", "g", g_setup)

        def _dg1(g):
            g.tensor_copy(dsm[0:1, 3:4], NEGM[0:1, 0:1])
            g.tensor_copy(dsm[0:1, 4:5], RSTDG[0:1, 0:1])
            return g.tensor_tensor(dsm[0:1, 7:8], dsm[0:1, 3:4], dsm[0:1, 4:5], A.mult)

        g_d1 = op("g", _dg1, "g")
        dma_out("g", vec_d[0, :], sc1[:, 0:1])
        st_b = dma_out("g", vec_d[1, :], sc1[:, 1:2])
        wait("sy", "st", st_b)
        dma_in(MNR[:, 0:128], vec_d[0:1, :])
        dma_in(MXR[:, 0:128], vec_d[1:2, :])
        t_x0 = dma_in(XROW, xfull[0:1, :])
        wait("v", "ld", t_x0)

        def _dv1(v):
            v.tensor_reduce(dsm[0:1, 0:1], MNR[:, 0:128], AX, A.min)
            v.reduce_max(dsm[0:1, 1:2], MXR[:, 0:128], AX)
            return v.tensor_copy(sc1[:, 3:4], c3buf[:])

        v_d1 = op("v", _dv1, "v")
        wait("g", "v", v_d1)

        def _dg2(g):
            g.tensor_tensor(dsm[0:1, 5:6], dsm[0:1, 1:2], dsm[0:1, 0:1], A.subtract)
            return g.tensor_scalar(dsm[0:1, 5:6], dsm[0:1, 5:6], 1e-10, None, A.add)

        g_d2 = op("g", _dg2, "g")
        wait("v", "g", g_d2)
        v_d2 = op("v", lambda v: v.reciprocal(dsm[0:1, 2:3], dsm[0:1, 5:6]), "v")
        wait("g", "v", v_d2)

        def _dg3(g):
            g.tensor_tensor(dsm[0:1, 6:7], dsm[0:1, 0:1], dsm[0:1, 2:3], A.mult)
            return g.tensor_scalar(dsm[0:1, 6:7], dsm[0:1, 6:7], -1.0, None, A.mult)

        g_d3 = op("g", _dg3, "g")
        wait("sa", "g", g_d3)

        def _da1(s):
            s.activation(XNROW, XROW, AF.Identity,
                         bias=dsm[0:1, 7:8], scale=dsm[0:1, 4:5])
            return s.activation(ZROW, XNROW, AF.Identity,
                                bias=dsm[0:1, 6:7], scale=dsm[0:1, 2:3])

        a_d1 = op("sa", _da1, "a")
        wait("v", "a", a_d1)
        NNJ = (4, 16, 64, 256, 1024)

        def _dv3(v):
            for j, nn in enumerate(NNJ):
                v.tensor_reduce(dly[0:1, j:j + 1], ZROW[:, 0:nn], AX, A.add)
            return v.tensor_copy(sc1[:, 3:4], c3buf[:])

        v_d3 = op("v", _dv3, "v")
        wait("g", "v", v_d3)

        def _dg4(g):
            last = None
            for j, nn in enumerate(NNJ):
                last = g.tensor_scalar(dly2[0:1, j:j + 1], dly[0:1, j:j + 1],
                                       1.0 / nn, None, A.mult)
            return last

        g_d4 = op("g", _dg4, "g")
        wait("v", "g", g_d4)

        def _dv4(v):
            for j, nn in enumerate(NNJ):
                v.tensor_scalar(MROW, ZROW, dly2[0:1, j:j + 1], None, A.is_gt)
                v.tensor_reduce(dcnt[0:1, j:j + 1], MROW[:, 0:nn], AX, A.add)
            return v.tensor_copy(sc1[:, 3:4], c3buf[:])

        v_d4 = op("v", _dv4, "v")
        wait("g", "v", v_d4)
        g_d5 = op("g", lambda g: g.tensor_scalar(dcnt[0:1, 0:5], dcnt[0:1, 0:5],
                                                 1.0, None, A.max), "g")
        wait("sa", "g", g_d5)
        a_d2 = op("sa", lambda s: s.activation(dll[0:1, 0:5], dcnt[0:1, 0:5], AF.Ln), "a")
        wait("g", "a", a_d2)
        g_d6 = op("g", lambda g: g.tensor_tensor(dslp[0:1, 0:5], dll[0:1, 0:5],
                                                 acf[0:1, :], A.mult), "g")
        wait("v", "g", g_d6)

        def _dv5(v):
            v.tensor_reduce(dsm[0:1, 8:9], dslp[0:1, 0:5], AX, A.add)
            return v.tensor_copy(sc1[:, 3:4], c3buf[:])

        v_d5 = op("v", _dv5, "v")
        wait("g", "v", v_d5)
        g_d7 = op("g", lambda g: g.tensor_scalar(dsm[0:1, 9:10], dsm[0:1, 8:9],
                                                 1.0, 2.0, A.max, A.min), "g")
        wait("g", "g", g_d7)
        st_d = dma_out("g", vec_d[2, 0:1], dsm[0:1, 9:10])
        wait("sy", "st", st_d)
        t_db = dma_in(Db[:], vec_d[2:3, 0:1].partition_broadcast(128))

        wait("sa", "ld", t_db)
        a_f0 = op("sa", lambda s: s.activation(alf0[:], Db[:], AF.Square), "a")
        wait("g", "a", a_f0)

        def _fg1(g):
            g.tensor_scalar(gtm[:], alf0[:], 0.2, None, A.mult)
            g.tensor_tensor(alf[:], gtm[:], Db[:], A.add)
            g.tensor_scalar(bet[:], Db[:], 0.5, 1.0, A.mult, A.add)
            return g.tensor_scalar(gam[:], Db[:], 0.1, None, A.mult)

        g_f1 = op("g", _fg1, "g")
        wait("sa", "g", g_f1)

        def _fa1(s):
            s.activation(tmp5[:], kas[:], AF.Identity, bias=alf[:], scale=bet[:])
            s.activation(ub5[:], tmp5[:], AF.Abs, bias=HPIN)
            s.activation(fr1[:], ub5[:], AF.Sin, bias=HPIP, scale=-1.0)
            return s.activation(t5c[:], ka2s[:], AF.Identity, bias=alf[:], scale=gam[:])

        a_f1 = op("sa", _fa1, "a")
        wait("g", "a", a_f1)
        g_f2 = op("g", lambda g: g.tensor_tensor(t5c[:], t5c[:], kas[:], A.subtract), "g")
        wait("sa", "g", g_f2)
        a_f2 = op("sa", lambda s: s.activation(fr2[:], t5c[:], AF.Sin,
                                               bias=HPIP, scale=-1.0), "a")
        wait("g", "a", a_f2)

        def _fg3(g):
            g.tensor_tensor(cfr[:], fr1[:], fr2[:], A.mult)
            return g.tensor_tensor(cfr[:], cfr[:], cws[:], A.mult)

        g_cfr = op("g", _fg3, "g")

        wait("sy", "t", aw_last_t)
        t_m2 = dma_in(S1, ci["cAchi"])
        t_m2b = dma_in(S2, ci["cAclo"])
        m_last_t = None
        m_last_v = None
        for nm in ("c", "s"):
            AW = AWc if nm == "c" else AWs
            if nm == "s":
                wait("sy", "t", m_last_t)
                t_m2 = dma_in(S1, ci["cAshi"])
                t_m2b = dma_in(S2, ci["cAslo"])
            cfrb32 = cfr[:].rearrange("p (c o) -> p c o", o=1).broadcast_to([128, 5, 32])

            def _rbuild(v, AW=AW, cfrb32=cfrb32):
                return v.tensor_tensor(Rf, AW[:, :, 0:32], cfrb32, A.mult)

            if nm == "c":
                wait("v", "g", g_cfr)
            v_r = op("v", _rbuild, "v")
            op("v", lambda v: v.tensor_copy(Rh[:], Rf))
            v_rl = op("v", lambda v: v.scalar_tensor_tensor(
                Rl[:].rearrange("p c n -> p (c n)"),
                Rh[:].rearrange("p c n -> p (c n)"), -1.0,
                Rf.rearrange("p c n -> p (c n)"), A.mult, A.add), "v")
            wait("te", "v", v_rl)
            wait("te", "ld", t_m2b)
            for jc in range(8):
                jsl = slice(jc * 128, (jc + 1) * 128)
                mms = []
                for kc in range(5):
                    mms.append((S1[:, kc, jsl], Rh[:, kc, 0:24]))
                    mms.append((S1[:, kc, jsl], Rl[:, kc, 0:24]))
                    mms.append((S2[:, kc, jsl], Rh[:, kc, 0:24]))
                t_mk = None
                for i, (lh, rh) in enumerate(mms):
                    def _mm(te, lh=lh, rh=rh, st=(i == 0), sp=(i == len(mms) - 1)):
                        return te.matmul(pAW, lh, rh, start=st, stop=sp)
                    if i < len(mms) - 1:
                        op("te", _mm)
                    else:
                        t_mk = op("te", _mm, "t")
                m_last_t = t_mk
                wait("v", "t", t_mk)
                if nm == "c":
                    m_last_v = op("v", lambda v, jc=jc: v.tensor_copy(Mf[:, jc, :], pAW64), "v")
                else:
                    m_last_v = op("v", lambda v, jc=jc: v.tensor_tensor(
                        Mf[:, jc, :], Mf[:, jc, :], pAW64, A.add), "v")
                wait("te", "v", m_last_v)

        def _msplit(v):
            v.tensor_copy(MfC, Mf[:, :, 0:24])
            v.tensor_copy(Mhi[:], MfC)
            return v.scalar_tensor_tensor(
                Mlo[:].rearrange("p c n -> p (c n)"),
                Mhi[:].rearrange("p c n -> p (c n)"), -1.0,
                MfC.rearrange("p c n -> p (c n)"), A.mult, A.add)

        v_msp = op("v", _msplit, "v")

        c_last_v = v_msp
        for nm in ("c", "s"):
            wait("sy", "t", m_last_t)
            wait("sy", "v", c_last_v)
            t_c1 = dma_in(S1, ci["cAchi"] if nm == "c" else ci["cAshi"])
            t_c2 = dma_in(S2, ci["cAchi"] if nm == "c" else ci["cAshi"])
            wait("v", "ld", t_c2)
            cfrb1k = cfr[:].rearrange("p (c o) -> p c o", o=1).broadcast_to([128, 5, 1024])

            def _pbuild(v, cfrb1k=cfrb1k):
                return v.tensor_tensor(S2, S2, cfrb1k, A.mult)

            v_p = op("v", _pbuild, "v")
            wait("te", "v", v_p)
            wait("te", "ld", t_c1)
            for jc in range(8):
                jsl = slice(jc * 128, (jc + 1) * 128)
                t_ck = None
                for half in range(2):
                    hsl = slice(half * 512, (half + 1) * 512)
                    for kc in range(5):
                        def _mm(te, kc=kc, jsl=jsl, hsl=hsl, st=(kc == 0), sp=(kc == 4)):
                            return te.matmul(pdn[:, hsl], S1[:, kc, jsl],
                                             S2[:, kc, hsl], start=st, stop=sp)
                        if kc < 4 or half == 0:
                            op("te", _mm)
                        else:
                            t_ck = op("te", _mm, "t")
                wait("v", "t", t_ck)
                if nm == "c":
                    c_last_v = op("v", lambda v, jc=jc: v.tensor_copy(Chi[:, jc, :], pdn[:]), "v")
                else:
                    c_last_v = op("v", lambda v, jc=jc: v.tensor_add(Chi[:, jc, :], Chi[:, jc, :], pdn[:]), "v")
                wait("te", "v", c_last_v)

        barrier()

        prev_v = None
        prev_tr = None

        def wq_tile(rows_src, m, dst_fn, subw):
            nonlocal prev_v, prev_tr
            marks = []
            for s_i, w in enumerate(subw):
                o = sum(subw[:s_i])
                if prev_v is not None:
                    wait("sy", "v", prev_v)
                t_ld = dma_in(XT0[:, 0:w], rows_src[:, o:o + w])
                wait("sa", "ld", t_ld)
                if prev_v is not None:
                    wait("sa", "v", prev_v)
                a_sg = op("sa", lambda s, w=w: s.activation(QSG[:, 0:w], XT0[:, 0:w], AF.Sign), "a")
                wait("v", "ld", t_ld)
                wait("v", "a", a_sg)
                if prev_tr is not None:
                    wait("v", "t", prev_tr)
                v_mk = op("v", emit_quant(XT0[:, 0:w], QOUT[:, o:o + w], w), "v")
                marks.append(v_mk)
                prev_v = v_mk
            nblk = sum(subw) // 128
            wait("te", "v", marks[-1])
            for g0 in range(0, nblk, 8):
                gn = min(8, nblk - g0)
                t_tr = None
                for b in range(gn):
                    def _tr(te, b=b, g0=g0):
                        return te.transpose(pT[:, b * 128:(b + 1) * 128],
                                            QOUT[:, (g0 + b) * 128:(g0 + b + 1) * 128],
                                            ident[:])
                    if b < gn - 1:
                        op("te", _tr)
                    else:
                        t_tr = op("te", _tr, "t")
                prev_tr = t_tr
                wait("sa", "t", t_tr)
                a_ev = op("sa", lambda s, d=dst_fn(m, g0, gn), gn=gn:
                          s.copy(d, pT[:, 0:gn * 128].rearrange("p (c n) -> p c n", n=128)), "a")
                wait("te", "a", a_ev)

        for m in range(22):
            wq_tile(wg[m * 128:(m + 1) * 128, :], m,
                    lambda m, g0, gn: WGb[:, g0:g0 + gn, m * 128:(m + 1) * 128], [1024])
        for m in range(22):
            wq_tile(wu[m * 128:(m + 1) * 128, :], m,
                    lambda m, g0, gn: WUb[:, g0:g0 + gn, m * 128:(m + 1) * 128], [1024])
        for m in range(8):
            wq_tile(wd[m * 128:(m + 1) * 128, :], m,
                    lambda m, g0, gn: WDb[:, g0:g0 + gn, m * 128:(m + 1) * 128],
                    [1024, 1024, 768])

        barrier()

        def _owp1(v):
            v.tensor_scalar(NEGM64, OWNSUM, -1.0 / 1024.0, None, A.mult)
            return v.tensor_copy(sc1[:, 3:4], c3buf[:])

        v_o1 = op("v", _owp1, "v")
        wait("sa", "v", v_o1)
        a_o1 = op("sa", lambda s: s.activation(MSQ64, NEGM64, AF.Square), "a")
        wait("v", "a", a_o1)

        def _owp2(v):
            v.scalar_tensor_tensor(VARE64, OWNSQ, 1.0 / 1024.0, MSQ64,
                                   A.mult, A.subtract)
            return v.tensor_scalar(VARE64, VARE64, 1e-5, None, A.add)

        v_o2 = op("v", _owp2, "v")
        wait("sa", "v", v_o2)
        a_o2 = op("sa", lambda s: s.activation(SQ64, VARE64, AF.Sqrt), "a")
        wait("v", "a", a_o2)
        v_o3 = op("v", lambda v: v.reciprocal(R064, SQ64), "v")
        wait("sa", "v", v_o3)
        a_o3 = op("sa", lambda s: s.activation(RR64, R064, AF.Square), "a")
        wait("v", "a", a_o3)

        def _owp4(v):
            v.tensor_tensor(B64, RR64, VARE64, A.mult)
            v.tensor_scalar(B64, B64, -0.5, 1.5, A.mult, A.add)
            return v.tensor_tensor(RSTD64, R064, B64, A.mult)

        v_own = op("v", _owp4, "v")

        ln_marks = []
        tr_last = []
        for t in range(TPC):
            buf = XT0 if t % 2 == 0 else XT1
            if t >= 2:
                wait("sy", "v", ln_marks[t - 2])
            t_ld = dma_in(buf, xb[t * 128:(t + 1) * 128, :])
            wait("v", "ld", t_ld)
            v_ln = op("v", lambda v, b=buf, t=t: v.tensor_scalar(
                XNF, b, NEGM64[:, 32 + t:33 + t], RSTD64[:, 32 + t:33 + t],
                A.add, A.mult), "v")
            ln_marks.append(v_ln)
            wait("sa", "v", v_ln)
            if t >= 1:
                wait("sa", "t", tr_last[t - 1])

            def _ahi(s):
                s.activation(QSG[:, 0:1024], XNF, AF.Sign)
                return s.activation(EV1, XNF, AF.Copy)

            a_hi = op("sa", _ahi, "a")
            wait("v", "a", a_hi)
            if t >= 1:
                wait("v", "t", tr_last[t - 1])
            op("v", emit_quant(XNF, QOUT[:, 0:1024], 1024))
            v_a = op("v", lambda v: v.scalar_tensor_tensor(EV2, EV1, -1.0, XNF, A.mult, A.add), "v")

            wait("te", "v", v_a)
            for src, dstd in ((QOUT[:, 0:1024], xqT_d), (EV1, xhT_d), (EV2, xlT_d)):
                t_tr = None
                for b in range(8):
                    def _tr(te, src=src, b=b):
                        return te.transpose(pT[:, b * 128:(b + 1) * 128],
                                            src[:, b * 128:(b + 1) * 128], ident[:])
                    if b < 7:
                        op("te", _tr)
                    else:
                        t_tr = op("te", _tr, "t")
                wait("sa", "t", t_tr)
                a_ev = op("sa", lambda s: s.copy(TSTG, pT[:]), "a")
                wait("te", "a", a_ev)
                wait("g", "a", a_ev)
                st_mk = dma_out("g", dstd[t], TSTG.rearrange("p (c n) -> p c n", n=128))
                wait("sa", "st", st_mk)
            tr_last.append(t_tr)

        barrier()

        chunks = [(0, 512), (512, 512), (1024, 512), (1536, 512), (2048, 512), (2560, 256)]
        sil_marks, h_marks, hq_marks, tr_marks, gy_marks = [], [], [], [], []
        tile_last_gy = []
        prev_g_st = None
        for t in range(TPC):
            xq = XQT0 if t % 2 == 0 else XQT1
            if t >= 2:
                wait("sy", "t", tile_last_gy[t - 2])
            t_ld = dma_in(xq, xqT_d[t])
            wait("te", "ld", t_ld)
            for ci_, (io, iw) in enumerate(chunks):
                gk = t * 6 + ci_
                pg = pg0 if gk % 2 == 0 else pg1
                py = py0 if gk % 2 == 0 else py1
                if gk >= 2:
                    wait("te", "a", sil_marks[gk - 2])
                    wait("te", "v", h_marks[gk - 2])
                for cc in range(8):
                    def _mmg(te, cc=cc, io=io, iw=iw, pg=pg, xq=xq,
                             st=(cc == 0), sp=(cc == 7)):
                        return te.matmul(pg[:, 0:iw], xq[:, cc, :],
                                         WGb[:, cc, io:io + iw], start=st, stop=sp)
                    op("te", _mmg)
                gy_mk = None
                for cc in range(8):
                    def _mmy(te, cc=cc, io=io, iw=iw, py=py, xq=xq,
                             st=(cc == 0), sp=(cc == 7)):
                        return te.matmul(py[:, 0:iw], xq[:, cc, :],
                                         WUb[:, cc, io:io + iw], start=st, stop=sp)
                    if cc < 7:
                        op("te", _mmy)
                    else:
                        gy_mk = op("te", _mmy, "t")
                gy_marks.append(gy_mk)
                wait("sa", "t", gy_mk)
                if gk >= 1:
                    wait("sa", "v", h_marks[gk - 1])
                sil_marks.append(op("sa", lambda s, pg=pg, iw=iw:
                                    s.activation(SIL[:, 0:iw], pg[:, 0:iw], AF.Silu), "a"))
                wait("v", "a", sil_marks[gk])
                h_marks.append(op("v", lambda v, py=py, iw=iw:
                                  v.tensor_mul(HB[:, 0:iw], SIL[:, 0:iw], py[:, 0:iw]), "v"))
                wait("sa", "v", h_marks[gk])
                if gk >= 1:
                    wait("sa", "v", hq_marks[gk - 1])
                a_sgn = op("sa", lambda s, iw=iw:
                           s.activation(QSG[:, 0:iw], HB[:, 0:iw], AF.Sign), "a")
                wait("v", "a", a_sgn)
                hq_marks.append(op("v", emit_quant(HB[:, 0:iw], QOUT[:, io:io + iw], iw), "v"))
                wait("te", "v", hq_marks[gk])
                nb = iw // 128
                tr_mk = None
                for b in range(nb):
                    def _tr(te, b=b, io=io):
                        return te.transpose(pT[:, b * 128:(b + 1) * 128],
                                            QOUT[:, io + b * 128: io + (b + 1) * 128],
                                            ident[:])
                    if b < nb - 1:
                        op("te", _tr)
                    else:
                        tr_mk = op("te", _tr, "t")
                tr_marks.append(tr_mk)
                wait("sa", "t", tr_mk)
                a_ev = op("sa", lambda s, io=io, nb=nb:
                          s.copy(HQT[:, io // 128: io // 128 + nb, :],
                                 pT[:, 0:nb * 128].rearrange("p (c n) -> p c n", n=128)), "a")
                wait("te", "a", a_ev)
            tile_last_gy.append(gy_marks[-1])
            t_dn = None
            for ic in range(22):
                for half in range(2):
                    def _mmd(te, ic=ic, half=half, st=(ic == 0), sp=(ic == 21)):
                        return te.matmul(pdn[:, half * 512:(half + 1) * 512],
                                         HQT[:, ic, :], WDb[:, ic, half * 512:(half + 1) * 512],
                                         start=st, stop=sp)
                    if ic == 21 and half == 1:
                        t_dn = op("te", _mmd, "t")
                    else:
                        op("te", _mmd)
            wait("sa", "t", t_dn)
            if prev_g_st is not None:
                wait("sa", "st", prev_g_st)
            a_g = op("sa", lambda s: s.activation(GSTGF, pdn[:], AF.Silu), "a")
            wait("g", "a", a_g)
            prev_g_st = dma_out("g", g_d[t], GSTGF)
            wait("te", "a", a_g)

        barrier()

        wait("sy", "st", prev_g_st)
        t_ble = dma_in(BLEB, ble.partition_broadcast(128))
        wait("v", "ld", t_ble)
        Cn["st"] += 16
        _wle_st = Cn["st"]
        em("g", lambda e: e.dma_start(out=WleB[:], in_=wle).then_inc(sSt, 16))
        wait("te", "st", _wle_st)

        def _gmask(g):
            g.memset(maskb[:], 0.0)
            return g.memset(maskb[:, 0:12], 1.0)

        g_mask = op("g", _gmask, "g")
        wait("v", "g", g_mask)
        prev_cb = None
        prev_sty = None
        for t in range(TPC):
            if prev_cb is not None:
                wait("sy", "v", prev_cb)
                wait("sy", "st", prev_sty)
            dma_in(XHT, xhT_d[t])
            t2 = dma_in(XLT, xlT_d[t])
            t3 = dma_in(XT0, xb[t * 128:(t + 1) * 128, :])
            t4 = dma_in(GTF, g_d[t])
            wait("te", "ld", t2)
            for half in range(2):
                hsl = slice(half * 512, (half + 1) * 512)
                for cc in range(8):
                    def _mmx(te, cc=cc, hsl=hsl, st=(cc == 0)):
                        return te.matmul(pdn[:, hsl], XHT[:, cc, :],
                                         Chi[:, cc, hsl], start=st, stop=False)
                    op("te", _mmx)
            lat_mms = ([(XHT[:, cc, :], Mhi[:, cc, :]) for cc in range(8)]
                       + [(XHT[:, cc, :], Mlo[:, cc, :]) for cc in range(8)]
                       + [(XLT[:, cc, :], Mhi[:, cc, :]) for cc in range(8)])
            t_lat = None
            for i, (lh, rh) in enumerate(lat_mms):
                def _mml(te, lh=lh, rh=rh, st=(i == 0), sp=(i == 23)):
                    return te.matmul(latP, lh, rh, start=st, stop=sp)
                if i < 23:
                    op("te", _mml)
                else:
                    t_lat = op("te", _mml, "t")
            wait("v", "t", t_lat)
            v_bits = op("v", lambda v: v.tensor_tensor(
                BITSF, pg0[:, 0:64], negb[:], A.is_gt), "v")
            wait("sa", "v", v_bits)
            a_bc = op("sa", lambda s: s.copy(bits[:], BITSF[:, 0:24]), "a")
            wait("te", "a", a_bc)
            t_bt = op("te", lambda te: te.transpose(psA, bits[:], ident[:]), "t")
            wait("sa", "t", t_bt)
            a_bt = op("sa", lambda s: s.copy(bitsT[:], psA), "a")
            wait("te", "a", a_bt)
            t_syn = op("te", lambda te: te.matmul(synP, bitsT[:], HTs[:], start=True, stop=True), "t")
            wait("v", "t", t_syn)

            def _golay(v):
                v.tensor_scalar(synpm, py0[:, 0:64], 0.5, None, A.is_gt)
                v.tensor_reduce(sc1[:, 2:3], synpm[:, 0:12], AX, A.add)
                v.tensor_copy(sc1[:, 3:4], c3buf[:])
                v.tensor_copy(GW, sc1[:, 2:3].broadcast_to([128, 64]))
                v.tensor_scalar(GU, GW, 3.5, None, A.is_lt)
                v.tensor_tensor(GA, synpm, GU, A.mult)
                v.tensor_tensor(GF, GA, maskb, A.mult)
                v.tensor_tensor(GD, BITSF, GF, A.mult)
                v.tensor_tensor(GE, BITSF, GF, A.add)
                return v.scalar_tensor_tensor(CORRF, GD, -2.0, GE, A.mult, A.add)

            v_go = op("v", _golay, "v")
            wait("sa", "v", v_go)

            def _acorr(s):
                s.copy(corr[:, 0:12], CORRF[:, 0:12])
                return s.copy(corr[:, 12:24], BITSF[:, 12:24])

            a_co = op("sa", _acorr, "a")
            wait("te", "a", a_co)
            t_ct = op("te", lambda te: te.transpose(psB, corr[:], ident[:]), "t")
            wait("sa", "t", t_ct)
            a_ct = op("sa", lambda s: s.copy(corrT[:], psB), "a")
            wait("te", "a", a_ct)
            t_wle2 = None
            for half in range(2):
                def _mmw(te, half=half):
                    return te.matmul(pdn[:, half * 512:(half + 1) * 512], corrT[:],
                                     WleB[:, half * 512:(half + 1) * 512],
                                     start=False, stop=True)
                if half == 0:
                    op("te", _mmw)
                else:
                    t_wle2 = op("te", _mmw, "t")
            wait("v", "t", t_wle2)
            wait("v", "ld", max(t3, t4))

            def _comb(v):
                v.tensor_add(SSB, pdn[:], BLEB)
                v.tensor_mul(SSB, SSB, GTF)
                return v.tensor_add(OSB, SSB, XT0)

            prev_cb = op("v", _comb, "v")
            wait("te", "v", prev_cb)
            wait("g", "v", prev_cb)
            prev_sty = dma_out("g", yout[t * 128:(t + 1) * 128, :], OSB)

        wait("g", "st", Cn["st"])

        @block.sync
        def _(e):
            for fn in Q["sy"]:
                fn(e)

        @block.vector
        def _(e):
            for fn in Q["v"]:
                fn(e)

        @block.scalar
        def _(e):
            for fn in Q["sa"]:
                fn(e)

        @block.tensor
        def _(e):
            for fn in Q["te"]:
                fn(e)

        @block.gpsimd
        def _(e):
            for fn in Q["g"]:
                fn(e)

    mybir.codegen_inst_isa_subclasses(nc)
    return nc


def kernel(L0_attn_x, L0_attn_W_el, L0_attn_b_el, L0_attn_W_le, L0_attn_b_le,
           L0_ffn_w_gate, L0_ffn_w_up, L0_ffn_w_down):
    x = np.asarray(L0_attn_x)
    xf = np.ascontiguousarray(x.reshape(ROWS, E))
    if "nc" not in _NC_CACHE:
        _NC_CACHE["nc"] = build_program()
    nc = _NC_CACHE["nc"]
    base = {
        "xfull": xf,
        "wg": np.ascontiguousarray(np.asarray(L0_ffn_w_gate)),
        "wu": np.ascontiguousarray(np.asarray(L0_ffn_w_up)),
        "wd": np.ascontiguousarray(np.asarray(L0_ffn_w_down)),
        "wel": np.ascontiguousarray(np.asarray(L0_attn_W_el)),
        "bel": np.ascontiguousarray(np.asarray(L0_attn_b_el).reshape(1, 24)),
        "wle": np.ascontiguousarray(np.asarray(L0_attn_W_le)),
        "ble": np.ascontiguousarray(np.asarray(L0_attn_b_le).reshape(1, E)),
    }
    for name, arr in _CONSTS.items():
        base[name] = arr
    in_maps = []
    for c in range(N_CORES):
        m = dict(base)
        m["xb"] = np.ascontiguousarray(x[c])
        in_maps.append(m)
    res = run_bass_kernel_spmd(nc, in_maps, core_ids=list(range(N_CORES)))
    out = np.stack([res.results[c]["yout"] for c in range(N_CORES)], axis=0)
    return out.astype(np.float32, copy=False)


# version 08
# speedup vs baseline: 1.1537x; 1.1537x over previous
import math

import numpy as np
import ml_dtypes

import concourse.bass as bass
import concourse.mybir as mybir
from concourse.bass_utils import run_bass_kernel_spmd
from concourse.dve_op import DveOp
from concourse.dve_spec import Spec, Src0, C0, C1, C2, C3, lower as dve_lower, sq
from concourse.dve_table_gen import dve_ver_for
import concourse.dve_ops as dve_ops_mod

F32 = mybir.dt.float32
BF16 = mybir.dt.bfloat16
A = mybir.AluOpType
AF = mybir.ActivationFunctionType
AX = mybir.AxisListType.X
BF = ml_dtypes.bfloat16

B_, SEQ, E = 8, 4096, 1024
ROWS = B_ * SEQ
TPC = SEQ // 128
STILES = ROWS // 128
I_FFN = 2816
HALFPI = float(np.float32(math.pi / 2.0))
SIXTH = float(np.float32(1.0 / 6.0))
N_CORES = 8


def _register_op(name, spec, reference):
    for op_ in dve_ops_mod.OPS:
        if op_.name == name:
            return op_
    ver = dve_ver_for("TRN2")
    sha = dve_lower(spec, ver=ver, perf_en=False, name=name).sha(ver)
    op_ = DveOp(name, spec, subdim=False, uops_sha={ver: sha}, reference=reference)
    dve_ops_mod.OPS.append(op_)
    row = max(dve_ops_mod._SUB_OPCODE_FOR_NAME.values()) + 1
    assert row < 0x20, "custom-DVE opcode rows exhausted"
    dve_ops_mod._SUB_OPCODE_FOR_NAME[name] = row
    dve_ops_mod.CUSTOM_DVE_REFS[name] = reference
    return op_


def _fp4a_ref(in0, in1, s0, s1, imm2):
    y = in0 * in0
    return (y > s0) * 1.0 + (y > s1) + (y > imm2) + (y > in1)


def _fp4b_ref(in0, in1, s0, s1, imm2):
    y = in0 * in0
    return 2.0 * ((y > s0) * 1.0 + (y > s1) + 2.0 * (y > imm2))


_y = sq(Src0)
FP4A = _register_op(
    "ANT_FP4A", Spec(body=(_y > C0) + (_y > C1) + ((_y > C2) + (_y > C3))), _fp4a_ref
)
_c7 = _y > C2
_inner = ((_y > C0) + (_y > C1)) + (_c7 + _c7)
FP4B = _register_op("ANT_FP4B", Spec(body=_inner + _inner), _fp4b_ref)


def _build_consts():
    k = np.arange(513, dtype=np.float64)
    n = np.arange(1024, dtype=np.float64)
    ang = 2.0 * np.pi * np.outer(k, n) / 1024.0
    Acp = np.zeros((640, 1024), dtype=np.float64)
    Asp = np.zeros((640, 1024), dtype=np.float64)
    Acp[:513] = np.cos(ang)
    Asp[:513] = np.sin(ang)
    ac_ch = np.ascontiguousarray(Acp.reshape(5, 128, 1024).transpose(1, 0, 2)).astype(np.float32)
    as_ch = np.ascontiguousarray(Asp.reshape(5, 128, 1024).transpose(1, 0, 2)).astype(np.float32)
    Ac_hi = ac_ch.astype(BF)
    As_hi = as_ch.astype(BF)
    AcT = np.ascontiguousarray(Acp.T.reshape(8, 128, 640).transpose(1, 0, 2)).astype(np.float32)
    AsT = np.ascontiguousarray(Asp.T.reshape(8, 128, 640).transpose(1, 0, 2)).astype(np.float32)
    AcT_hi = AcT.astype(BF)
    AsT_hi = AsT.astype(BF)
    AcT_lo = (AcT - AcT_hi.astype(np.float32)).astype(BF)
    AsT_lo = (AsT - AsT_hi.astype(np.float32)).astype(BF)
    cw = np.zeros(640, dtype=np.float64)
    cw[:513] = 2.0 / 1024.0
    cw[0] = 1.0 / 1024.0
    cw[512] = 1.0 / 1024.0
    cconst = np.ascontiguousarray(cw.reshape(5, 128).T.astype(np.float32))
    kk = np.fft.fftfreq(1024).astype(np.float32)[:513].copy()
    kk[512] = np.float32(-0.5)
    ka = (np.abs(kk) + np.float32(1e-8)).astype(np.float32)
    ka2 = (ka * ka).astype(np.float32)
    kap = np.zeros(640, dtype=np.float32)
    ka2p = np.zeros(640, dtype=np.float32)
    kap[:513] = ka
    ka2p[:513] = ka2
    ka_ch = np.ascontiguousarray(kap.reshape(5, 128).T)
    ka2_ch = np.ascontiguousarray(ka2p.reshape(5, 128).T)
    lx = np.log(np.array([2.0, 4.0, 8.0, 16.0, 32.0], dtype=np.float32)).astype(np.float64)
    den = 5.0 * np.sum(lx * lx) - np.sum(lx) ** 2
    acoef = np.ascontiguousarray((-(5.0 * lx - np.sum(lx)) / den).astype(np.float32)[None, :])
    ident = np.eye(128, dtype=np.float32).astype(BF)
    _A = np.array(
        [[1,1,1,0,1,1,1,0,0,0,1,1],[1,1,0,1,1,1,0,0,0,1,1,0],[1,0,1,1,1,0,0,0,1,1,0,1],
         [0,1,1,1,0,0,0,1,1,0,1,1],[0,0,0,0,1,1,1,1,1,1,1,1],[1,1,1,1,0,0,0,0,0,0,0,0],
         [0,0,0,0,1,1,1,1,0,0,0,0],[1,1,0,0,1,1,0,0,1,1,0,0],[1,0,1,0,1,0,1,0,1,0,1,0],
         [0,1,0,1,0,1,0,1,0,1,0,1],[1,0,0,1,1,0,0,1,1,0,0,1],[0,1,1,0,0,1,1,0,0,1,1,0]],
        dtype=np.float32)
    H = np.concatenate([_A.T, np.eye(12, dtype=np.float32)], axis=1)
    HT = np.ascontiguousarray(H.T).astype(BF)
    Ac_lo = (ac_ch - Ac_hi.astype(np.float32)).astype(BF)
    As_lo = (as_ch - As_hi.astype(np.float32)).astype(BF)
    return dict(cAchi=Ac_hi, cAshi=As_hi, cAclo=Ac_lo, cAslo=As_lo,
                cAcThi=AcT_hi, cAcTlo=AcT_lo, cAsThi=AsT_hi, cAsTlo=AsT_lo,
                ccw=cconst, cka=ka_ch, cka2=ka2_ch, cacoef=acoef,
                cident=ident, cHT=HT)


_CONSTS = _build_consts()
_NC_CACHE = {}


def _v3(ap, n=16):
    return ap.rearrange("p (s n) -> p s n", n=n)


def _b16(small_ap, S):
    return small_ap.rearrange("p (s o) -> p s o", o=1).broadcast_to([128, S, 16])


def build_program():
    nc = bass.Bass("TRN2", target_bir_lowering=False, debug=False)

    xb = nc.dram_tensor("xb", [SEQ, E], F32, kind="ExternalInput").ap()
    xfull = nc.dram_tensor("xfull", [ROWS, E], F32, kind="ExternalInput").ap()
    wg = nc.dram_tensor("wg", [I_FFN, E], F32, kind="ExternalInput").ap()
    wu = nc.dram_tensor("wu", [I_FFN, E], F32, kind="ExternalInput").ap()
    wd = nc.dram_tensor("wd", [E, I_FFN], F32, kind="ExternalInput").ap()
    wel = nc.dram_tensor("wel", [E, 24], F32, kind="ExternalInput").ap()
    bel = nc.dram_tensor("bel", [1, 24], F32, kind="ExternalInput").ap()
    wle = nc.dram_tensor("wle", [24, E], F32, kind="ExternalInput").ap()
    ble = nc.dram_tensor("ble", [1, E], F32, kind="ExternalInput").ap()
    yout = nc.dram_tensor("yout", [SEQ, E], F32, kind="ExternalOutput").ap()
    ci = {}
    for name, arr in _CONSTS.items():
        dt = BF16 if arr.dtype == BF else F32
        ci[name] = nc.dram_tensor(name, list(arr.shape), dt, kind="ExternalInput").ap()

    xqT_d = nc.dram_tensor("xqT_d", [TPC, 128, 8, 128], BF16, kind="Internal").ap()
    xhT_d = nc.dram_tensor("xhT_d", [TPC, 128, 8, 128], BF16, kind="Internal").ap()
    xlT_d = nc.dram_tensor("xlT_d", [TPC, 128, 8, 128], BF16, kind="Internal").ap()
    g_d = nc.dram_tensor("g_d", [TPC, 128, E], F32, kind="Internal").ap()
    vec_d = nc.dram_tensor("vec_d", [4, 128], F32, kind="Internal").ap()

    from contextlib import ExitStack
    es = ExitStack()
    with es:
        WGb = es.enter_context(nc.sbuf_tensor("WGb", [128, 8, I_FFN], BF16))
        WUb = es.enter_context(nc.sbuf_tensor("WUb", [128, 8, I_FFN], BF16))
        WDb = es.enter_context(nc.sbuf_tensor("WDb", [128, 22, E], BF16))
        Chi = es.enter_context(nc.sbuf_tensor("Chi", [128, 8, E], BF16))
        Mhi = es.enter_context(nc.sbuf_tensor("Mhi", [128, 8, 24], BF16))
        Mlo = es.enter_context(nc.sbuf_tensor("Mlo", [128, 8, 24], BF16))
        AF_ = es.enter_context(nc.sbuf_tensor("AF_", [128, 6144], F32))
        ABF = es.enter_context(nc.sbuf_tensor("ABF", [128, 13824], BF16))
        ident = es.enter_context(nc.sbuf_tensor("ident", [128, 128], BF16))
        HTs = es.enter_context(nc.sbuf_tensor("HTs", [24, 12], BF16))
        negb = es.enter_context(nc.sbuf_tensor("negb", [128, 64], F32))
        c3buf = es.enter_context(nc.sbuf_tensor("c3buf", [128, 1], F32))
        sc1 = es.enter_context(nc.sbuf_tensor("sc1", [128, 4], F32))
        cstb = es.enter_context(nc.sbuf_tensor("cstb", [128, 8], F32))
        dsm = es.enter_context(nc.sbuf_tensor("dsm", [1, 16], F32))
        dly = es.enter_context(nc.sbuf_tensor("dly", [1, 8], F32))
        dly2 = es.enter_context(nc.sbuf_tensor("dly2", [1, 8], F32))
        dcnt = es.enter_context(nc.sbuf_tensor("dcnt", [1, 8], F32))
        dll = es.enter_context(nc.sbuf_tensor("dll", [1, 8], F32))
        dslp = es.enter_context(nc.sbuf_tensor("dslp", [1, 8], F32))
        acf = es.enter_context(nc.sbuf_tensor("acf", [1, 5], F32))
        Db = es.enter_context(nc.sbuf_tensor("Db", [128, 1], F32))
        alf0 = es.enter_context(nc.sbuf_tensor("alf0", [128, 1], F32))
        gtm = es.enter_context(nc.sbuf_tensor("gtm", [128, 1], F32))
        alf = es.enter_context(nc.sbuf_tensor("alf", [128, 1], F32))
        bet = es.enter_context(nc.sbuf_tensor("bet", [128, 1], F32))
        gam = es.enter_context(nc.sbuf_tensor("gam", [128, 1], F32))
        kas = es.enter_context(nc.sbuf_tensor("kas", [128, 5], F32))
        ka2s = es.enter_context(nc.sbuf_tensor("ka2s", [128, 5], F32))
        cws = es.enter_context(nc.sbuf_tensor("cws", [128, 5], F32))
        tmp5 = es.enter_context(nc.sbuf_tensor("tmp5", [128, 5], F32))
        t5c = es.enter_context(nc.sbuf_tensor("t5c", [128, 5], F32))
        ub5 = es.enter_context(nc.sbuf_tensor("ub5", [128, 5], F32))
        fr1 = es.enter_context(nc.sbuf_tensor("fr1", [128, 5], F32))
        fr2 = es.enter_context(nc.sbuf_tensor("fr2", [128, 5], F32))
        cfr = es.enter_context(nc.sbuf_tensor("cfr", [128, 5], F32))
        WelS = es.enter_context(nc.sbuf_tensor("WelS", [128, 8, 24], F32))
        WelH = es.enter_context(nc.sbuf_tensor("WelH", [128, 8, 24], BF16))
        WelL = es.enter_context(nc.sbuf_tensor("WelL", [128, 8, 24], BF16))
        AWc = es.enter_context(nc.sbuf_tensor("AWc", [128, 5, 64], F32))
        AWs = es.enter_context(nc.sbuf_tensor("AWs", [128, 5, 64], F32))
        Rh = es.enter_context(nc.sbuf_tensor("Rh", [128, 5, 32], BF16))
        Rl = es.enter_context(nc.sbuf_tensor("Rl", [128, 5, 32], BF16))
        Mf = es.enter_context(nc.sbuf_tensor("Mf", [128, 8, 64], F32))
        bits = es.enter_context(nc.sbuf_tensor("bits", [128, 24], BF16))
        corr = es.enter_context(nc.sbuf_tensor("corr", [128, 24], BF16))
        pg0 = es.enter_context(nc.psum_tensor("pg0", [128, 512], F32))
        pg1 = es.enter_context(nc.psum_tensor("pg1", [128, 512], F32))
        py0 = es.enter_context(nc.psum_tensor("py0", [128, 512], F32))
        py1 = es.enter_context(nc.psum_tensor("py1", [128, 512], F32))
        pdn = es.enter_context(nc.psum_tensor("pdn", [128, 1024], F32))
        pT = es.enter_context(nc.psum_tensor("pT", [128, 1024], BF16))
        sLd = es.enter_context(nc.semaphore("sLd"))
        sSt = es.enter_context(nc.semaphore("sSt"))
        sV = es.enter_context(nc.semaphore("sV"))
        sA = es.enter_context(nc.semaphore("sA"))
        sT = es.enter_context(nc.semaphore("sT"))
        sSy = es.enter_context(nc.semaphore("sSy"))
        sG = es.enter_context(nc.semaphore("sG"))
        block = es.enter_context(nc.Block())
        XT0 = AF_[:, 0:1024]
        XT1 = AF_[:, 1024:2048]
        QW = AF_[:, 2048:3072]
        SIL = AF_[:, 3072:3584]
        HB = AF_[:, 3584:4096]
        SMV3 = AF_[:, 4096:4608].rearrange("p (t o) -> p t o", o=2)
        SMN = AF_[:, 4608:4864]
        SMX = AF_[:, 4864:5120]
        XNF = AF_[:, 5120:6144]
        BLEB = AF_[:, 5120:6144]
        SSB = AF_[:, 4096:5120]
        OSB = AF_[:, 3072:4096]
        NEGM = AF_[:, 2048:2304]
        MSQ = AF_[:, 2304:2560]
        VARE = AF_[:, 2560:2816]
        SQV = AF_[:, 2816:3072]
        R0G = AF_[:, 3072:3328]
        RRG = AF_[:, 3328:3584]
        BVG = AF_[:, 3584:3840]
        RSTDG = AF_[:, 3840:4096]
        OWNSUM = AF_[:, 5120:5184]
        OWNSQ = AF_[:, 5184:5248]
        NEGM64 = AF_[:, 3072:3136]
        RSTD64 = AF_[:, 3136:3200]
        MSQ64 = AF_[:, 5248:5312]
        VARE64 = AF_[:, 5312:5376]
        SQ64 = AF_[:, 5376:5440]
        R064 = AF_[:, 5440:5504]
        RR64 = AF_[:, 5504:5568]
        B64 = AF_[:, 5568:5632]
        MfC = AF_[:, 2048:2240].rearrange("p (c n) -> p c n", n=24)
        Rf = AF_[:, 2048:2208].rearrange("p (c n) -> p c n", n=32)
        qm = AF_[:, 4096:4160]
        qsb = AF_[:, 4160:4224]
        qrb = AF_[:, 4224:4288]
        qs2 = AF_[:, 4288:4352]
        XROW = AF_[0:1, 0:1024]
        XNROW = AF_[0:1, 1024:2048]
        ZROW = AF_[0:1, 3072:4096]
        MROW = AF_[0:1, 0:1024]
        MNR = AF_[0:1, 2048:2176]
        MXR = AF_[0:1, 2176:2304]

        S1 = ABF[:, 0:5120].rearrange("p (c n) -> p c n", n=1024)
        S2 = ABF[:, 5120:10240].rearrange("p (c n) -> p c n", n=1024)
        S1f = ABF[:, 0:5120]
        S2f = ABF[:, 5120:10240]
        AcT_hi_v = S1f.rearrange("p (c n) -> p c n", n=640)
        AcT_lo_v = S2f.rearrange("p (c n) -> p c n", n=640)
        QGA = ABF[:, 0:1024]
        QGB = ABF[:, 1024:2048]
        QSG = ABF[:, 2048:3072]
        QOUT = ABF[:, 3072:5888]
        EV1 = ABF[:, 5888:6912]
        EV2 = ABF[:, 6912:7936]
        TSTG = ABF[:, 7936:8960]
        GSTGF = AF_[:, 5120:6144]
        HQT = ABF[:, 8960:11776].rearrange("p (c n) -> p c n", n=128)
        XQT0 = ABF[:, 11776:12800].rearrange("p (c n) -> p c n", n=128)
        XQT1 = ABF[:, 12800:13824].rearrange("p (c n) -> p c n", n=128)
        XHT = ABF[:, 5888:6912].rearrange("p (c n) -> p c n", n=128)
        XLT = ABF[:, 6912:7936].rearrange("p (c n) -> p c n", n=128)
        GTF = AF_[:, 2048:3072]
        GW = AF_[:, 1024:1088]
        GU = AF_[:, 1088:1152]
        GA = AF_[:, 1152:1216]
        GB = AF_[:, 1024:1088]
        GF0 = AF_[:, 1088:1152]
        GF = AF_[:, 1152:1216]
        BITSF = AF_[:, 1216:1280]
        GD = AF_[:, 1024:1088]
        GE = AF_[:, 1088:1152]
        CORRF = AF_[:, 1152:1216]
        synpm = AF_[:, 1280:1344]
        maskb = AF_[:, 1344:1408]
        WleB = ABF[0:24, 8960:9984]
        bitsT = ABF[0:24, 9984:10112]
        corrT = ABF[0:24, 10112:10240]
        NB05 = cstb[:, 0:1]
        NB5 = cstb[:, 1:2]
        HPIN = cstb[:, 2:3]
        HPIP = cstb[:, 3:4]

        pAW = pg0[:, 0:24]
        pAW64 = pg0[:, 0:64]
        latP = pg0[:, 0:24]
        synP = py0[:, 0:12]
        psA = pT[0:24, 0:128]
        psB = pT[0:24, 128:256]

        Q = {"sy": [], "v": [], "sa": [], "te": [], "g": []}
        Cn = {"ld": 0, "st": 0, "v": 0, "a": 0, "t": 0, "sy": 0, "g": 0}
        SEM = {"ld": sLd, "st": sSt, "v": sV, "a": sA, "t": sT, "sy": sSy, "g": sG}

        def em(eng, fn):
            Q[eng].append(fn)

        def dma_in(dst, src):
            Cn["ld"] += 16
            tgt = Cn["ld"]
            em("sy", lambda e, d=dst, s=src: e.dma_start(out=d, in_=s).then_inc(sLd, 16))
            return tgt

        def wait(eng, sem_key, val):
            em(eng, lambda e, k=sem_key, vv=val: e.wait_ge(SEM[k], vv))

        def op(eng, fn, inc_key=None):
            if inc_key is None:
                em(eng, fn)
                return None
            Cn[inc_key] += 1
            tgt = Cn[inc_key]

            def _f(e, f=fn, k=inc_key):
                f(e).then_inc(SEM[k], 1)

            em(eng, _f)
            return tgt

        def dma_out(eng_key, dst, src):
            Cn["st"] += 16
            tgt = Cn["st"]
            em(eng_key, lambda e, d=dst, s=src: e.dma_start(out=d, in_=s).then_inc(sSt, 16))
            return tgt

        def barrier():
            dp = {
                "sy": lambda e: e.nop(),
                "v": lambda e: e.engine_nop(),
                "sa": lambda e: e.copy(cstb[:, 7:8], c3buf[:]),
                "te": lambda e: e.ldweights(ident[:]),
                "g": lambda e: e.engine_nop(),
            }
            marks = {}
            for ek, sk in (("sy", "sy"), ("v", "v"), ("sa", "a"), ("te", "t"), ("g", "g")):
                Cn[sk] += 1
                marks[ek] = Cn[sk]
                em(ek, lambda e, k=sk, f=dp[ek]: f(e).then_inc(SEM[k], 1))
            for ek in ("sy", "v", "sa", "te", "g"):
                for ok, sk in (("sy", "sy"), ("v", "v"), ("sa", "a"), ("te", "t"), ("g", "g")):
                    if ok != ek:
                        wait(ek, sk, marks[ok])

        def emit_quant(src, dst, W):
            S = W // 16

            def _q(v, s=src, d=dst, W=W, S=S):
                v.tensor_reduce(qm[:, 0:S], _v3(s), AX, A.max, apply_absolute_value=True)
                v.tensor_copy(sc1[:, 3:4], c3buf[:])
                v.tensor_scalar(qsb[:, 0:64], qm[:, 0:64], SIXTH, 1e-12, A.mult, A.max)
                v.reciprocal(qrb[:, 0:64], qsb[:, 0:64])
                v.tensor_scalar(qs2[:, 0:64], qsb[:, 0:64], 0.5, None, A.mult)
                v.tensor_tensor(_v3(QW[:, :W]), _v3(s), _b16(qrb[:, 0:S], S), A.mult)
                v._custom_dve(FP4A, out=QGA[:, :W], in0=QW[:, :W], in1=c3buf[:],
                              s0=0.0625, s1=0.5625, imm2=1.5625)
                v._custom_dve(FP4B, out=QGB[:, :W], in0=QW[:, :W],
                              s0=6.25, s1=12.25, imm2=25.0)
                v.tensor_tensor(QGA[:, :W], QGA[:, :W], QGB[:, :W], A.add)
                v.tensor_tensor(QGA[:, :W], QGA[:, :W], QSG[:, :W], A.mult)
                return v.tensor_tensor(_v3(d), _v3(QGA[:, :W]), _b16(qs2[:, 0:S], S), A.mult)

            return _q

        for dst, key in ((ident[:], "cident"), (HTs[:], "cHT"), (acf[:], "cacoef"),
                         (kas[:], "cka"), (ka2s[:], "cka2"), (cws[:], "ccw")):
            dma_in(dst, ci[key])
        dma_in(WelS[:], wel.rearrange("(c p) e -> p c e", p=128))
        t_bel = dma_in(negb[:, 0:24], bel.partition_broadcast(128))
        t_act1 = dma_in(AcT_hi_v, ci["cAcThi"])
        t_act2 = dma_in(AcT_lo_v, ci["cAcTlo"])
        consts_ld = Cn["ld"]

        wait("v", "ld", consts_ld)
        op("v", lambda v: v.memset(c3buf[:], 3.0625))
        wait("g", "ld", t_bel)

        def _gsetup(g):
            g.memset(negb[:, 24:64], 0.0)
            g.tensor_scalar(negb[:, 0:24], negb[:, 0:24], -1.0, None, A.mult)
            g.memset(cstb[:, 0:1], -0.5)
            g.memset(cstb[:, 1:2], -5.0)
            g.memset(cstb[:, 2:3], -HALFPI)
            return g.memset(cstb[:, 3:4], HALFPI)

        g_setup = op("g", _gsetup, "g")
        op("v", lambda v: v.tensor_copy(WelH[:], WelS[:]))
        v_wel = op("v", lambda v: v.scalar_tensor_tensor(
            WelL[:].rearrange("p c n -> p (c n)"),
            WelH[:].rearrange("p c n -> p (c n)"), -1.0,
            WelS[:].rearrange("p c n -> p (c n)"), A.mult, A.add), "v")

        wait("te", "v", v_wel)
        wait("te", "ld", t_act2)
        aw_last_t = None
        for nm, AW in (("c", AWc), ("s", AWs)):
            if nm == "s":
                wait("sy", "t", aw_last_t)
                t_ast1 = dma_in(AcT_hi_v, ci["cAsThi"])
                t_ast2 = dma_in(AcT_lo_v, ci["cAsTlo"])
                wait("te", "ld", t_ast2)
            for kc in range(5):
                ksl = slice(kc * 128, (kc + 1) * 128)
                mms = []
                for cc in range(8):
                    mms.append((AcT_hi_v[:, cc, ksl], WelH[:, cc, :]))
                    mms.append((AcT_hi_v[:, cc, ksl], WelL[:, cc, :]))
                    mms.append((AcT_lo_v[:, cc, ksl], WelH[:, cc, :]))
                t_mark = None
                for i, (lh, rh) in enumerate(mms):
                    def _mm(te, lh=lh, rh=rh, st=(i == 0), sp=(i == len(mms) - 1)):
                        return te.matmul(pAW, lh, rh, start=st, stop=sp)
                    if i < len(mms) - 1:
                        op("te", _mm)
                    else:
                        t_mark = op("te", _mm, "t")
                aw_last_t = t_mark
                wait("sa", "t", t_mark)
                a_mark = op("sa", lambda s, d=AW[:, kc, :]: s.copy(d, pAW64), "a")
                wait("te", "a", a_mark)

        op("v", lambda v: v.memset(OWNSUM[:, 0:32], 1.0))
        op("v", lambda v: v.memset(OWNSQ[:, 0:32], 1.0))
        v_stats_base = Cn["v"]
        NT_ALL = STILES + TPC
        for t in range(NT_ALL):
            buf = XT0 if t % 2 == 0 else XT1
            if t >= 2:
                wait("sy", "v", v_stats_base + (t - 1))
            if t < STILES:
                t_ld = dma_in(buf, xfull[t * 128:(t + 1) * 128, :])
            else:
                tt = t - STILES
                t_ld = dma_in(buf, xb[tt * 128:(tt + 1) * 128, :])
            wait("sa", "ld", t_ld)
            if t >= 1:
                wait("sa", "v", v_stats_base + t)
            a_sq0 = op("sa", lambda s, b=buf: s.activation(QW, b, AF.Square), "a")
            wait("v", "ld", t_ld)
            wait("v", "a", a_sq0)

            if t < STILES:
                def _st(v, b=buf, t=t):
                    v.tensor_reduce(SMV3[:, t, 0:1], b, AX, A.add)
                    v.tensor_reduce(SMV3[:, t, 1:2], QW, AX, A.add)
                    v.tensor_reduce(SMN[:, t:t + 1], b, AX, A.min)
                    v.reduce_max(SMX[:, t:t + 1], b, AX)
                    return v.tensor_copy(sc1[:, 3:4], c3buf[:])
            else:
                def _st(v, b=buf, tt=t - STILES):
                    v.tensor_reduce(OWNSUM[:, tt + 32:tt + 33], b, AX, A.add)
                    v.tensor_reduce(OWNSQ[:, tt + 32:tt + 33], QW, AX, A.add)
                    return v.tensor_copy(sc1[:, 3:4], c3buf[:])

            op("v", _st, "v")

        def _wp1(v):
            v.tensor_scalar(NEGM.rearrange("p (a b) -> p a b", b=1),
                            SMV3[:, :, 0:1], -1.0 / 1024.0, None, A.mult)
            return v.tensor_copy(sc1[:, 3:4], c3buf[:])

        v_wp1 = op("v", _wp1, "v")
        wait("sa", "v", v_wp1)
        a_wp1 = op("sa", lambda s: s.activation(MSQ, NEGM, AF.Square), "a")
        wait("v", "a", a_wp1)

        def _wp2(v):
            v.scalar_tensor_tensor(VARE.rearrange("p (a b) -> p a b", b=1),
                                   SMV3[:, :, 1:2], 1.0 / 1024.0,
                                   MSQ.rearrange("p (a b) -> p a b", b=1),
                                   A.mult, A.subtract)
            return v.tensor_scalar(VARE, VARE, 1e-5, None, A.add)

        v_wp2 = op("v", _wp2, "v")
        wait("sa", "v", v_wp2)
        a_wp2 = op("sa", lambda s: s.activation(SQV, VARE, AF.Sqrt), "a")
        wait("v", "a", a_wp2)
        v_wp3 = op("v", lambda v: v.reciprocal(R0G, SQV), "v")
        wait("sa", "v", v_wp3)
        a_wp3 = op("sa", lambda s: s.activation(RRG, R0G, AF.Square), "a")
        wait("v", "a", a_wp3)

        def _wp4(v):
            v.tensor_tensor(BVG, RRG, VARE, A.mult)
            v.tensor_scalar(BVG, BVG, -0.5, 1.5, A.mult, A.add)
            v.tensor_tensor(RSTDG, R0G, BVG, A.mult)
            v.tensor_tensor(SMN, SMN, NEGM, A.add)
            v.tensor_tensor(SMN, SMN, RSTDG, A.mult)
            v.tensor_tensor(SMX, SMX, NEGM, A.add)
            v.tensor_tensor(SMX, SMX, RSTDG, A.mult)
            v.tensor_reduce(sc1[:, 0:1], SMN, AX, A.min)
            v.reduce_max(sc1[:, 1:2], SMX, AX)
            return v.tensor_copy(sc1[:, 3:4], c3buf[:])

        v_wp = op("v", _wp4, "v")

        wait("g", "v", v_wp)
        wait("g", "g", g_setup)

        def _dg1(g):
            g.tensor_copy(dsm[0:1, 3:4], NEGM[0:1, 0:1])
            g.tensor_copy(dsm[0:1, 4:5], RSTDG[0:1, 0:1])
            return g.tensor_tensor(dsm[0:1, 7:8], dsm[0:1, 3:4], dsm[0:1, 4:5], A.mult)

        g_d1 = op("g", _dg1, "g")
        dma_out("g", vec_d[0, :], sc1[:, 0:1])
        st_b = dma_out("g", vec_d[1, :], sc1[:, 1:2])
        wait("sy", "st", st_b)
        dma_in(MNR[:, 0:128], vec_d[0:1, :])
        dma_in(MXR[:, 0:128], vec_d[1:2, :])
        t_x0 = dma_in(XROW, xfull[0:1, :])
        wait("v", "ld", t_x0)

        def _dv1(v):
            v.tensor_reduce(dsm[0:1, 0:1], MNR[:, 0:128], AX, A.min)
            v.reduce_max(dsm[0:1, 1:2], MXR[:, 0:128], AX)
            return v.tensor_copy(sc1[:, 3:4], c3buf[:])

        v_d1 = op("v", _dv1, "v")
        wait("g", "v", v_d1)

        def _dg2(g):
            g.tensor_tensor(dsm[0:1, 5:6], dsm[0:1, 1:2], dsm[0:1, 0:1], A.subtract)
            return g.tensor_scalar(dsm[0:1, 5:6], dsm[0:1, 5:6], 1e-10, None, A.add)

        g_d2 = op("g", _dg2, "g")
        wait("v", "g", g_d2)
        v_d2 = op("v", lambda v: v.reciprocal(dsm[0:1, 2:3], dsm[0:1, 5:6]), "v")
        wait("g", "v", v_d2)

        def _dg3(g):
            g.tensor_tensor(dsm[0:1, 6:7], dsm[0:1, 0:1], dsm[0:1, 2:3], A.mult)
            return g.tensor_scalar(dsm[0:1, 6:7], dsm[0:1, 6:7], -1.0, None, A.mult)

        g_d3 = op("g", _dg3, "g")
        wait("sa", "g", g_d3)

        def _da1(s):
            s.activation(XNROW, XROW, AF.Identity,
                         bias=dsm[0:1, 7:8], scale=dsm[0:1, 4:5])
            return s.activation(ZROW, XNROW, AF.Identity,
                                bias=dsm[0:1, 6:7], scale=dsm[0:1, 2:3])

        a_d1 = op("sa", _da1, "a")
        wait("v", "a", a_d1)
        NNJ = (4, 16, 64, 256, 1024)

        def _dv3(v):
            for j, nn in enumerate(NNJ):
                v.tensor_reduce(dly[0:1, j:j + 1], ZROW[:, 0:nn], AX, A.add)
            return v.tensor_copy(sc1[:, 3:4], c3buf[:])

        v_d3 = op("v", _dv3, "v")
        wait("g", "v", v_d3)

        def _dg4(g):
            last = None
            for j, nn in enumerate(NNJ):
                last = g.tensor_scalar(dly2[0:1, j:j + 1], dly[0:1, j:j + 1],
                                       1.0 / nn, None, A.mult)
            return last

        g_d4 = op("g", _dg4, "g")
        wait("v", "g", g_d4)

        def _dv4(v):
            for j, nn in enumerate(NNJ):
                v.tensor_scalar(MROW, ZROW, dly2[0:1, j:j + 1], None, A.is_gt)
                v.tensor_reduce(dcnt[0:1, j:j + 1], MROW[:, 0:nn], AX, A.add)
            return v.tensor_copy(sc1[:, 3:4], c3buf[:])

        v_d4 = op("v", _dv4, "v")
        wait("g", "v", v_d4)
        g_d5 = op("g", lambda g: g.tensor_scalar(dcnt[0:1, 0:5], dcnt[0:1, 0:5],
                                                 1.0, None, A.max), "g")
        wait("sa", "g", g_d5)
        a_d2 = op("sa", lambda s: s.activation(dll[0:1, 0:5], dcnt[0:1, 0:5], AF.Ln), "a")
        wait("g", "a", a_d2)
        g_d6 = op("g", lambda g: g.tensor_tensor(dslp[0:1, 0:5], dll[0:1, 0:5],
                                                 acf[0:1, :], A.mult), "g")
        wait("v", "g", g_d6)

        def _dv5(v):
            v.tensor_reduce(dsm[0:1, 8:9], dslp[0:1, 0:5], AX, A.add)
            return v.tensor_copy(sc1[:, 3:4], c3buf[:])

        v_d5 = op("v", _dv5, "v")
        wait("g", "v", v_d5)
        g_d7 = op("g", lambda g: g.tensor_scalar(dsm[0:1, 9:10], dsm[0:1, 8:9],
                                                 1.0, 2.0, A.max, A.min), "g")
        wait("g", "g", g_d7)
        st_d = dma_out("g", vec_d[2, 0:1], dsm[0:1, 9:10])
        wait("sy", "st", st_d)
        t_db = dma_in(Db[:], vec_d[2:3, 0:1].partition_broadcast(128))

        wait("sa", "ld", t_db)
        a_f0 = op("sa", lambda s: s.activation(alf0[:], Db[:], AF.Square), "a")
        wait("g", "a", a_f0)

        def _fg1(g):
            g.tensor_scalar(gtm[:], alf0[:], 0.2, None, A.mult)
            g.tensor_tensor(alf[:], gtm[:], Db[:], A.add)
            g.tensor_scalar(bet[:], Db[:], 0.5, 1.0, A.mult, A.add)
            return g.tensor_scalar(gam[:], Db[:], 0.1, None, A.mult)

        g_f1 = op("g", _fg1, "g")
        wait("sa", "g", g_f1)

        def _fa1(s):
            s.activation(tmp5[:], kas[:], AF.Identity, bias=alf[:], scale=bet[:])
            s.activation(ub5[:], tmp5[:], AF.Abs, bias=HPIN)
            s.activation(fr1[:], ub5[:], AF.Sin, bias=HPIP, scale=-1.0)
            return s.activation(t5c[:], ka2s[:], AF.Identity, bias=alf[:], scale=gam[:])

        a_f1 = op("sa", _fa1, "a")
        wait("g", "a", a_f1)
        g_f2 = op("g", lambda g: g.tensor_tensor(t5c[:], t5c[:], kas[:], A.subtract), "g")
        wait("sa", "g", g_f2)
        a_f2 = op("sa", lambda s: s.activation(fr2[:], t5c[:], AF.Sin,
                                               bias=HPIP, scale=-1.0), "a")
        wait("g", "a", a_f2)

        def _fg3(g):
            g.tensor_tensor(cfr[:], fr1[:], fr2[:], A.mult)
            return g.tensor_tensor(cfr[:], cfr[:], cws[:], A.mult)

        g_cfr = op("g", _fg3, "g")

        wait("sy", "t", aw_last_t)
        t_m2 = dma_in(S1, ci["cAchi"])
        t_m2b = dma_in(S2, ci["cAclo"])
        m_last_t = None
        m_last_v = None
        for nm in ("c", "s"):
            AW = AWc if nm == "c" else AWs
            if nm == "s":
                wait("sy", "t", m_last_t)
                t_m2 = dma_in(S1, ci["cAshi"])
                t_m2b = dma_in(S2, ci["cAslo"])
            cfrb32 = cfr[:].rearrange("p (c o) -> p c o", o=1).broadcast_to([128, 5, 32])

            def _rbuild(v, AW=AW, cfrb32=cfrb32):
                return v.tensor_tensor(Rf, AW[:, :, 0:32], cfrb32, A.mult)

            if nm == "c":
                wait("v", "g", g_cfr)
            v_r = op("v", _rbuild, "v")
            op("v", lambda v: v.tensor_copy(Rh[:], Rf))
            v_rl = op("v", lambda v: v.scalar_tensor_tensor(
                Rl[:].rearrange("p c n -> p (c n)"),
                Rh[:].rearrange("p c n -> p (c n)"), -1.0,
                Rf.rearrange("p c n -> p (c n)"), A.mult, A.add), "v")
            wait("te", "v", v_rl)
            wait("te", "ld", t_m2b)
            for jc in range(8):
                jsl = slice(jc * 128, (jc + 1) * 128)
                mms = []
                for kc in range(5):
                    mms.append((S1[:, kc, jsl], Rh[:, kc, 0:24]))
                    mms.append((S1[:, kc, jsl], Rl[:, kc, 0:24]))
                    mms.append((S2[:, kc, jsl], Rh[:, kc, 0:24]))
                t_mk = None
                for i, (lh, rh) in enumerate(mms):
                    def _mm(te, lh=lh, rh=rh, st=(i == 0), sp=(i == len(mms) - 1)):
                        return te.matmul(pAW, lh, rh, start=st, stop=sp)
                    if i < len(mms) - 1:
                        op("te", _mm)
                    else:
                        t_mk = op("te", _mm, "t")
                m_last_t = t_mk
                wait("v", "t", t_mk)
                if nm == "c":
                    m_last_v = op("v", lambda v, jc=jc: v.tensor_copy(Mf[:, jc, :], pAW64), "v")
                else:
                    m_last_v = op("v", lambda v, jc=jc: v.tensor_tensor(
                        Mf[:, jc, :], Mf[:, jc, :], pAW64, A.add), "v")
                wait("te", "v", m_last_v)

        def _msplit(v):
            v.tensor_copy(MfC, Mf[:, :, 0:24])
            v.tensor_copy(Mhi[:], MfC)
            return v.scalar_tensor_tensor(
                Mlo[:].rearrange("p c n -> p (c n)"),
                Mhi[:].rearrange("p c n -> p (c n)"), -1.0,
                MfC.rearrange("p c n -> p (c n)"), A.mult, A.add)

        v_msp = op("v", _msplit, "v")

        c_last_v = v_msp
        for nm in ("c", "s"):
            wait("sy", "t", m_last_t)
            wait("sy", "v", c_last_v)
            t_c1 = dma_in(S1, ci["cAchi"] if nm == "c" else ci["cAshi"])
            t_c2 = dma_in(S2, ci["cAchi"] if nm == "c" else ci["cAshi"])
            wait("v", "ld", t_c2)
            cfrb1k = cfr[:].rearrange("p (c o) -> p c o", o=1).broadcast_to([128, 5, 1024])

            def _pbuild(v, cfrb1k=cfrb1k):
                return v.tensor_tensor(S2, S2, cfrb1k, A.mult)

            v_p = op("v", _pbuild, "v")
            wait("te", "v", v_p)
            wait("te", "ld", t_c1)
            for jc in range(8):
                jsl = slice(jc * 128, (jc + 1) * 128)
                t_ck = None
                for half in range(2):
                    hsl = slice(half * 512, (half + 1) * 512)
                    for kc in range(5):
                        def _mm(te, kc=kc, jsl=jsl, hsl=hsl, st=(kc == 0), sp=(kc == 4)):
                            return te.matmul(pdn[:, hsl], S1[:, kc, jsl],
                                             S2[:, kc, hsl], start=st, stop=sp)
                        if kc < 4 or half == 0:
                            op("te", _mm)
                        else:
                            t_ck = op("te", _mm, "t")
                wait("v", "t", t_ck)
                if nm == "c":
                    c_last_v = op("v", lambda v, jc=jc: v.tensor_copy(Chi[:, jc, :], pdn[:]), "v")
                else:
                    c_last_v = op("v", lambda v, jc=jc: v.tensor_add(Chi[:, jc, :], Chi[:, jc, :], pdn[:]), "v")
                wait("te", "v", c_last_v)

        barrier()

        prev_v = None
        prev_tr = None

        def wq_tile(rows_src, m, dst_fn, subw):
            nonlocal prev_v, prev_tr
            marks = []
            for s_i, w in enumerate(subw):
                o = sum(subw[:s_i])
                if prev_v is not None:
                    wait("sy", "v", prev_v)
                t_ld = dma_in(XT0[:, 0:w], rows_src[:, o:o + w])
                wait("sa", "ld", t_ld)
                if prev_v is not None:
                    wait("sa", "v", prev_v)
                a_sg = op("sa", lambda s, w=w: s.activation(QSG[:, 0:w], XT0[:, 0:w], AF.Sign), "a")
                wait("v", "ld", t_ld)
                wait("v", "a", a_sg)
                if prev_tr is not None:
                    wait("v", "t", prev_tr)
                v_mk = op("v", emit_quant(XT0[:, 0:w], QOUT[:, o:o + w], w), "v")
                marks.append(v_mk)
                prev_v = v_mk
            nblk = sum(subw) // 128
            wait("te", "v", marks[-1])
            for g0 in range(0, nblk, 8):
                gn = min(8, nblk - g0)
                t_tr = None
                for b in range(gn):
                    def _tr(te, b=b, g0=g0):
                        return te.transpose(pT[:, b * 128:(b + 1) * 128],
                                            QOUT[:, (g0 + b) * 128:(g0 + b + 1) * 128],
                                            ident[:])
                    if b < gn - 1:
                        op("te", _tr)
                    else:
                        t_tr = op("te", _tr, "t")
                prev_tr = t_tr
                wait("sa", "t", t_tr)
                a_ev = op("sa", lambda s, d=dst_fn(m, g0, gn), gn=gn:
                          s.copy(d, pT[:, 0:gn * 128].rearrange("p (c n) -> p c n", n=128)), "a")
                wait("te", "a", a_ev)

        for m in range(22):
            wq_tile(wg[m * 128:(m + 1) * 128, :], m,
                    lambda m, g0, gn: WGb[:, g0:g0 + gn, m * 128:(m + 1) * 128], [1024])
        for m in range(22):
            wq_tile(wu[m * 128:(m + 1) * 128, :], m,
                    lambda m, g0, gn: WUb[:, g0:g0 + gn, m * 128:(m + 1) * 128], [1024])
        for m in range(8):
            wq_tile(wd[m * 128:(m + 1) * 128, :], m,
                    lambda m, g0, gn: WDb[:, g0:g0 + gn, m * 128:(m + 1) * 128],
                    [1024, 1024, 768])

        barrier()

        def _owp1(v):
            v.tensor_scalar(NEGM64, OWNSUM, -1.0 / 1024.0, None, A.mult)
            return v.tensor_copy(sc1[:, 3:4], c3buf[:])

        v_o1 = op("v", _owp1, "v")
        wait("sa", "v", v_o1)
        a_o1 = op("sa", lambda s: s.activation(MSQ64, NEGM64, AF.Square), "a")
        wait("v", "a", a_o1)

        def _owp2(v):
            v.scalar_tensor_tensor(VARE64, OWNSQ, 1.0 / 1024.0, MSQ64,
                                   A.mult, A.subtract)
            return v.tensor_scalar(VARE64, VARE64, 1e-5, None, A.add)

        v_o2 = op("v", _owp2, "v")
        wait("sa", "v", v_o2)
        a_o2 = op("sa", lambda s: s.activation(SQ64, VARE64, AF.Sqrt), "a")
        wait("v", "a", a_o2)
        v_o3 = op("v", lambda v: v.reciprocal(R064, SQ64), "v")
        wait("sa", "v", v_o3)
        a_o3 = op("sa", lambda s: s.activation(RR64, R064, AF.Square), "a")
        wait("v", "a", a_o3)

        def _owp4(v):
            v.tensor_tensor(B64, RR64, VARE64, A.mult)
            v.tensor_scalar(B64, B64, -0.5, 1.5, A.mult, A.add)
            return v.tensor_tensor(RSTD64, R064, B64, A.mult)

        v_own = op("v", _owp4, "v")

        ln_marks = []
        tr_last = []
        for t in range(TPC):
            buf = XT0 if t % 2 == 0 else XT1
            if t >= 2:
                wait("sy", "v", ln_marks[t - 2])
            t_ld = dma_in(buf, xb[t * 128:(t + 1) * 128, :])
            wait("v", "ld", t_ld)
            v_ln = op("v", lambda v, b=buf, t=t: v.tensor_scalar(
                XNF, b, NEGM64[:, 32 + t:33 + t], RSTD64[:, 32 + t:33 + t],
                A.add, A.mult), "v")
            ln_marks.append(v_ln)
            wait("sa", "v", v_ln)
            if t >= 1:
                wait("sa", "t", tr_last[t - 1])

            def _ahi(s):
                s.activation(QSG[:, 0:1024], XNF, AF.Sign)
                return s.activation(EV1, XNF, AF.Copy)

            a_hi = op("sa", _ahi, "a")
            wait("v", "a", a_hi)
            if t >= 1:
                wait("v", "t", tr_last[t - 1])
            op("v", emit_quant(XNF, QOUT[:, 0:1024], 1024))
            v_a = op("v", lambda v: v.scalar_tensor_tensor(EV2, EV1, -1.0, XNF, A.mult, A.add), "v")

            wait("te", "v", v_a)
            for src, dstd in ((QOUT[:, 0:1024], xqT_d), (EV1, xhT_d), (EV2, xlT_d)):
                t_tr = None
                for b in range(8):
                    def _tr(te, src=src, b=b):
                        return te.transpose(pT[:, b * 128:(b + 1) * 128],
                                            src[:, b * 128:(b + 1) * 128], ident[:])
                    if b < 7:
                        op("te", _tr)
                    else:
                        t_tr = op("te", _tr, "t")
                wait("sa", "t", t_tr)
                a_ev = op("sa", lambda s: s.copy(TSTG, pT[:]), "a")
                wait("te", "a", a_ev)
                wait("g", "a", a_ev)
                st_mk = dma_out("g", dstd[t], TSTG.rearrange("p (c n) -> p c n", n=128))
                wait("sa", "st", st_mk)
            tr_last.append(t_tr)

        barrier()

        chunks = [(0, 512), (512, 512), (1024, 512), (1536, 512), (2048, 512), (2560, 256)]
        sil_marks, h_marks, hq_marks, tr_marks, gy_marks = [], [], [], [], []
        tile_last_gy = []
        prev_g_st = None
        pending_tr = None

        def emit_tr(gk, io, iw):
            wait("te", "v", hq_marks[gk])
            nb = iw // 128
            tr_mk = None
            for b in range(nb):
                def _tr(te, b=b, io=io):
                    return te.transpose(pT[:, b * 128:(b + 1) * 128],
                                        QOUT[:, io + b * 128: io + (b + 1) * 128],
                                        ident[:])
                if b < nb - 1:
                    op("te", _tr)
                else:
                    tr_mk = op("te", _tr, "t")
            tr_marks.append(tr_mk)
            wait("sa", "t", tr_mk)
            a_ev = op("sa", lambda s, io=io, nb=nb:
                      s.copy(HQT[:, io // 128: io // 128 + nb, :],
                             pT[:, 0:nb * 128].rearrange("p (c n) -> p c n", n=128)), "a")
            wait("te", "a", a_ev)

        for t in range(TPC):
            xq = XQT0 if t % 2 == 0 else XQT1
            if t >= 2:
                wait("sy", "t", tile_last_gy[t - 2])
            t_ld = dma_in(xq, xqT_d[t])
            wait("te", "ld", t_ld)
            for ci_, (io, iw) in enumerate(chunks):
                gk = t * 6 + ci_
                pg = pg0 if gk % 2 == 0 else pg1
                py = py0 if gk % 2 == 0 else py1
                if gk >= 2:
                    wait("te", "a", sil_marks[gk - 2])
                    wait("te", "v", h_marks[gk - 2])
                for cc in range(8):
                    def _mmg(te, cc=cc, io=io, iw=iw, pg=pg, xq=xq,
                             st=(cc == 0), sp=(cc == 7)):
                        return te.matmul(pg[:, 0:iw], xq[:, cc, :],
                                         WGb[:, cc, io:io + iw], start=st, stop=sp)
                    op("te", _mmg)
                gy_mk = None
                for cc in range(8):
                    def _mmy(te, cc=cc, io=io, iw=iw, py=py, xq=xq,
                             st=(cc == 0), sp=(cc == 7)):
                        return te.matmul(py[:, 0:iw], xq[:, cc, :],
                                         WUb[:, cc, io:io + iw], start=st, stop=sp)
                    if cc < 7:
                        op("te", _mmy)
                    else:
                        gy_mk = op("te", _mmy, "t")
                gy_marks.append(gy_mk)
                if pending_tr is not None:
                    emit_tr(*pending_tr)
                    pending_tr = None
                wait("sa", "t", gy_mk)
                if gk >= 1:
                    wait("sa", "v", h_marks[gk - 1])
                sil_marks.append(op("sa", lambda s, pg=pg, iw=iw:
                                    s.activation(SIL[:, 0:iw], pg[:, 0:iw], AF.Silu), "a"))
                wait("v", "a", sil_marks[gk])
                h_marks.append(op("v", lambda v, py=py, iw=iw:
                                  v.tensor_mul(HB[:, 0:iw], SIL[:, 0:iw], py[:, 0:iw]), "v"))
                wait("sa", "v", h_marks[gk])
                if gk >= 1:
                    wait("sa", "v", hq_marks[gk - 1])
                a_sgn = op("sa", lambda s, iw=iw:
                           s.activation(QSG[:, 0:iw], HB[:, 0:iw], AF.Sign), "a")
                wait("v", "a", a_sgn)
                if gk >= 6:
                    wait("v", "t", tr_marks[gk - 6])
                hq_marks.append(op("v", emit_quant(HB[:, 0:iw], QOUT[:, io:io + iw], iw), "v"))
                pending_tr = (gk, io, iw)
            emit_tr(*pending_tr)
            pending_tr = None
            tile_last_gy.append(gy_marks[-1])
            t_dn = None
            for ic in range(22):
                for half in range(2):
                    def _mmd(te, ic=ic, half=half, st=(ic == 0), sp=(ic == 21)):
                        return te.matmul(pdn[:, half * 512:(half + 1) * 512],
                                         HQT[:, ic, :], WDb[:, ic, half * 512:(half + 1) * 512],
                                         start=st, stop=sp)
                    if ic == 21 and half == 1:
                        t_dn = op("te", _mmd, "t")
                    else:
                        op("te", _mmd)
            wait("sa", "t", t_dn)
            if prev_g_st is not None:
                wait("sa", "st", prev_g_st)
            a_g = op("sa", lambda s: s.activation(GSTGF, pdn[:], AF.Silu), "a")
            wait("g", "a", a_g)
            prev_g_st = dma_out("g", g_d[t], GSTGF)
            wait("te", "a", a_g)

        barrier()

        wait("sy", "st", prev_g_st)
        t_ble = dma_in(BLEB, ble.partition_broadcast(128))
        wait("v", "ld", t_ble)
        Cn["st"] += 16
        _wle_st = Cn["st"]
        em("g", lambda e: e.dma_start(out=WleB[:], in_=wle).then_inc(sSt, 16))
        wait("te", "st", _wle_st)

        def _gmask(g):
            g.memset(maskb[:], 0.0)
            return g.memset(maskb[:, 0:12], 1.0)

        g_mask = op("g", _gmask, "g")
        wait("v", "g", g_mask)
        prev_cb = None
        prev_sty = None
        for t in range(TPC):
            if prev_cb is not None:
                wait("sy", "v", prev_cb)
                wait("sy", "st", prev_sty)
            dma_in(XHT, xhT_d[t])
            t2 = dma_in(XLT, xlT_d[t])
            t3 = dma_in(XT0, xb[t * 128:(t + 1) * 128, :])
            t4 = dma_in(GTF, g_d[t])
            wait("te", "ld", t2)
            for half in range(2):
                hsl = slice(half * 512, (half + 1) * 512)
                for cc in range(8):
                    def _mmx(te, cc=cc, hsl=hsl, st=(cc == 0)):
                        return te.matmul(pdn[:, hsl], XHT[:, cc, :],
                                         Chi[:, cc, hsl], start=st, stop=False)
                    op("te", _mmx)
            lat_mms = ([(XHT[:, cc, :], Mhi[:, cc, :]) for cc in range(8)]
                       + [(XHT[:, cc, :], Mlo[:, cc, :]) for cc in range(8)]
                       + [(XLT[:, cc, :], Mhi[:, cc, :]) for cc in range(8)])
            t_lat = None
            for i, (lh, rh) in enumerate(lat_mms):
                def _mml(te, lh=lh, rh=rh, st=(i == 0), sp=(i == 23)):
                    return te.matmul(latP, lh, rh, start=st, stop=sp)
                if i < 23:
                    op("te", _mml)
                else:
                    t_lat = op("te", _mml, "t")
            wait("v", "t", t_lat)
            v_bits = op("v", lambda v: v.tensor_tensor(
                BITSF, pg0[:, 0:64], negb[:], A.is_gt), "v")
            wait("sa", "v", v_bits)
            a_bc = op("sa", lambda s: s.copy(bits[:], BITSF[:, 0:24]), "a")
            wait("te", "a", a_bc)
            t_bt = op("te", lambda te: te.transpose(psA, bits[:], ident[:]), "t")
            wait("sa", "t", t_bt)
            a_bt = op("sa", lambda s: s.copy(bitsT[:], psA), "a")
            wait("te", "a", a_bt)
            t_syn = op("te", lambda te: te.matmul(synP, bitsT[:], HTs[:], start=True, stop=True), "t")
            wait("v", "t", t_syn)

            def _golay(v):
                v.tensor_scalar(synpm, py0[:, 0:64], 0.5, None, A.is_gt)
                v.tensor_reduce(sc1[:, 2:3], synpm[:, 0:12], AX, A.add)
                v.tensor_copy(sc1[:, 3:4], c3buf[:])
                v.tensor_copy(GW, sc1[:, 2:3].broadcast_to([128, 64]))
                v.tensor_scalar(GU, GW, 3.5, None, A.is_lt)
                v.tensor_tensor(GA, synpm, GU, A.mult)
                v.tensor_tensor(GF, GA, maskb, A.mult)
                v.tensor_tensor(GD, BITSF, GF, A.mult)
                v.tensor_tensor(GE, BITSF, GF, A.add)
                return v.scalar_tensor_tensor(CORRF, GD, -2.0, GE, A.mult, A.add)

            v_go = op("v", _golay, "v")
            wait("sa", "v", v_go)

            def _acorr(s):
                s.copy(corr[:, 0:12], CORRF[:, 0:12])
                return s.copy(corr[:, 12:24], BITSF[:, 12:24])

            a_co = op("sa", _acorr, "a")
            wait("te", "a", a_co)
            t_ct = op("te", lambda te: te.transpose(psB, corr[:], ident[:]), "t")
            wait("sa", "t", t_ct)
            a_ct = op("sa", lambda s: s.copy(corrT[:], psB), "a")
            wait("te", "a", a_ct)
            t_wle2 = None
            for half in range(2):
                def _mmw(te, half=half):
                    return te.matmul(pdn[:, half * 512:(half + 1) * 512], corrT[:],
                                     WleB[:, half * 512:(half + 1) * 512],
                                     start=False, stop=True)
                if half == 0:
                    op("te", _mmw)
                else:
                    t_wle2 = op("te", _mmw, "t")
            wait("v", "t", t_wle2)
            wait("v", "ld", max(t3, t4))

            def _comb(v):
                v.tensor_add(SSB, pdn[:], BLEB)
                v.tensor_mul(SSB, SSB, GTF)
                return v.tensor_add(OSB, SSB, XT0)

            prev_cb = op("v", _comb, "v")
            wait("te", "v", prev_cb)
            wait("g", "v", prev_cb)
            prev_sty = dma_out("g", yout[t * 128:(t + 1) * 128, :], OSB)

        wait("g", "st", Cn["st"])

        @block.sync
        def _(e):
            for fn in Q["sy"]:
                fn(e)

        @block.vector
        def _(e):
            for fn in Q["v"]:
                fn(e)

        @block.scalar
        def _(e):
            for fn in Q["sa"]:
                fn(e)

        @block.tensor
        def _(e):
            for fn in Q["te"]:
                fn(e)

        @block.gpsimd
        def _(e):
            for fn in Q["g"]:
                fn(e)

    mybir.codegen_inst_isa_subclasses(nc)
    return nc


def kernel(L0_attn_x, L0_attn_W_el, L0_attn_b_el, L0_attn_W_le, L0_attn_b_le,
           L0_ffn_w_gate, L0_ffn_w_up, L0_ffn_w_down):
    x = np.asarray(L0_attn_x)
    xf = np.ascontiguousarray(x.reshape(ROWS, E))
    if "nc" not in _NC_CACHE:
        _NC_CACHE["nc"] = build_program()
    nc = _NC_CACHE["nc"]
    base = {
        "xfull": xf,
        "wg": np.ascontiguousarray(np.asarray(L0_ffn_w_gate)),
        "wu": np.ascontiguousarray(np.asarray(L0_ffn_w_up)),
        "wd": np.ascontiguousarray(np.asarray(L0_ffn_w_down)),
        "wel": np.ascontiguousarray(np.asarray(L0_attn_W_el)),
        "bel": np.ascontiguousarray(np.asarray(L0_attn_b_el).reshape(1, 24)),
        "wle": np.ascontiguousarray(np.asarray(L0_attn_W_le)),
        "ble": np.ascontiguousarray(np.asarray(L0_attn_b_le).reshape(1, E)),
    }
    for name, arr in _CONSTS.items():
        base[name] = arr
    in_maps = []
    for c in range(N_CORES):
        m = dict(base)
        m["xb"] = np.ascontiguousarray(x[c])
        in_maps.append(m)
    res = run_bass_kernel_spmd(nc, in_maps, core_ids=list(range(N_CORES)))
    out = np.stack([res.results[c]["yout"] for c in range(N_CORES)], axis=0)
    return out.astype(np.float32, copy=False)


# version 09
# speedup vs baseline: 1.2281x; 1.0644x over previous
import math

import numpy as np
import ml_dtypes

import concourse.bass as bass
import concourse.mybir as mybir
from concourse.bass_utils import run_bass_kernel_spmd
from concourse.dve_op import DveOp
from concourse.dve_spec import Spec, Src0, C0, C1, C2, C3, lower as dve_lower, sq
from concourse.dve_table_gen import dve_ver_for
import concourse.dve_ops as dve_ops_mod

F32 = mybir.dt.float32
BF16 = mybir.dt.bfloat16
A = mybir.AluOpType
AF = mybir.ActivationFunctionType
AX = mybir.AxisListType.X
BF = ml_dtypes.bfloat16

B_, SEQ, E = 8, 4096, 1024
ROWS = B_ * SEQ
TPC = SEQ // 128
STILES = ROWS // 128
I_FFN = 2816
HALFPI = float(np.float32(math.pi / 2.0))
SIXTH = float(np.float32(1.0 / 6.0))
N_CORES = 8


def _register_op(name, spec, reference):
    for op_ in dve_ops_mod.OPS:
        if op_.name == name:
            return op_
    ver = dve_ver_for("TRN2")
    sha = dve_lower(spec, ver=ver, perf_en=False, name=name).sha(ver)
    op_ = DveOp(name, spec, subdim=False, uops_sha={ver: sha}, reference=reference)
    dve_ops_mod.OPS.append(op_)
    row = max(dve_ops_mod._SUB_OPCODE_FOR_NAME.values()) + 1
    assert row < 0x20, "custom-DVE opcode rows exhausted"
    dve_ops_mod._SUB_OPCODE_FOR_NAME[name] = row
    dve_ops_mod.CUSTOM_DVE_REFS[name] = reference
    return op_


def _fp4a_ref(in0, in1, s0, s1, imm2):
    y = in0 * in0
    return (y > s0) * 1.0 + (y > s1) + (y > imm2) + (y > in1)


def _fp4b_ref(in0, in1, s0, s1, imm2):
    y = in0 * in0
    return 2.0 * ((y > s0) * 1.0 + (y > s1) + 2.0 * (y > imm2))


_y = sq(Src0)
FP4A = _register_op(
    "ANT_FP4A", Spec(body=(_y > C0) + (_y > C1) + ((_y > C2) + (_y > C3))), _fp4a_ref
)
_c7 = _y > C2
_inner = ((_y > C0) + (_y > C1)) + (_c7 + _c7)
FP4B = _register_op("ANT_FP4B", Spec(body=_inner + _inner), _fp4b_ref)


def _build_consts():
    k = np.arange(513, dtype=np.float64)
    n = np.arange(1024, dtype=np.float64)
    ang = 2.0 * np.pi * np.outer(k, n) / 1024.0
    Acp = np.zeros((640, 1024), dtype=np.float64)
    Asp = np.zeros((640, 1024), dtype=np.float64)
    Acp[:513] = np.cos(ang)
    Asp[:513] = np.sin(ang)
    ac_ch = np.ascontiguousarray(Acp.reshape(5, 128, 1024).transpose(1, 0, 2)).astype(np.float32)
    as_ch = np.ascontiguousarray(Asp.reshape(5, 128, 1024).transpose(1, 0, 2)).astype(np.float32)
    Ac_hi = ac_ch.astype(BF)
    As_hi = as_ch.astype(BF)
    AcT = np.ascontiguousarray(Acp.T.reshape(8, 128, 640).transpose(1, 0, 2)).astype(np.float32)
    AsT = np.ascontiguousarray(Asp.T.reshape(8, 128, 640).transpose(1, 0, 2)).astype(np.float32)
    AcT_hi = AcT.astype(BF)
    AsT_hi = AsT.astype(BF)
    AcT_lo = (AcT - AcT_hi.astype(np.float32)).astype(BF)
    AsT_lo = (AsT - AsT_hi.astype(np.float32)).astype(BF)
    cw = np.zeros(640, dtype=np.float64)
    cw[:513] = 2.0 / 1024.0
    cw[0] = 1.0 / 1024.0
    cw[512] = 1.0 / 1024.0
    cconst = np.ascontiguousarray(cw.reshape(5, 128).T.astype(np.float32))
    kk = np.fft.fftfreq(1024).astype(np.float32)[:513].copy()
    kk[512] = np.float32(-0.5)
    ka = (np.abs(kk) + np.float32(1e-8)).astype(np.float32)
    ka2 = (ka * ka).astype(np.float32)
    kap = np.zeros(640, dtype=np.float32)
    ka2p = np.zeros(640, dtype=np.float32)
    kap[:513] = ka
    ka2p[:513] = ka2
    ka_ch = np.ascontiguousarray(kap.reshape(5, 128).T)
    ka2_ch = np.ascontiguousarray(ka2p.reshape(5, 128).T)
    lx = np.log(np.array([2.0, 4.0, 8.0, 16.0, 32.0], dtype=np.float32)).astype(np.float64)
    den = 5.0 * np.sum(lx * lx) - np.sum(lx) ** 2
    acoef = np.ascontiguousarray((-(5.0 * lx - np.sum(lx)) / den).astype(np.float32)[None, :])
    ident = np.eye(128, dtype=np.float32).astype(BF)
    _A = np.array(
        [[1,1,1,0,1,1,1,0,0,0,1,1],[1,1,0,1,1,1,0,0,0,1,1,0],[1,0,1,1,1,0,0,0,1,1,0,1],
         [0,1,1,1,0,0,0,1,1,0,1,1],[0,0,0,0,1,1,1,1,1,1,1,1],[1,1,1,1,0,0,0,0,0,0,0,0],
         [0,0,0,0,1,1,1,1,0,0,0,0],[1,1,0,0,1,1,0,0,1,1,0,0],[1,0,1,0,1,0,1,0,1,0,1,0],
         [0,1,0,1,0,1,0,1,0,1,0,1],[1,0,0,1,1,0,0,1,1,0,0,1],[0,1,1,0,0,1,1,0,0,1,1,0]],
        dtype=np.float32)
    H = np.concatenate([_A.T, np.eye(12, dtype=np.float32)], axis=1)
    HT = np.ascontiguousarray(H.T).astype(BF)
    Ac_lo = (ac_ch - Ac_hi.astype(np.float32)).astype(BF)
    As_lo = (as_ch - As_hi.astype(np.float32)).astype(BF)
    return dict(cAchi=Ac_hi, cAshi=As_hi, cAclo=Ac_lo, cAslo=As_lo,
                cAcThi=AcT_hi, cAcTlo=AcT_lo, cAsThi=AsT_hi, cAsTlo=AsT_lo,
                ccw=cconst, cka=ka_ch, cka2=ka2_ch, cacoef=acoef,
                cident=ident, cHT=HT)


_CONSTS = _build_consts()
_NC_CACHE = {}


def _v3(ap, n=16):
    return ap.rearrange("p (s n) -> p s n", n=n)


def _b16(small_ap, S):
    return small_ap.rearrange("p (s o) -> p s o", o=1).broadcast_to([128, S, 16])


def build_program():
    nc = bass.Bass("TRN2", target_bir_lowering=False, debug=False)

    xb = nc.dram_tensor("xb", [SEQ, E], F32, kind="ExternalInput").ap()
    xfull = nc.dram_tensor("xfull", [ROWS, E], F32, kind="ExternalInput").ap()
    wg = nc.dram_tensor("wg", [I_FFN, E], F32, kind="ExternalInput").ap()
    wu = nc.dram_tensor("wu", [I_FFN, E], F32, kind="ExternalInput").ap()
    wd = nc.dram_tensor("wd", [E, I_FFN], F32, kind="ExternalInput").ap()
    wel = nc.dram_tensor("wel", [E, 24], F32, kind="ExternalInput").ap()
    bel = nc.dram_tensor("bel", [1, 24], F32, kind="ExternalInput").ap()
    wle = nc.dram_tensor("wle", [24, E], F32, kind="ExternalInput").ap()
    ble = nc.dram_tensor("ble", [1, E], F32, kind="ExternalInput").ap()
    yout = nc.dram_tensor("yout", [SEQ, E], F32, kind="ExternalOutput").ap()
    ci = {}
    for name, arr in _CONSTS.items():
        dt = BF16 if arr.dtype == BF else F32
        ci[name] = nc.dram_tensor(name, list(arr.shape), dt, kind="ExternalInput").ap()

    xqT_d = nc.dram_tensor("xqT_d", [TPC, 128, 8, 128], BF16, kind="Internal").ap()
    xhT_d = nc.dram_tensor("xhT_d", [TPC, 128, 8, 128], BF16, kind="Internal").ap()
    xlT_d = nc.dram_tensor("xlT_d", [TPC, 128, 8, 128], BF16, kind="Internal").ap()
    g_d = nc.dram_tensor("g_d", [TPC, 128, E], F32, kind="Internal").ap()
    vec_d = nc.dram_tensor("vec_d", [4, 128], F32, kind="Internal").ap()

    from contextlib import ExitStack
    es = ExitStack()
    with es:
        WGb = es.enter_context(nc.sbuf_tensor("WGb", [128, 8, I_FFN], BF16))
        WUb = es.enter_context(nc.sbuf_tensor("WUb", [128, 8, I_FFN], BF16))
        WDb = es.enter_context(nc.sbuf_tensor("WDb", [128, 22, E], BF16))
        Chi = es.enter_context(nc.sbuf_tensor("Chi", [128, 8, E], BF16))
        Mhi = es.enter_context(nc.sbuf_tensor("Mhi", [128, 8, 24], BF16))
        Mlo = es.enter_context(nc.sbuf_tensor("Mlo", [128, 8, 24], BF16))
        AF_ = es.enter_context(nc.sbuf_tensor("AF_", [128, 6144], F32))
        ABF = es.enter_context(nc.sbuf_tensor("ABF", [128, 13824], BF16))
        ident = es.enter_context(nc.sbuf_tensor("ident", [128, 128], BF16))
        HTs = es.enter_context(nc.sbuf_tensor("HTs", [24, 12], BF16))
        negb = es.enter_context(nc.sbuf_tensor("negb", [128, 64], F32))
        c3buf = es.enter_context(nc.sbuf_tensor("c3buf", [128, 1], F32))
        sc1 = es.enter_context(nc.sbuf_tensor("sc1", [128, 4], F32))
        cstb = es.enter_context(nc.sbuf_tensor("cstb", [128, 8], F32))
        dsm = es.enter_context(nc.sbuf_tensor("dsm", [1, 16], F32))
        dly = es.enter_context(nc.sbuf_tensor("dly", [1, 8], F32))
        dly2 = es.enter_context(nc.sbuf_tensor("dly2", [1, 8], F32))
        dcnt = es.enter_context(nc.sbuf_tensor("dcnt", [1, 8], F32))
        dll = es.enter_context(nc.sbuf_tensor("dll", [1, 8], F32))
        dslp = es.enter_context(nc.sbuf_tensor("dslp", [1, 8], F32))
        acf = es.enter_context(nc.sbuf_tensor("acf", [1, 5], F32))
        Db = es.enter_context(nc.sbuf_tensor("Db", [128, 1], F32))
        alf0 = es.enter_context(nc.sbuf_tensor("alf0", [128, 1], F32))
        gtm = es.enter_context(nc.sbuf_tensor("gtm", [128, 1], F32))
        alf = es.enter_context(nc.sbuf_tensor("alf", [128, 1], F32))
        bet = es.enter_context(nc.sbuf_tensor("bet", [128, 1], F32))
        gam = es.enter_context(nc.sbuf_tensor("gam", [128, 1], F32))
        kas = es.enter_context(nc.sbuf_tensor("kas", [128, 5], F32))
        ka2s = es.enter_context(nc.sbuf_tensor("ka2s", [128, 5], F32))
        cws = es.enter_context(nc.sbuf_tensor("cws", [128, 5], F32))
        tmp5 = es.enter_context(nc.sbuf_tensor("tmp5", [128, 5], F32))
        t5c = es.enter_context(nc.sbuf_tensor("t5c", [128, 5], F32))
        ub5 = es.enter_context(nc.sbuf_tensor("ub5", [128, 5], F32))
        fr1 = es.enter_context(nc.sbuf_tensor("fr1", [128, 5], F32))
        fr2 = es.enter_context(nc.sbuf_tensor("fr2", [128, 5], F32))
        cfr = es.enter_context(nc.sbuf_tensor("cfr", [128, 5], F32))
        WelS = es.enter_context(nc.sbuf_tensor("WelS", [128, 8, 24], F32))
        WelH = es.enter_context(nc.sbuf_tensor("WelH", [128, 8, 24], BF16))
        WelL = es.enter_context(nc.sbuf_tensor("WelL", [128, 8, 24], BF16))
        AWc = es.enter_context(nc.sbuf_tensor("AWc", [128, 5, 64], F32))
        AWs = es.enter_context(nc.sbuf_tensor("AWs", [128, 5, 64], F32))
        Rh = es.enter_context(nc.sbuf_tensor("Rh", [128, 5, 32], BF16))
        Rl = es.enter_context(nc.sbuf_tensor("Rl", [128, 5, 32], BF16))
        Mf = es.enter_context(nc.sbuf_tensor("Mf", [128, 8, 64], F32))
        bits = es.enter_context(nc.sbuf_tensor("bits", [128, 24], BF16))
        corr = es.enter_context(nc.sbuf_tensor("corr", [128, 24], BF16))
        pg0 = es.enter_context(nc.psum_tensor("pg0", [128, 512], F32))
        pg1 = es.enter_context(nc.psum_tensor("pg1", [128, 512], F32))
        py0 = es.enter_context(nc.psum_tensor("py0", [128, 512], F32))
        py1 = es.enter_context(nc.psum_tensor("py1", [128, 512], F32))
        pdn = es.enter_context(nc.psum_tensor("pdn", [128, 1024], F32))
        pT = es.enter_context(nc.psum_tensor("pT", [128, 1024], BF16))
        sLd = es.enter_context(nc.semaphore("sLd"))
        sL2 = es.enter_context(nc.semaphore("sL2"))
        sSt = es.enter_context(nc.semaphore("sSt"))
        sV = es.enter_context(nc.semaphore("sV"))
        sA = es.enter_context(nc.semaphore("sA"))
        sT = es.enter_context(nc.semaphore("sT"))
        sSy = es.enter_context(nc.semaphore("sSy"))
        sG = es.enter_context(nc.semaphore("sG"))
        block = es.enter_context(nc.Block())
        XT0 = AF_[:, 0:1024]
        XT1 = AF_[:, 1024:2048]
        QW = AF_[:, 2048:3072]
        SIL = AF_[:, 3072:3584]
        HB = AF_[:, 3584:4096]
        SMVS = AF_[:, 4096:4352]
        SMVQ = AF_[:, 4352:4608]
        SMN = AF_[:, 4608:4864]
        SMX = AF_[:, 4864:5120]
        XNF = AF_[:, 5120:6144]
        BLEB = AF_[:, 5120:6144]
        SSB = AF_[:, 4096:5120]
        OSB = AF_[:, 3072:4096]
        NEGM = AF_[:, 2048:2304]
        MSQ = AF_[:, 2304:2560]
        VARE = AF_[:, 2560:2816]
        SQV = AF_[:, 2816:3072]
        R0G = AF_[:, 3072:3328]
        RRG = AF_[:, 3328:3584]
        BVG = AF_[:, 3584:3840]
        RSTDG = AF_[:, 3840:4096]
        OWNSUM = AF_[:, 5120:5184]
        OWNSQ = AF_[:, 5184:5248]
        NEGM64 = AF_[:, 3072:3136]
        RSTD64 = AF_[:, 3136:3200]
        MSQ64 = AF_[:, 5248:5312]
        VARE64 = AF_[:, 5312:5376]
        SQ64 = AF_[:, 5376:5440]
        R064 = AF_[:, 5440:5504]
        RR64 = AF_[:, 5504:5568]
        B64 = AF_[:, 5568:5632]
        MfC = AF_[:, 2048:2240].rearrange("p (c n) -> p c n", n=24)
        Rf = AF_[:, 2048:2208].rearrange("p (c n) -> p c n", n=32)
        qm = AF_[:, 4096:4160]
        qsb = AF_[:, 4160:4224]
        qrb = AF_[:, 4224:4288]
        qs2 = AF_[:, 4288:4352]
        XROW = AF_[0:1, 0:1024]
        XNROW = AF_[0:1, 1024:2048]
        ZROW = AF_[0:1, 3072:4096]
        MROW = AF_[0:1, 0:1024]
        MNR = AF_[0:1, 2048:2176]
        MXR = AF_[0:1, 2176:2304]

        S1 = ABF[:, 0:5120].rearrange("p (c n) -> p c n", n=1024)
        S2 = ABF[:, 5120:10240].rearrange("p (c n) -> p c n", n=1024)
        S1f = ABF[:, 0:5120]
        S2f = ABF[:, 5120:10240]
        AcT_hi_v = S1f.rearrange("p (c n) -> p c n", n=640)
        AcT_lo_v = S2f.rearrange("p (c n) -> p c n", n=640)
        QGA = ABF[:, 0:1024]
        QGB = ABF[:, 1024:2048]
        QSG = ABF[:, 2048:3072]
        QOUT = ABF[:, 3072:5888]
        EV1 = ABF[:, 5888:6912]
        EV2 = ABF[:, 6912:7936]
        TSTG = ABF[:, 7936:8960]
        GSTGF = AF_[:, 5120:6144]
        HQT = ABF[:, 8960:11776].rearrange("p (c n) -> p c n", n=128)
        XQT0 = ABF[:, 11776:12800].rearrange("p (c n) -> p c n", n=128)
        XQT1 = ABF[:, 12800:13824].rearrange("p (c n) -> p c n", n=128)
        XHT = ABF[:, 5888:6912].rearrange("p (c n) -> p c n", n=128)
        XLT = ABF[:, 6912:7936].rearrange("p (c n) -> p c n", n=128)
        GTF = AF_[:, 2048:3072]
        GW = AF_[:, 1024:1088]
        GU = AF_[:, 1088:1152]
        GA = AF_[:, 1152:1216]
        GB = AF_[:, 1024:1088]
        GF0 = AF_[:, 1088:1152]
        GF = AF_[:, 1152:1216]
        BITSF = AF_[:, 1216:1280]
        GD = AF_[:, 1024:1088]
        GE = AF_[:, 1088:1152]
        CORRF = AF_[:, 1152:1216]
        synpm = AF_[:, 1280:1344]
        maskb = AF_[:, 1344:1408]
        WleB = ABF[0:24, 8960:9984]
        bitsT = ABF[0:24, 9984:10112]
        corrT = ABF[0:24, 10112:10240]
        NB05 = cstb[:, 0:1]
        NB5 = cstb[:, 1:2]
        HPIN = cstb[:, 2:3]
        HPIP = cstb[:, 3:4]

        pAW = pg0[:, 0:24]
        pAW64 = pg0[:, 0:64]
        latP = pg0[:, 0:24]
        synP = py0[:, 0:12]
        psA = pT[0:24, 0:128]
        psB = pT[0:24, 128:256]

        Q = {"sy": [], "v": [], "sa": [], "te": [], "g": []}
        Cn = {"ld": 0, "ld2": 0, "st": 0, "v": 0, "a": 0, "t": 0, "sy": 0, "g": 0}
        SEM = {"ld": sLd, "ld2": sL2, "st": sSt, "v": sV, "a": sA, "t": sT,
               "sy": sSy, "g": sG}

        def em(eng, fn):
            Q[eng].append(fn)

        def dma_in(dst, src):
            Cn["ld"] += 16
            tgt = Cn["ld"]
            em("sy", lambda e, d=dst, s=src: e.dma_start(out=d, in_=s).then_inc(sLd, 16))
            return tgt

        def dma_in2(dst, src):
            Cn["ld2"] += 16
            tgt = Cn["ld2"]
            em("g", lambda e, d=dst, s=src: e.dma_start(out=d, in_=s).then_inc(sL2, 16))
            return tgt

        def wait(eng, sem_key, val):
            em(eng, lambda e, k=sem_key, vv=val: e.wait_ge(SEM[k], vv))

        def op(eng, fn, inc_key=None):
            if inc_key is None:
                em(eng, fn)
                return None
            Cn[inc_key] += 1
            tgt = Cn[inc_key]

            def _f(e, f=fn, k=inc_key):
                f(e).then_inc(SEM[k], 1)

            em(eng, _f)
            return tgt

        def dma_out(eng_key, dst, src):
            Cn["st"] += 16
            tgt = Cn["st"]
            em(eng_key, lambda e, d=dst, s=src: e.dma_start(out=d, in_=s).then_inc(sSt, 16))
            return tgt

        def barrier():
            dp = {
                "sy": lambda e: e.nop(),
                "v": lambda e: e.engine_nop(),
                "sa": lambda e: e.copy(cstb[:, 7:8], c3buf[:]),
                "te": lambda e: e.ldweights(ident[:]),
                "g": lambda e: e.engine_nop(),
            }
            marks = {}
            for ek, sk in (("sy", "sy"), ("v", "v"), ("sa", "a"), ("te", "t"), ("g", "g")):
                Cn[sk] += 1
                marks[ek] = Cn[sk]
                em(ek, lambda e, k=sk, f=dp[ek]: f(e).then_inc(SEM[k], 1))
            for ek in ("sy", "v", "sa", "te", "g"):
                for ok, sk in (("sy", "sy"), ("v", "v"), ("sa", "a"), ("te", "t"), ("g", "g")):
                    if ok != ek:
                        wait(ek, sk, marks[ok])

        def emit_quant(src, dst, W):
            S = W // 16

            def _q(v, s=src, d=dst, W=W, S=S):
                v.tensor_reduce(qm[:, 0:S], _v3(s), AX, A.max, apply_absolute_value=True)
                v.tensor_copy(sc1[:, 3:4], c3buf[:])
                v.tensor_scalar(qsb[:, 0:64], qm[:, 0:64], SIXTH, 1e-12, A.mult, A.max)
                v.reciprocal(qrb[:, 0:64], qsb[:, 0:64])
                v.tensor_scalar(qs2[:, 0:64], qsb[:, 0:64], 0.5, None, A.mult)
                v.tensor_tensor(_v3(QW[:, :W]), _v3(s), _b16(qrb[:, 0:S], S), A.mult)
                v._custom_dve(FP4A, out=QGA[:, :W], in0=QW[:, :W], in1=c3buf[:],
                              s0=0.0625, s1=0.5625, imm2=1.5625)
                v._custom_dve(FP4B, out=QGB[:, :W], in0=QW[:, :W],
                              s0=6.25, s1=12.25, imm2=25.0)
                v.tensor_tensor(QGA[:, :W], QGA[:, :W], QGB[:, :W], A.add)
                v.tensor_tensor(QGA[:, :W], QGA[:, :W], QSG[:, :W], A.mult)
                return v.tensor_tensor(_v3(d), _v3(QGA[:, :W]), _b16(qs2[:, 0:S], S), A.mult)

            return _q

        for dst, key in ((ident[:], "cident"), (HTs[:], "cHT"), (acf[:], "cacoef"),
                         (kas[:], "cka"), (ka2s[:], "cka2"), (cws[:], "ccw")):
            dma_in(dst, ci[key])
        dma_in(WelS[:], wel.rearrange("(c p) e -> p c e", p=128))
        t_bel = dma_in(negb[:, 0:24], bel.partition_broadcast(128))
        t_act1 = dma_in(AcT_hi_v, ci["cAcThi"])
        t_act2 = dma_in(AcT_lo_v, ci["cAcTlo"])
        consts_ld = Cn["ld"]

        wait("v", "ld", consts_ld)
        op("v", lambda v: v.memset(c3buf[:], 3.0625))
        wait("g", "ld", t_bel)

        def _gsetup(g):
            g.memset(negb[:, 24:64], 0.0)
            g.tensor_scalar(negb[:, 0:24], negb[:, 0:24], -1.0, None, A.mult)
            g.memset(cstb[:, 0:1], -0.5)
            g.memset(cstb[:, 1:2], -5.0)
            g.memset(cstb[:, 2:3], -HALFPI)
            return g.memset(cstb[:, 3:4], HALFPI)

        g_setup = op("g", _gsetup, "g")
        op("v", lambda v: v.tensor_copy(WelH[:], WelS[:]))
        v_wel = op("v", lambda v: v.scalar_tensor_tensor(
            WelL[:].rearrange("p c n -> p (c n)"),
            WelH[:].rearrange("p c n -> p (c n)"), -1.0,
            WelS[:].rearrange("p c n -> p (c n)"), A.mult, A.add), "v")

        wait("te", "v", v_wel)
        wait("te", "ld", t_act2)
        aw_last_t = None
        for nm, AW in (("c", AWc), ("s", AWs)):
            if nm == "s":
                wait("sy", "t", aw_last_t)
                t_ast1 = dma_in(AcT_hi_v, ci["cAsThi"])
                t_ast2 = dma_in(AcT_lo_v, ci["cAsTlo"])
                wait("te", "ld", t_ast2)
            for kc in range(5):
                ksl = slice(kc * 128, (kc + 1) * 128)
                mms = []
                for cc in range(8):
                    mms.append((AcT_hi_v[:, cc, ksl], WelH[:, cc, :]))
                    mms.append((AcT_hi_v[:, cc, ksl], WelL[:, cc, :]))
                    mms.append((AcT_lo_v[:, cc, ksl], WelH[:, cc, :]))
                t_mark = None
                for i, (lh, rh) in enumerate(mms):
                    def _mm(te, lh=lh, rh=rh, st=(i == 0), sp=(i == len(mms) - 1)):
                        return te.matmul(pAW, lh, rh, start=st, stop=sp)
                    if i < len(mms) - 1:
                        op("te", _mm)
                    else:
                        t_mark = op("te", _mm, "t")
                aw_last_t = t_mark
                wait("sa", "t", t_mark)
                a_mark = op("sa", lambda s, d=AW[:, kc, :]: s.copy(d, pAW64), "a")
                wait("te", "a", a_mark)

        op("v", lambda v: v.memset(OWNSUM[:, 0:32], 1.0))
        op("v", lambda v: v.memset(OWNSQ[:, 0:32], 1.0))
        v_stats_base = Cn["v"]
        a_stats_base = Cn["a"]
        NT_ALL = STILES + TPC
        for t in range(NT_ALL):
            buf = XT0 if t % 2 == 0 else XT1
            qk = "sy" if t % 2 == 0 else "g"
            lsem = "ld" if t % 2 == 0 else "ld2"
            if t >= 2:
                wait(qk, "v", v_stats_base + (t - 1))
                wait(qk, "a", a_stats_base + (t - 1))
            if t < STILES:
                srcap = xfull[t * 128:(t + 1) * 128, :]
            else:
                tt = t - STILES
                srcap = xb[tt * 128:(tt + 1) * 128, :]
            t_ld = dma_in(buf, srcap) if t % 2 == 0 else dma_in2(buf, srcap)
            wait("sa", lsem, t_ld)
            if t < STILES:
                op("sa", lambda s, b=buf, t=t: s.activation(
                    QW, b, AF.Square, accum_out=SMVQ[:, t:t + 1]), "a")
            else:
                op("sa", lambda s, b=buf, tt=t - STILES: s.activation(
                    QW, b, AF.Square, accum_out=OWNSQ[:, tt + 32:tt + 33]), "a")
            wait("v", lsem, t_ld)

            if t < STILES:
                def _st(v, b=buf, t=t):
                    v.tensor_reduce(SMVS[:, t:t + 1], b, AX, A.add)
                    v.tensor_reduce(SMN[:, t:t + 1], b, AX, A.min)
                    v.reduce_max(SMX[:, t:t + 1], b, AX)
                    return v.tensor_copy(sc1[:, 3:4], c3buf[:])
            else:
                def _st(v, b=buf, tt=t - STILES):
                    v.tensor_reduce(OWNSUM[:, tt + 32:tt + 33], b, AX, A.add)
                    return v.tensor_copy(sc1[:, 3:4], c3buf[:])

            op("v", _st, "v")

        def _wp1(v):
            v.tensor_scalar(NEGM, SMVS, -1.0 / 1024.0, None, A.mult)
            return v.tensor_copy(sc1[:, 3:4], c3buf[:])

        v_wp1 = op("v", _wp1, "v")
        wait("sa", "v", v_wp1)
        a_wp1 = op("sa", lambda s: s.activation(MSQ, NEGM, AF.Square), "a")
        wait("v", "a", a_wp1)

        def _wp2(v):
            v.scalar_tensor_tensor(VARE, SMVQ, 1.0 / 1024.0, MSQ,
                                   A.mult, A.subtract)
            return v.tensor_scalar(VARE, VARE, 1e-5, None, A.add)

        v_wp2 = op("v", _wp2, "v")
        wait("sa", "v", v_wp2)
        a_wp2 = op("sa", lambda s: s.activation(SQV, VARE, AF.Sqrt), "a")
        wait("v", "a", a_wp2)
        v_wp3 = op("v", lambda v: v.reciprocal(R0G, SQV), "v")
        wait("sa", "v", v_wp3)
        a_wp3 = op("sa", lambda s: s.activation(RRG, R0G, AF.Square), "a")
        wait("v", "a", a_wp3)

        def _wp4(v):
            v.tensor_tensor(BVG, RRG, VARE, A.mult)
            v.tensor_scalar(BVG, BVG, -0.5, 1.5, A.mult, A.add)
            v.tensor_tensor(RSTDG, R0G, BVG, A.mult)
            v.tensor_tensor(SMN, SMN, NEGM, A.add)
            v.tensor_tensor(SMN, SMN, RSTDG, A.mult)
            v.tensor_tensor(SMX, SMX, NEGM, A.add)
            v.tensor_tensor(SMX, SMX, RSTDG, A.mult)
            v.tensor_reduce(sc1[:, 0:1], SMN, AX, A.min)
            v.reduce_max(sc1[:, 1:2], SMX, AX)
            return v.tensor_copy(sc1[:, 3:4], c3buf[:])

        v_wp = op("v", _wp4, "v")

        wait("g", "v", v_wp)
        wait("g", "g", g_setup)

        def _dg1(g):
            g.tensor_copy(dsm[0:1, 3:4], NEGM[0:1, 0:1])
            g.tensor_copy(dsm[0:1, 4:5], RSTDG[0:1, 0:1])
            return g.tensor_tensor(dsm[0:1, 7:8], dsm[0:1, 3:4], dsm[0:1, 4:5], A.mult)

        g_d1 = op("g", _dg1, "g")
        dma_out("g", vec_d[0, :], sc1[:, 0:1])
        st_b = dma_out("g", vec_d[1, :], sc1[:, 1:2])
        wait("sy", "st", st_b)
        dma_in(MNR[:, 0:128], vec_d[0:1, :])
        dma_in(MXR[:, 0:128], vec_d[1:2, :])
        t_x0 = dma_in(XROW, xfull[0:1, :])
        wait("v", "ld", t_x0)

        def _dv1(v):
            v.tensor_reduce(dsm[0:1, 0:1], MNR[:, 0:128], AX, A.min)
            v.reduce_max(dsm[0:1, 1:2], MXR[:, 0:128], AX)
            return v.tensor_copy(sc1[:, 3:4], c3buf[:])

        v_d1 = op("v", _dv1, "v")
        wait("g", "v", v_d1)

        def _dg2(g):
            g.tensor_tensor(dsm[0:1, 5:6], dsm[0:1, 1:2], dsm[0:1, 0:1], A.subtract)
            return g.tensor_scalar(dsm[0:1, 5:6], dsm[0:1, 5:6], 1e-10, None, A.add)

        g_d2 = op("g", _dg2, "g")
        wait("v", "g", g_d2)
        v_d2 = op("v", lambda v: v.reciprocal(dsm[0:1, 2:3], dsm[0:1, 5:6]), "v")
        wait("g", "v", v_d2)

        def _dg3(g):
            g.tensor_tensor(dsm[0:1, 6:7], dsm[0:1, 0:1], dsm[0:1, 2:3], A.mult)
            return g.tensor_scalar(dsm[0:1, 6:7], dsm[0:1, 6:7], -1.0, None, A.mult)

        g_d3 = op("g", _dg3, "g")
        wait("sa", "g", g_d3)

        def _da1(s):
            s.activation(XNROW, XROW, AF.Identity,
                         bias=dsm[0:1, 7:8], scale=dsm[0:1, 4:5])
            return s.activation(ZROW, XNROW, AF.Identity,
                                bias=dsm[0:1, 6:7], scale=dsm[0:1, 2:3])

        a_d1 = op("sa", _da1, "a")
        wait("v", "a", a_d1)
        NNJ = (4, 16, 64, 256, 1024)

        def _dv3(v):
            for j, nn in enumerate(NNJ):
                v.tensor_reduce(dly[0:1, j:j + 1], ZROW[:, 0:nn], AX, A.add)
            return v.tensor_copy(sc1[:, 3:4], c3buf[:])

        v_d3 = op("v", _dv3, "v")
        wait("g", "v", v_d3)

        def _dg4(g):
            last = None
            for j, nn in enumerate(NNJ):
                last = g.tensor_scalar(dly2[0:1, j:j + 1], dly[0:1, j:j + 1],
                                       1.0 / nn, None, A.mult)
            return last

        g_d4 = op("g", _dg4, "g")
        wait("v", "g", g_d4)

        def _dv4(v):
            for j, nn in enumerate(NNJ):
                v.tensor_scalar(MROW, ZROW, dly2[0:1, j:j + 1], None, A.is_gt)
                v.tensor_reduce(dcnt[0:1, j:j + 1], MROW[:, 0:nn], AX, A.add)
            return v.tensor_copy(sc1[:, 3:4], c3buf[:])

        v_d4 = op("v", _dv4, "v")
        wait("g", "v", v_d4)
        g_d5 = op("g", lambda g: g.tensor_scalar(dcnt[0:1, 0:5], dcnt[0:1, 0:5],
                                                 1.0, None, A.max), "g")
        wait("sa", "g", g_d5)
        a_d2 = op("sa", lambda s: s.activation(dll[0:1, 0:5], dcnt[0:1, 0:5], AF.Ln), "a")
        wait("g", "a", a_d2)
        g_d6 = op("g", lambda g: g.tensor_tensor(dslp[0:1, 0:5], dll[0:1, 0:5],
                                                 acf[0:1, :], A.mult), "g")
        wait("v", "g", g_d6)

        def _dv5(v):
            v.tensor_reduce(dsm[0:1, 8:9], dslp[0:1, 0:5], AX, A.add)
            return v.tensor_copy(sc1[:, 3:4], c3buf[:])

        v_d5 = op("v", _dv5, "v")
        wait("g", "v", v_d5)
        g_d7 = op("g", lambda g: g.tensor_scalar(dsm[0:1, 9:10], dsm[0:1, 8:9],
                                                 1.0, 2.0, A.max, A.min), "g")
        wait("g", "g", g_d7)
        st_d = dma_out("g", vec_d[2, 0:1], dsm[0:1, 9:10])
        wait("sy", "st", st_d)
        t_db = dma_in(Db[:], vec_d[2:3, 0:1].partition_broadcast(128))

        wait("sa", "ld", t_db)
        a_f0 = op("sa", lambda s: s.activation(alf0[:], Db[:], AF.Square), "a")
        wait("g", "a", a_f0)

        def _fg1(g):
            g.tensor_scalar(gtm[:], alf0[:], 0.2, None, A.mult)
            g.tensor_tensor(alf[:], gtm[:], Db[:], A.add)
            g.tensor_scalar(bet[:], Db[:], 0.5, 1.0, A.mult, A.add)
            return g.tensor_scalar(gam[:], Db[:], 0.1, None, A.mult)

        g_f1 = op("g", _fg1, "g")
        wait("sa", "g", g_f1)

        def _fa1(s):
            s.activation(tmp5[:], kas[:], AF.Identity, bias=alf[:], scale=bet[:])
            s.activation(ub5[:], tmp5[:], AF.Abs, bias=HPIN)
            s.activation(fr1[:], ub5[:], AF.Sin, bias=HPIP, scale=-1.0)
            return s.activation(t5c[:], ka2s[:], AF.Identity, bias=alf[:], scale=gam[:])

        a_f1 = op("sa", _fa1, "a")
        wait("g", "a", a_f1)
        g_f2 = op("g", lambda g: g.tensor_tensor(t5c[:], t5c[:], kas[:], A.subtract), "g")
        wait("sa", "g", g_f2)
        a_f2 = op("sa", lambda s: s.activation(fr2[:], t5c[:], AF.Sin,
                                               bias=HPIP, scale=-1.0), "a")
        wait("g", "a", a_f2)

        def _fg3(g):
            g.tensor_tensor(cfr[:], fr1[:], fr2[:], A.mult)
            return g.tensor_tensor(cfr[:], cfr[:], cws[:], A.mult)

        g_cfr = op("g", _fg3, "g")

        wait("sy", "t", aw_last_t)
        t_m2 = dma_in(S1, ci["cAchi"])
        t_m2b = dma_in(S2, ci["cAclo"])
        m_last_t = None
        m_last_v = None
        for nm in ("c", "s"):
            AW = AWc if nm == "c" else AWs
            if nm == "s":
                wait("sy", "t", m_last_t)
                t_m2 = dma_in(S1, ci["cAshi"])
                t_m2b = dma_in(S2, ci["cAslo"])
            cfrb32 = cfr[:].rearrange("p (c o) -> p c o", o=1).broadcast_to([128, 5, 32])

            def _rbuild(v, AW=AW, cfrb32=cfrb32):
                return v.tensor_tensor(Rf, AW[:, :, 0:32], cfrb32, A.mult)

            if nm == "c":
                wait("v", "g", g_cfr)
            v_r = op("v", _rbuild, "v")
            op("v", lambda v: v.tensor_copy(Rh[:], Rf))
            v_rl = op("v", lambda v: v.scalar_tensor_tensor(
                Rl[:].rearrange("p c n -> p (c n)"),
                Rh[:].rearrange("p c n -> p (c n)"), -1.0,
                Rf.rearrange("p c n -> p (c n)"), A.mult, A.add), "v")
            wait("te", "v", v_rl)
            wait("te", "ld", t_m2b)
            for jc in range(8):
                jsl = slice(jc * 128, (jc + 1) * 128)
                mms = []
                for kc in range(5):
                    mms.append((S1[:, kc, jsl], Rh[:, kc, 0:24]))
                    mms.append((S1[:, kc, jsl], Rl[:, kc, 0:24]))
                    mms.append((S2[:, kc, jsl], Rh[:, kc, 0:24]))
                t_mk = None
                for i, (lh, rh) in enumerate(mms):
                    def _mm(te, lh=lh, rh=rh, st=(i == 0), sp=(i == len(mms) - 1)):
                        return te.matmul(pAW, lh, rh, start=st, stop=sp)
                    if i < len(mms) - 1:
                        op("te", _mm)
                    else:
                        t_mk = op("te", _mm, "t")
                m_last_t = t_mk
                wait("v", "t", t_mk)
                if nm == "c":
                    m_last_v = op("v", lambda v, jc=jc: v.tensor_copy(Mf[:, jc, :], pAW64), "v")
                else:
                    m_last_v = op("v", lambda v, jc=jc: v.tensor_tensor(
                        Mf[:, jc, :], Mf[:, jc, :], pAW64, A.add), "v")
                wait("te", "v", m_last_v)

        def _msplit(v):
            v.tensor_copy(MfC, Mf[:, :, 0:24])
            v.tensor_copy(Mhi[:], MfC)
            return v.scalar_tensor_tensor(
                Mlo[:].rearrange("p c n -> p (c n)"),
                Mhi[:].rearrange("p c n -> p (c n)"), -1.0,
                MfC.rearrange("p c n -> p (c n)"), A.mult, A.add)

        v_msp = op("v", _msplit, "v")

        c_last_v = v_msp
        for nm in ("c", "s"):
            wait("sy", "t", m_last_t)
            wait("sy", "v", c_last_v)
            t_c1 = dma_in(S1, ci["cAchi"] if nm == "c" else ci["cAshi"])
            t_c2 = dma_in(S2, ci["cAchi"] if nm == "c" else ci["cAshi"])
            wait("v", "ld", t_c2)
            cfrb1k = cfr[:].rearrange("p (c o) -> p c o", o=1).broadcast_to([128, 5, 1024])

            def _pbuild(v, cfrb1k=cfrb1k):
                return v.tensor_tensor(S2, S2, cfrb1k, A.mult)

            v_p = op("v", _pbuild, "v")
            wait("te", "v", v_p)
            wait("te", "ld", t_c1)
            for jc in range(8):
                jsl = slice(jc * 128, (jc + 1) * 128)
                t_ck = None
                for half in range(2):
                    hsl = slice(half * 512, (half + 1) * 512)
                    for kc in range(5):
                        def _mm(te, kc=kc, jsl=jsl, hsl=hsl, st=(kc == 0), sp=(kc == 4)):
                            return te.matmul(pdn[:, hsl], S1[:, kc, jsl],
                                             S2[:, kc, hsl], start=st, stop=sp)
                        if kc < 4 or half == 0:
                            op("te", _mm)
                        else:
                            t_ck = op("te", _mm, "t")
                wait("v", "t", t_ck)
                if nm == "c":
                    c_last_v = op("v", lambda v, jc=jc: v.tensor_copy(Chi[:, jc, :], pdn[:]), "v")
                else:
                    c_last_v = op("v", lambda v, jc=jc: v.tensor_add(Chi[:, jc, :], Chi[:, jc, :], pdn[:]), "v")
                wait("te", "v", c_last_v)

        barrier()

        prev_v = None
        prev_tr = None

        def wq_tile(rows_src, m, dst_fn, subw):
            nonlocal prev_v, prev_tr
            marks = []
            for s_i, w in enumerate(subw):
                o = sum(subw[:s_i])
                if prev_v is not None:
                    wait("sy", "v", prev_v)
                t_ld = dma_in(XT0[:, 0:w], rows_src[:, o:o + w])
                wait("sa", "ld", t_ld)
                if prev_v is not None:
                    wait("sa", "v", prev_v)
                a_sg = op("sa", lambda s, w=w: s.activation(QSG[:, 0:w], XT0[:, 0:w], AF.Sign), "a")
                wait("v", "ld", t_ld)
                wait("v", "a", a_sg)
                if prev_tr is not None:
                    wait("v", "t", prev_tr)
                v_mk = op("v", emit_quant(XT0[:, 0:w], QOUT[:, o:o + w], w), "v")
                marks.append(v_mk)
                prev_v = v_mk
            nblk = sum(subw) // 128
            wait("te", "v", marks[-1])
            for g0 in range(0, nblk, 8):
                gn = min(8, nblk - g0)
                t_tr = None
                for b in range(gn):
                    def _tr(te, b=b, g0=g0):
                        return te.transpose(pT[:, b * 128:(b + 1) * 128],
                                            QOUT[:, (g0 + b) * 128:(g0 + b + 1) * 128],
                                            ident[:])
                    if b < gn - 1:
                        op("te", _tr)
                    else:
                        t_tr = op("te", _tr, "t")
                prev_tr = t_tr
                wait("sa", "t", t_tr)
                a_ev = op("sa", lambda s, d=dst_fn(m, g0, gn), gn=gn:
                          s.copy(d, pT[:, 0:gn * 128].rearrange("p (c n) -> p c n", n=128)), "a")
                wait("te", "a", a_ev)

        for m in range(22):
            wq_tile(wg[m * 128:(m + 1) * 128, :], m,
                    lambda m, g0, gn: WGb[:, g0:g0 + gn, m * 128:(m + 1) * 128], [1024])
        for m in range(22):
            wq_tile(wu[m * 128:(m + 1) * 128, :], m,
                    lambda m, g0, gn: WUb[:, g0:g0 + gn, m * 128:(m + 1) * 128], [1024])
        for m in range(8):
            wq_tile(wd[m * 128:(m + 1) * 128, :], m,
                    lambda m, g0, gn: WDb[:, g0:g0 + gn, m * 128:(m + 1) * 128],
                    [1024, 1024, 768])

        barrier()

        def _owp1(v):
            v.tensor_scalar(NEGM64, OWNSUM, -1.0 / 1024.0, None, A.mult)
            return v.tensor_copy(sc1[:, 3:4], c3buf[:])

        v_o1 = op("v", _owp1, "v")
        wait("sa", "v", v_o1)
        a_o1 = op("sa", lambda s: s.activation(MSQ64, NEGM64, AF.Square), "a")
        wait("v", "a", a_o1)

        def _owp2(v):
            v.scalar_tensor_tensor(VARE64, OWNSQ, 1.0 / 1024.0, MSQ64,
                                   A.mult, A.subtract)
            return v.tensor_scalar(VARE64, VARE64, 1e-5, None, A.add)

        v_o2 = op("v", _owp2, "v")
        wait("sa", "v", v_o2)
        a_o2 = op("sa", lambda s: s.activation(SQ64, VARE64, AF.Sqrt), "a")
        wait("v", "a", a_o2)
        v_o3 = op("v", lambda v: v.reciprocal(R064, SQ64), "v")
        wait("sa", "v", v_o3)
        a_o3 = op("sa", lambda s: s.activation(RR64, R064, AF.Square), "a")
        wait("v", "a", a_o3)

        def _owp4(v):
            v.tensor_tensor(B64, RR64, VARE64, A.mult)
            v.tensor_scalar(B64, B64, -0.5, 1.5, A.mult, A.add)
            return v.tensor_tensor(RSTD64, R064, B64, A.mult)

        v_own = op("v", _owp4, "v")

        ln_marks = []
        tr_last = []
        for t in range(TPC):
            buf = XT0 if t % 2 == 0 else XT1
            if t >= 2:
                wait("sy", "v", ln_marks[t - 2])
            t_ld = dma_in(buf, xb[t * 128:(t + 1) * 128, :])
            wait("v", "ld", t_ld)
            v_ln = op("v", lambda v, b=buf, t=t: v.tensor_scalar(
                XNF, b, NEGM64[:, 32 + t:33 + t], RSTD64[:, 32 + t:33 + t],
                A.add, A.mult), "v")
            ln_marks.append(v_ln)
            wait("sa", "v", v_ln)
            if t >= 1:
                wait("sa", "t", tr_last[t - 1])

            def _ahi(s):
                s.activation(QSG[:, 0:1024], XNF, AF.Sign)
                return s.activation(EV1, XNF, AF.Copy)

            a_hi = op("sa", _ahi, "a")
            wait("v", "a", a_hi)
            if t >= 1:
                wait("v", "t", tr_last[t - 1])
            op("v", emit_quant(XNF, QOUT[:, 0:1024], 1024))
            v_a = op("v", lambda v: v.scalar_tensor_tensor(EV2, EV1, -1.0, XNF, A.mult, A.add), "v")

            wait("te", "v", v_a)
            for src, dstd in ((QOUT[:, 0:1024], xqT_d), (EV1, xhT_d), (EV2, xlT_d)):
                t_tr = None
                for b in range(8):
                    def _tr(te, src=src, b=b):
                        return te.transpose(pT[:, b * 128:(b + 1) * 128],
                                            src[:, b * 128:(b + 1) * 128], ident[:])
                    if b < 7:
                        op("te", _tr)
                    else:
                        t_tr = op("te", _tr, "t")
                wait("sa", "t", t_tr)
                a_ev = op("sa", lambda s: s.copy(TSTG, pT[:]), "a")
                wait("te", "a", a_ev)
                wait("g", "a", a_ev)
                st_mk = dma_out("g", dstd[t], TSTG.rearrange("p (c n) -> p c n", n=128))
                wait("sa", "st", st_mk)
            tr_last.append(t_tr)

        barrier()

        chunks = [(0, 512), (512, 512), (1024, 512), (1536, 512), (2048, 512), (2560, 256)]
        sil_marks, h_marks, hq_marks, tr_marks, gy_marks = [], [], [], [], []
        tile_last_gy = []
        prev_g_st = None
        pending_tr = None

        def emit_tr(gk, io, iw):
            wait("te", "v", hq_marks[gk])
            nb = iw // 128
            tr_mk = None
            for b in range(nb):
                def _tr(te, b=b, io=io):
                    return te.transpose(pT[:, b * 128:(b + 1) * 128],
                                        QOUT[:, io + b * 128: io + (b + 1) * 128],
                                        ident[:])
                if b < nb - 1:
                    op("te", _tr)
                else:
                    tr_mk = op("te", _tr, "t")
            tr_marks.append(tr_mk)
            wait("sa", "t", tr_mk)
            a_ev = op("sa", lambda s, io=io, nb=nb:
                      s.copy(HQT[:, io // 128: io // 128 + nb, :],
                             pT[:, 0:nb * 128].rearrange("p (c n) -> p c n", n=128)), "a")
            wait("te", "a", a_ev)

        for t in range(TPC):
            xq = XQT0 if t % 2 == 0 else XQT1
            if t >= 2:
                wait("sy", "t", tile_last_gy[t - 2])
            t_ld = dma_in(xq, xqT_d[t])
            wait("te", "ld", t_ld)
            for ci_, (io, iw) in enumerate(chunks):
                gk = t * 6 + ci_
                pg = pg0 if gk % 2 == 0 else pg1
                py = py0 if gk % 2 == 0 else py1
                if gk >= 2:
                    wait("te", "a", sil_marks[gk - 2])
                    wait("te", "v", h_marks[gk - 2])
                for cc in range(8):
                    def _mmg(te, cc=cc, io=io, iw=iw, pg=pg, xq=xq,
                             st=(cc == 0), sp=(cc == 7)):
                        return te.matmul(pg[:, 0:iw], xq[:, cc, :],
                                         WGb[:, cc, io:io + iw], start=st, stop=sp)
                    op("te", _mmg)
                gy_mk = None
                for cc in range(8):
                    def _mmy(te, cc=cc, io=io, iw=iw, py=py, xq=xq,
                             st=(cc == 0), sp=(cc == 7)):
                        return te.matmul(py[:, 0:iw], xq[:, cc, :],
                                         WUb[:, cc, io:io + iw], start=st, stop=sp)
                    if cc < 7:
                        op("te", _mmy)
                    else:
                        gy_mk = op("te", _mmy, "t")
                gy_marks.append(gy_mk)
                if pending_tr is not None:
                    emit_tr(*pending_tr)
                    pending_tr = None
                wait("sa", "t", gy_mk)
                if gk >= 1:
                    wait("sa", "v", h_marks[gk - 1])
                sil_marks.append(op("sa", lambda s, pg=pg, iw=iw:
                                    s.activation(SIL[:, 0:iw], pg[:, 0:iw], AF.Silu), "a"))
                wait("v", "a", sil_marks[gk])
                h_marks.append(op("v", lambda v, py=py, iw=iw:
                                  v.tensor_mul(HB[:, 0:iw], SIL[:, 0:iw], py[:, 0:iw]), "v"))
                wait("sa", "v", h_marks[gk])
                if gk >= 1:
                    wait("sa", "v", hq_marks[gk - 1])
                a_sgn = op("sa", lambda s, iw=iw:
                           s.activation(QSG[:, 0:iw], HB[:, 0:iw], AF.Sign), "a")
                wait("v", "a", a_sgn)
                if gk >= 6:
                    wait("v", "t", tr_marks[gk - 6])
                hq_marks.append(op("v", emit_quant(HB[:, 0:iw], QOUT[:, io:io + iw], iw), "v"))
                pending_tr = (gk, io, iw)
            emit_tr(*pending_tr)
            pending_tr = None
            tile_last_gy.append(gy_marks[-1])
            t_dn = None
            for ic in range(22):
                for half in range(2):
                    def _mmd(te, ic=ic, half=half, st=(ic == 0), sp=(ic == 21)):
                        return te.matmul(pdn[:, half * 512:(half + 1) * 512],
                                         HQT[:, ic, :], WDb[:, ic, half * 512:(half + 1) * 512],
                                         start=st, stop=sp)
                    if ic == 21 and half == 1:
                        t_dn = op("te", _mmd, "t")
                    else:
                        op("te", _mmd)
            wait("sa", "t", t_dn)
            if prev_g_st is not None:
                wait("sa", "st", prev_g_st)
            a_g = op("sa", lambda s: s.activation(GSTGF, pdn[:], AF.Silu), "a")
            wait("g", "a", a_g)
            prev_g_st = dma_out("g", g_d[t], GSTGF)
            wait("te", "a", a_g)

        barrier()

        wait("sy", "st", prev_g_st)
        t_ble = dma_in(BLEB, ble.partition_broadcast(128))
        wait("v", "ld", t_ble)
        Cn["st"] += 16
        _wle_st = Cn["st"]
        em("g", lambda e: e.dma_start(out=WleB[:], in_=wle).then_inc(sSt, 16))
        wait("te", "st", _wle_st)

        def _gmask(g):
            g.memset(maskb[:], 0.0)
            return g.memset(maskb[:, 0:12], 1.0)

        g_mask = op("g", _gmask, "g")
        wait("v", "g", g_mask)
        prev_cb = None
        prev_sty = None
        for t in range(TPC):
            if prev_cb is not None:
                wait("sy", "v", prev_cb)
                wait("sy", "st", prev_sty)
            dma_in(XHT, xhT_d[t])
            t2 = dma_in(XLT, xlT_d[t])
            t3 = dma_in(XT0, xb[t * 128:(t + 1) * 128, :])
            t4 = dma_in(GTF, g_d[t])
            wait("te", "ld", t2)
            for half in range(2):
                hsl = slice(half * 512, (half + 1) * 512)
                for cc in range(8):
                    def _mmx(te, cc=cc, hsl=hsl, st=(cc == 0)):
                        return te.matmul(pdn[:, hsl], XHT[:, cc, :],
                                         Chi[:, cc, hsl], start=st, stop=False)
                    op("te", _mmx)
            lat_mms = ([(XHT[:, cc, :], Mhi[:, cc, :]) for cc in range(8)]
                       + [(XHT[:, cc, :], Mlo[:, cc, :]) for cc in range(8)]
                       + [(XLT[:, cc, :], Mhi[:, cc, :]) for cc in range(8)])
            t_lat = None
            for i, (lh, rh) in enumerate(lat_mms):
                def _mml(te, lh=lh, rh=rh, st=(i == 0), sp=(i == 23)):
                    return te.matmul(latP, lh, rh, start=st, stop=sp)
                if i < 23:
                    op("te", _mml)
                else:
                    t_lat = op("te", _mml, "t")
            wait("v", "t", t_lat)
            v_bits = op("v", lambda v: v.tensor_tensor(
                BITSF, pg0[:, 0:64], negb[:], A.is_gt), "v")
            wait("sa", "v", v_bits)
            a_bc = op("sa", lambda s: s.copy(bits[:], BITSF[:, 0:24]), "a")
            wait("te", "a", a_bc)
            t_bt = op("te", lambda te: te.transpose(psA, bits[:], ident[:]), "t")
            wait("sa", "t", t_bt)
            a_bt = op("sa", lambda s: s.copy(bitsT[:], psA), "a")
            wait("te", "a", a_bt)
            t_syn = op("te", lambda te: te.matmul(synP, bitsT[:], HTs[:], start=True, stop=True), "t")
            wait("v", "t", t_syn)

            def _golay(v):
                v.tensor_scalar(synpm, py0[:, 0:64], 0.5, None, A.is_gt)
                v.tensor_reduce(sc1[:, 2:3], synpm[:, 0:12], AX, A.add)
                v.tensor_copy(sc1[:, 3:4], c3buf[:])
                v.tensor_copy(GW, sc1[:, 2:3].broadcast_to([128, 64]))
                v.tensor_scalar(GU, GW, 3.5, None, A.is_lt)
                v.tensor_tensor(GA, synpm, GU, A.mult)
                v.tensor_tensor(GF, GA, maskb, A.mult)
                v.tensor_tensor(GD, BITSF, GF, A.mult)
                v.tensor_tensor(GE, BITSF, GF, A.add)
                return v.scalar_tensor_tensor(CORRF, GD, -2.0, GE, A.mult, A.add)

            v_go = op("v", _golay, "v")
            wait("sa", "v", v_go)

            def _acorr(s):
                s.copy(corr[:, 0:12], CORRF[:, 0:12])
                return s.copy(corr[:, 12:24], BITSF[:, 12:24])

            a_co = op("sa", _acorr, "a")
            wait("te", "a", a_co)
            t_ct = op("te", lambda te: te.transpose(psB, corr[:], ident[:]), "t")
            wait("sa", "t", t_ct)
            a_ct = op("sa", lambda s: s.copy(corrT[:], psB), "a")
            wait("te", "a", a_ct)
            t_wle2 = None
            for half in range(2):
                def _mmw(te, half=half):
                    return te.matmul(pdn[:, half * 512:(half + 1) * 512], corrT[:],
                                     WleB[:, half * 512:(half + 1) * 512],
                                     start=False, stop=True)
                if half == 0:
                    op("te", _mmw)
                else:
                    t_wle2 = op("te", _mmw, "t")
            wait("v", "t", t_wle2)
            wait("v", "ld", max(t3, t4))

            def _comb(v):
                v.tensor_add(SSB, pdn[:], BLEB)
                v.tensor_mul(SSB, SSB, GTF)
                return v.tensor_add(OSB, SSB, XT0)

            prev_cb = op("v", _comb, "v")
            wait("te", "v", prev_cb)
            wait("g", "v", prev_cb)
            prev_sty = dma_out("g", yout[t * 128:(t + 1) * 128, :], OSB)

        wait("g", "st", Cn["st"])

        @block.sync
        def _(e):
            for fn in Q["sy"]:
                fn(e)

        @block.vector
        def _(e):
            for fn in Q["v"]:
                fn(e)

        @block.scalar
        def _(e):
            for fn in Q["sa"]:
                fn(e)

        @block.tensor
        def _(e):
            for fn in Q["te"]:
                fn(e)

        @block.gpsimd
        def _(e):
            for fn in Q["g"]:
                fn(e)

    mybir.codegen_inst_isa_subclasses(nc)
    return nc


def kernel(L0_attn_x, L0_attn_W_el, L0_attn_b_el, L0_attn_W_le, L0_attn_b_le,
           L0_ffn_w_gate, L0_ffn_w_up, L0_ffn_w_down):
    x = np.asarray(L0_attn_x)
    xf = np.ascontiguousarray(x.reshape(ROWS, E))
    if "nc" not in _NC_CACHE:
        _NC_CACHE["nc"] = build_program()
    nc = _NC_CACHE["nc"]
    base = {
        "xfull": xf,
        "wg": np.ascontiguousarray(np.asarray(L0_ffn_w_gate)),
        "wu": np.ascontiguousarray(np.asarray(L0_ffn_w_up)),
        "wd": np.ascontiguousarray(np.asarray(L0_ffn_w_down)),
        "wel": np.ascontiguousarray(np.asarray(L0_attn_W_el)),
        "bel": np.ascontiguousarray(np.asarray(L0_attn_b_el).reshape(1, 24)),
        "wle": np.ascontiguousarray(np.asarray(L0_attn_W_le)),
        "ble": np.ascontiguousarray(np.asarray(L0_attn_b_le).reshape(1, E)),
    }
    for name, arr in _CONSTS.items():
        base[name] = arr
    in_maps = []
    for c in range(N_CORES):
        m = dict(base)
        m["xb"] = np.ascontiguousarray(x[c])
        in_maps.append(m)
    res = run_bass_kernel_spmd(nc, in_maps, core_ids=list(range(N_CORES)))
    out = np.stack([res.results[c]["yout"] for c in range(N_CORES)], axis=0)
    return out.astype(np.float32, copy=False)


# version 11
# speedup vs baseline: 1.2883x; 1.0491x over previous
import math

import numpy as np
import ml_dtypes

import concourse.bass as bass
import concourse.mybir as mybir
from concourse.bass_utils import run_bass_kernel_spmd
from concourse.dve_op import DveOp
from concourse.dve_spec import Spec, Src0, C0, C1, C2, C3, lower as dve_lower, sq
from concourse.dve_table_gen import dve_ver_for
import concourse.dve_ops as dve_ops_mod

F32 = mybir.dt.float32
BF16 = mybir.dt.bfloat16
A = mybir.AluOpType
AF = mybir.ActivationFunctionType
AX = mybir.AxisListType.X
BF = ml_dtypes.bfloat16

B_, SEQ, E = 8, 4096, 1024
ROWS = B_ * SEQ
TPC = SEQ // 128
STILES = ROWS // 128
I_FFN = 2816
HALFPI = float(np.float32(math.pi / 2.0))
SIXTH = float(np.float32(1.0 / 6.0))
N_CORES = 8


def _register_op(name, spec, reference):
    for op_ in dve_ops_mod.OPS:
        if op_.name == name:
            return op_
    ver = dve_ver_for("TRN2")
    sha = dve_lower(spec, ver=ver, perf_en=False, name=name).sha(ver)
    op_ = DveOp(name, spec, subdim=False, uops_sha={ver: sha}, reference=reference)
    dve_ops_mod.OPS.append(op_)
    row = max(dve_ops_mod._SUB_OPCODE_FOR_NAME.values()) + 1
    assert row < 0x20, "custom-DVE opcode rows exhausted"
    dve_ops_mod._SUB_OPCODE_FOR_NAME[name] = row
    dve_ops_mod.CUSTOM_DVE_REFS[name] = reference
    return op_


def _fp4a_ref(in0, in1, s0, s1, imm2):
    y = in0 * in0
    return (y > s0) * 1.0 + (y > s1) + (y > imm2) + (y > in1)


def _fp4b_ref(in0, in1, s0, s1, imm2):
    y = in0 * in0
    return 2.0 * ((y > s0) * 1.0 + (y > s1) + 2.0 * (y > imm2))


_y = sq(Src0)
FP4A = _register_op(
    "ANT_FP4A", Spec(body=(_y > C0) + (_y > C1) + ((_y > C2) + (_y > C3))), _fp4a_ref
)
_c7 = _y > C2
_inner = ((_y > C0) + (_y > C1)) + (_c7 + _c7)
FP4B = _register_op("ANT_FP4B", Spec(body=_inner + _inner), _fp4b_ref)


def _build_consts():
    k = np.arange(513, dtype=np.float64)
    n = np.arange(1024, dtype=np.float64)
    ang = 2.0 * np.pi * np.outer(k, n) / 1024.0
    Acp = np.zeros((640, 1024), dtype=np.float64)
    Asp = np.zeros((640, 1024), dtype=np.float64)
    Acp[:513] = np.cos(ang)
    Asp[:513] = np.sin(ang)
    ac_ch = np.ascontiguousarray(Acp.reshape(5, 128, 1024).transpose(1, 0, 2)).astype(np.float32)
    as_ch = np.ascontiguousarray(Asp.reshape(5, 128, 1024).transpose(1, 0, 2)).astype(np.float32)
    Ac_hi = ac_ch.astype(BF)
    As_hi = as_ch.astype(BF)
    AcT = np.ascontiguousarray(Acp.T.reshape(8, 128, 640).transpose(1, 0, 2)).astype(np.float32)
    AsT = np.ascontiguousarray(Asp.T.reshape(8, 128, 640).transpose(1, 0, 2)).astype(np.float32)
    AcT_hi = AcT.astype(BF)
    AsT_hi = AsT.astype(BF)
    AcT_lo = (AcT - AcT_hi.astype(np.float32)).astype(BF)
    AsT_lo = (AsT - AsT_hi.astype(np.float32)).astype(BF)
    cw = np.zeros(640, dtype=np.float64)
    cw[:513] = 2.0 / 1024.0
    cw[0] = 1.0 / 1024.0
    cw[512] = 1.0 / 1024.0
    cconst = np.ascontiguousarray(cw.reshape(5, 128).T.astype(np.float32))
    kk = np.fft.fftfreq(1024).astype(np.float32)[:513].copy()
    kk[512] = np.float32(-0.5)
    ka = (np.abs(kk) + np.float32(1e-8)).astype(np.float32)
    ka2 = (ka * ka).astype(np.float32)
    kap = np.zeros(640, dtype=np.float32)
    ka2p = np.zeros(640, dtype=np.float32)
    kap[:513] = ka
    ka2p[:513] = ka2
    ka_ch = np.ascontiguousarray(kap.reshape(5, 128).T)
    ka2_ch = np.ascontiguousarray(ka2p.reshape(5, 128).T)
    lx = np.log(np.array([2.0, 4.0, 8.0, 16.0, 32.0], dtype=np.float32)).astype(np.float64)
    den = 5.0 * np.sum(lx * lx) - np.sum(lx) ** 2
    acoef = np.ascontiguousarray((-(5.0 * lx - np.sum(lx)) / den).astype(np.float32)[None, :])
    ident = np.eye(128, dtype=np.float32).astype(BF)
    _A = np.array(
        [[1,1,1,0,1,1,1,0,0,0,1,1],[1,1,0,1,1,1,0,0,0,1,1,0],[1,0,1,1,1,0,0,0,1,1,0,1],
         [0,1,1,1,0,0,0,1,1,0,1,1],[0,0,0,0,1,1,1,1,1,1,1,1],[1,1,1,1,0,0,0,0,0,0,0,0],
         [0,0,0,0,1,1,1,1,0,0,0,0],[1,1,0,0,1,1,0,0,1,1,0,0],[1,0,1,0,1,0,1,0,1,0,1,0],
         [0,1,0,1,0,1,0,1,0,1,0,1],[1,0,0,1,1,0,0,1,1,0,0,1],[0,1,1,0,0,1,1,0,0,1,1,0]],
        dtype=np.float32)
    H = np.concatenate([_A.T, np.eye(12, dtype=np.float32)], axis=1)
    HT = np.ascontiguousarray(H.T).astype(BF)
    Ac_lo = (ac_ch - Ac_hi.astype(np.float32)).astype(BF)
    As_lo = (as_ch - As_hi.astype(np.float32)).astype(BF)
    return dict(cAchi=Ac_hi, cAshi=As_hi, cAclo=Ac_lo, cAslo=As_lo,
                cAcThi=AcT_hi, cAcTlo=AcT_lo, cAsThi=AsT_hi, cAsTlo=AsT_lo,
                ccw=cconst, cka=ka_ch, cka2=ka2_ch, cacoef=acoef,
                cident=ident, cHT=HT)


_CONSTS = _build_consts()
_NC_CACHE = {}


def _v3(ap, n=16):
    return ap.rearrange("p (s n) -> p s n", n=n)


def _b16(small_ap, S):
    return small_ap.rearrange("p (s o) -> p s o", o=1).broadcast_to([128, S, 16])


def build_program():
    nc = bass.Bass("TRN2", target_bir_lowering=False, debug=False)

    xb = nc.dram_tensor("xb", [SEQ, E], F32, kind="ExternalInput").ap()
    xfull = nc.dram_tensor("xfull", [ROWS, E], F32, kind="ExternalInput").ap()
    wg = nc.dram_tensor("wg", [I_FFN, E], F32, kind="ExternalInput").ap()
    wu = nc.dram_tensor("wu", [I_FFN, E], F32, kind="ExternalInput").ap()
    wd = nc.dram_tensor("wd", [E, I_FFN], F32, kind="ExternalInput").ap()
    wel = nc.dram_tensor("wel", [E, 24], F32, kind="ExternalInput").ap()
    bel = nc.dram_tensor("bel", [1, 24], F32, kind="ExternalInput").ap()
    wle = nc.dram_tensor("wle", [24, E], F32, kind="ExternalInput").ap()
    ble = nc.dram_tensor("ble", [1, E], F32, kind="ExternalInput").ap()
    yout = nc.dram_tensor("yout", [SEQ, E], F32, kind="ExternalOutput").ap()
    ci = {}
    for name, arr in _CONSTS.items():
        dt = BF16 if arr.dtype == BF else F32
        ci[name] = nc.dram_tensor(name, list(arr.shape), dt, kind="ExternalInput").ap()

    xqT_d = nc.dram_tensor("xqT_d", [TPC, 128, 8, 128], BF16, kind="Internal").ap()
    xhT_d = nc.dram_tensor("xhT_d", [TPC, 128, 8, 128], BF16, kind="Internal").ap()
    xlT_d = nc.dram_tensor("xlT_d", [TPC, 128, 8, 128], BF16, kind="Internal").ap()
    g_d = nc.dram_tensor("g_d", [TPC, 128, E], F32, kind="Internal").ap()
    vec_d = nc.dram_tensor("vec_d", [4, 128], F32, kind="Internal").ap()

    from contextlib import ExitStack
    es = ExitStack()
    with es:
        WGb = es.enter_context(nc.sbuf_tensor("WGb", [128, 8, I_FFN], BF16))
        WUb = es.enter_context(nc.sbuf_tensor("WUb", [128, 8, I_FFN], BF16))
        WDb = es.enter_context(nc.sbuf_tensor("WDb", [128, 22, E], BF16))
        Chi = es.enter_context(nc.sbuf_tensor("Chi", [128, 8, E], BF16))
        Mhi = es.enter_context(nc.sbuf_tensor("Mhi", [128, 8, 24], BF16))
        Mlo = es.enter_context(nc.sbuf_tensor("Mlo", [128, 8, 24], BF16))
        AF_ = es.enter_context(nc.sbuf_tensor("AF_", [128, 6144], F32))
        ABF = es.enter_context(nc.sbuf_tensor("ABF", [128, 13824], BF16))
        ident = es.enter_context(nc.sbuf_tensor("ident", [128, 128], BF16))
        HTs = es.enter_context(nc.sbuf_tensor("HTs", [24, 12], BF16))
        negb = es.enter_context(nc.sbuf_tensor("negb", [128, 64], F32))
        c3buf = es.enter_context(nc.sbuf_tensor("c3buf", [128, 1], F32))
        sc1 = es.enter_context(nc.sbuf_tensor("sc1", [128, 4], F32))
        cstb = es.enter_context(nc.sbuf_tensor("cstb", [128, 8], F32))
        dsm = es.enter_context(nc.sbuf_tensor("dsm", [1, 16], F32))
        dly = es.enter_context(nc.sbuf_tensor("dly", [1, 8], F32))
        dly2 = es.enter_context(nc.sbuf_tensor("dly2", [1, 8], F32))
        dcnt = es.enter_context(nc.sbuf_tensor("dcnt", [1, 8], F32))
        dll = es.enter_context(nc.sbuf_tensor("dll", [1, 8], F32))
        dslp = es.enter_context(nc.sbuf_tensor("dslp", [1, 8], F32))
        acf = es.enter_context(nc.sbuf_tensor("acf", [1, 5], F32))
        Db = es.enter_context(nc.sbuf_tensor("Db", [128, 1], F32))
        alf0 = es.enter_context(nc.sbuf_tensor("alf0", [128, 1], F32))
        gtm = es.enter_context(nc.sbuf_tensor("gtm", [128, 1], F32))
        alf = es.enter_context(nc.sbuf_tensor("alf", [128, 1], F32))
        bet = es.enter_context(nc.sbuf_tensor("bet", [128, 1], F32))
        gam = es.enter_context(nc.sbuf_tensor("gam", [128, 1], F32))
        kas = es.enter_context(nc.sbuf_tensor("kas", [128, 5], F32))
        ka2s = es.enter_context(nc.sbuf_tensor("ka2s", [128, 5], F32))
        cws = es.enter_context(nc.sbuf_tensor("cws", [128, 5], F32))
        tmp5 = es.enter_context(nc.sbuf_tensor("tmp5", [128, 5], F32))
        t5c = es.enter_context(nc.sbuf_tensor("t5c", [128, 5], F32))
        ub5 = es.enter_context(nc.sbuf_tensor("ub5", [128, 5], F32))
        fr1 = es.enter_context(nc.sbuf_tensor("fr1", [128, 5], F32))
        fr2 = es.enter_context(nc.sbuf_tensor("fr2", [128, 5], F32))
        cfr = es.enter_context(nc.sbuf_tensor("cfr", [128, 5], F32))
        WelS = es.enter_context(nc.sbuf_tensor("WelS", [128, 8, 24], F32))
        WelH = es.enter_context(nc.sbuf_tensor("WelH", [128, 8, 24], BF16))
        WelL = es.enter_context(nc.sbuf_tensor("WelL", [128, 8, 24], BF16))
        AWc = es.enter_context(nc.sbuf_tensor("AWc", [128, 5, 64], F32))
        AWs = es.enter_context(nc.sbuf_tensor("AWs", [128, 5, 64], F32))
        Rh = es.enter_context(nc.sbuf_tensor("Rh", [128, 5, 32], BF16))
        Rl = es.enter_context(nc.sbuf_tensor("Rl", [128, 5, 32], BF16))
        Mf = es.enter_context(nc.sbuf_tensor("Mf", [128, 8, 64], F32))
        bits = es.enter_context(nc.sbuf_tensor("bits", [128, 24], BF16))
        corr = es.enter_context(nc.sbuf_tensor("corr", [128, 24], BF16))
        pg0 = es.enter_context(nc.psum_tensor("pg0", [128, 512], F32))
        pg1 = es.enter_context(nc.psum_tensor("pg1", [128, 512], F32))
        py0 = es.enter_context(nc.psum_tensor("py0", [128, 512], F32))
        py1 = es.enter_context(nc.psum_tensor("py1", [128, 512], F32))
        pdn = es.enter_context(nc.psum_tensor("pdn", [128, 1024], F32))
        pT = es.enter_context(nc.psum_tensor("pT", [128, 1024], BF16))
        sLd = es.enter_context(nc.semaphore("sLd"))
        sL2 = es.enter_context(nc.semaphore("sL2"))
        sSt = es.enter_context(nc.semaphore("sSt"))
        sV = es.enter_context(nc.semaphore("sV"))
        sA = es.enter_context(nc.semaphore("sA"))
        sT = es.enter_context(nc.semaphore("sT"))
        sSy = es.enter_context(nc.semaphore("sSy"))
        sG = es.enter_context(nc.semaphore("sG"))
        block = es.enter_context(nc.Block())
        XT0 = AF_[:, 0:1024]
        XT1 = AF_[:, 1024:2048]
        QW = AF_[:, 2048:3072]
        SIL = AF_[:, 3072:3584]
        HB = AF_[:, 3584:4096]
        SMVS = AF_[:, 4096:4352]
        SMVQ = AF_[:, 4352:4608]
        SMN = AF_[:, 4608:4864]
        SMX = AF_[:, 4864:5120]
        XNF = AF_[:, 5120:6144]
        BLEB = AF_[:, 5120:6144]
        SSB = AF_[:, 4096:5120]
        OSB = AF_[:, 3072:4096]
        NEGM = AF_[:, 2048:2304]
        MSQ = AF_[:, 2304:2560]
        VARE = AF_[:, 2560:2816]
        SQV = AF_[:, 2816:3072]
        R0G = AF_[:, 3072:3328]
        RRG = AF_[:, 3328:3584]
        BVG = AF_[:, 3584:3840]
        RSTDG = AF_[:, 3840:4096]
        OWNSUM = AF_[:, 5120:5184]
        OWNSQ = AF_[:, 5184:5248]
        NEGM64 = AF_[:, 3072:3136]
        RSTD64 = AF_[:, 3136:3200]
        MSQ64 = AF_[:, 5248:5312]
        VARE64 = AF_[:, 5312:5376]
        SQ64 = AF_[:, 5376:5440]
        R064 = AF_[:, 5440:5504]
        RR64 = AF_[:, 5504:5568]
        B64 = AF_[:, 5568:5632]
        MfC = AF_[:, 2048:2240].rearrange("p (c n) -> p c n", n=24)
        Rf = AF_[:, 2048:2208].rearrange("p (c n) -> p c n", n=32)
        qm = AF_[:, 4096:4160]
        qsb = AF_[:, 4160:4224]
        qrb = AF_[:, 4224:4288]
        qs2 = AF_[:, 4288:4352]
        XROW = AF_[0:1, 0:1024]
        XNROW = AF_[0:1, 1024:2048]
        ZROW = AF_[0:1, 3072:4096]
        MROW = AF_[0:1, 0:1024]
        MNR = AF_[0:1, 2048:2176]
        MXR = AF_[0:1, 2176:2304]

        S1 = ABF[:, 0:5120].rearrange("p (c n) -> p c n", n=1024)
        S2 = ABF[:, 5120:10240].rearrange("p (c n) -> p c n", n=1024)
        S1f = ABF[:, 0:5120]
        S2f = ABF[:, 5120:10240]
        AcT_hi_v = S1f.rearrange("p (c n) -> p c n", n=640)
        AcT_lo_v = S2f.rearrange("p (c n) -> p c n", n=640)
        QGA = ABF[:, 0:1024]
        QGB = ABF[:, 1024:2048]
        QSG = ABF[:, 2048:3072]
        QOUT = ABF[:, 3072:5888]
        EV1 = ABF[:, 5888:6912]
        EV2 = ABF[:, 6912:7936]
        TSTG = ABF[:, 7936:8960]
        GSTGF = AF_[:, 5120:6144]
        HQT = ABF[:, 8960:11776].rearrange("p (c n) -> p c n", n=128)
        XQT0 = ABF[:, 11776:12800].rearrange("p (c n) -> p c n", n=128)
        XQT1 = ABF[:, 12800:13824].rearrange("p (c n) -> p c n", n=128)
        XHT = ABF[:, 5888:6912].rearrange("p (c n) -> p c n", n=128)
        XLT = ABF[:, 6912:7936].rearrange("p (c n) -> p c n", n=128)
        GTF = AF_[:, 2048:3072]
        GW = AF_[:, 1024:1088]
        GU = AF_[:, 1088:1152]
        GA = AF_[:, 1152:1216]
        GB = AF_[:, 1024:1088]
        GF0 = AF_[:, 1088:1152]
        GF = AF_[:, 1152:1216]
        BITSF = AF_[:, 1216:1280]
        GD = AF_[:, 1024:1088]
        GE = AF_[:, 1088:1152]
        CORRF = AF_[:, 1152:1216]
        synpm = AF_[:, 1280:1344]
        maskb = AF_[:, 1344:1408]
        WleB = ABF[0:24, 8960:9984]
        bitsT = ABF[0:24, 9984:10112]
        corrT = ABF[0:24, 10112:10240]
        NB05 = cstb[:, 0:1]
        NB5 = cstb[:, 1:2]
        HPIN = cstb[:, 2:3]
        HPIP = cstb[:, 3:4]
        oneb = cstb[:, 4:5]

        pAW = pg0[:, 0:24]
        pAW64 = pg0[:, 0:64]
        latP = pg0[:, 0:24]
        synP = py0[:, 0:12]
        psA = pT[0:24, 0:128]
        psB = pT[0:24, 128:256]

        Q = {"sy": [], "v": [], "sa": [], "te": [], "g": []}
        Cn = {"ld": 0, "ld2": 0, "st": 0, "v": 0, "a": 0, "t": 0, "sy": 0, "g": 0}
        SEM = {"ld": sLd, "ld2": sL2, "st": sSt, "v": sV, "a": sA, "t": sT,
               "sy": sSy, "g": sG}

        def em(eng, fn):
            Q[eng].append(fn)

        def dma_in(dst, src):
            Cn["ld"] += 16
            tgt = Cn["ld"]
            em("sy", lambda e, d=dst, s=src: e.dma_start(out=d, in_=s).then_inc(sLd, 16))
            return tgt

        def dma_in2(dst, src):
            Cn["ld2"] += 16
            tgt = Cn["ld2"]
            em("g", lambda e, d=dst, s=src: e.dma_start(out=d, in_=s).then_inc(sL2, 16))
            return tgt

        def wait(eng, sem_key, val):
            em(eng, lambda e, k=sem_key, vv=val: e.wait_ge(SEM[k], vv))

        def op(eng, fn, inc_key=None):
            if inc_key is None:
                em(eng, fn)
                return None
            Cn[inc_key] += 1
            tgt = Cn[inc_key]

            def _f(e, f=fn, k=inc_key):
                f(e).then_inc(SEM[k], 1)

            em(eng, _f)
            return tgt

        def dma_out(eng_key, dst, src):
            Cn["st"] += 16
            tgt = Cn["st"]
            em(eng_key, lambda e, d=dst, s=src: e.dma_start(out=d, in_=s).then_inc(sSt, 16))
            return tgt

        def barrier():
            dp = {
                "sy": lambda e: e.nop(),
                "v": lambda e: e.engine_nop(),
                "sa": lambda e: e.copy(cstb[:, 7:8], c3buf[:]),
                "te": lambda e: e.ldweights(ident[:]),
                "g": lambda e: e.engine_nop(),
            }
            marks = {}
            for ek, sk in (("sy", "sy"), ("v", "v"), ("sa", "a"), ("te", "t"), ("g", "g")):
                Cn[sk] += 1
                marks[ek] = Cn[sk]
                em(ek, lambda e, k=sk, f=dp[ek]: f(e).then_inc(SEM[k], 1))
            for ek in ("sy", "v", "sa", "te", "g"):
                for ok, sk in (("sy", "sy"), ("v", "v"), ("sa", "a"), ("te", "t"), ("g", "g")):
                    if ok != ek:
                        wait(ek, sk, marks[ok])

        def emit_quant(src, dst, W):
            S = W // 16

            def _q(v, s=src, d=dst, W=W, S=S):
                v.tensor_reduce(qm[:, 0:S], _v3(s), AX, A.max, apply_absolute_value=True)
                v.tensor_copy(sc1[:, 3:4], c3buf[:])
                v.tensor_scalar(qsb[:, 0:64], qm[:, 0:64], SIXTH, 1e-12, A.mult, A.max)
                v.reciprocal(qrb[:, 0:64], qsb[:, 0:64])
                v.tensor_scalar(qs2[:, 0:64], qsb[:, 0:64], 0.5, None, A.mult)
                v.tensor_tensor(_v3(QW[:, :W]), _v3(s), _b16(qrb[:, 0:S], S), A.mult)
                v._custom_dve(FP4A, out=QGA[:, :W], in0=QW[:, :W], in1=c3buf[:],
                              s0=0.0625, s1=0.5625, imm2=1.5625)
                v._custom_dve(FP4B, out=QGB[:, :W], in0=QW[:, :W],
                              s0=6.25, s1=12.25, imm2=25.0)
                v.tensor_tensor(QGA[:, :W], QGA[:, :W], QGB[:, :W], A.add)
                v.tensor_tensor(QGA[:, :W], QGA[:, :W], QSG[:, :W], A.mult)
                return v.tensor_tensor(_v3(d), _v3(QGA[:, :W]), _b16(qs2[:, 0:S], S), A.mult)

            return _q

        for dst, key in ((ident[:], "cident"), (HTs[:], "cHT"), (acf[:], "cacoef"),
                         (kas[:], "cka"), (ka2s[:], "cka2"), (cws[:], "ccw")):
            dma_in(dst, ci[key])
        dma_in(WelS[:], wel.rearrange("(c p) e -> p c e", p=128))
        t_bel = dma_in(negb[:, 0:24], bel.partition_broadcast(128))
        t_act1 = dma_in(AcT_hi_v, ci["cAcThi"])
        t_act2 = dma_in(AcT_lo_v, ci["cAcTlo"])
        consts_ld = Cn["ld"]

        wait("v", "ld", consts_ld)
        op("v", lambda v: v.memset(c3buf[:], 3.0625))
        wait("g", "ld", t_bel)

        def _gsetup(g):
            g.memset(negb[:, 24:64], 0.0)
            g.tensor_scalar(negb[:, 0:24], negb[:, 0:24], -1.0, None, A.mult)
            g.memset(cstb[:, 0:1], -0.5)
            g.memset(cstb[:, 1:2], -5.0)
            g.memset(cstb[:, 2:3], -HALFPI)
            g.memset(cstb[:, 4:5], 1.0)
            return g.memset(cstb[:, 3:4], HALFPI)

        g_setup = op("g", _gsetup, "g")
        op("v", lambda v: v.tensor_copy(WelH[:], WelS[:]))
        v_wel = op("v", lambda v: v.scalar_tensor_tensor(
            WelL[:].rearrange("p c n -> p (c n)"),
            WelH[:].rearrange("p c n -> p (c n)"), -1.0,
            WelS[:].rearrange("p c n -> p (c n)"), A.mult, A.add), "v")

        wait("te", "v", v_wel)
        wait("te", "ld", t_act2)
        aw_last_t = None
        for nm, AW in (("c", AWc), ("s", AWs)):
            if nm == "s":
                wait("sy", "t", aw_last_t)
                t_ast1 = dma_in(AcT_hi_v, ci["cAsThi"])
                t_ast2 = dma_in(AcT_lo_v, ci["cAsTlo"])
                wait("te", "ld", t_ast2)
            for kc in range(5):
                ksl = slice(kc * 128, (kc + 1) * 128)
                mms = []
                for cc in range(8):
                    mms.append((AcT_hi_v[:, cc, ksl], WelH[:, cc, :]))
                    mms.append((AcT_hi_v[:, cc, ksl], WelL[:, cc, :]))
                    mms.append((AcT_lo_v[:, cc, ksl], WelH[:, cc, :]))
                t_mark = None
                for i, (lh, rh) in enumerate(mms):
                    def _mm(te, lh=lh, rh=rh, st=(i == 0), sp=(i == len(mms) - 1)):
                        return te.matmul(pAW, lh, rh, start=st, stop=sp)
                    if i < len(mms) - 1:
                        op("te", _mm)
                    else:
                        t_mark = op("te", _mm, "t")
                aw_last_t = t_mark
                wait("sa", "t", t_mark)
                a_mark = op("sa", lambda s, d=AW[:, kc, :]: s.copy(d, pAW64), "a")
                wait("te", "a", a_mark)

        op("v", lambda v: v.memset(OWNSUM[:, 0:32], 1.0))
        op("v", lambda v: v.memset(OWNSQ[:, 0:32], 1.0))
        v_stats_base = Cn["v"]
        a_stats_base = Cn["a"]
        NT_ALL = STILES + TPC
        for t in range(NT_ALL):
            buf = XT0 if t % 2 == 0 else XT1
            qk = "sy" if t % 2 == 0 else "g"
            lsem = "ld" if t % 2 == 0 else "ld2"
            if t >= 2:
                wait(qk, "v", v_stats_base + (t - 1))
                wait(qk, "a", a_stats_base + (t - 1))
            if t < STILES:
                srcap = xfull[t * 128:(t + 1) * 128, :]
            else:
                tt = t - STILES
                srcap = xb[tt * 128:(tt + 1) * 128, :]
            t_ld = dma_in(buf, srcap) if t % 2 == 0 else dma_in2(buf, srcap)
            wait("sa", lsem, t_ld)
            if t < STILES:
                def _sq(s, b=buf, t=t):
                    s.activation(QW, b, AF.Square, accum_out=SMVQ[:, t:t + 1])
                    return s.activation(QW, b, AF.Square, bias=oneb,
                                        accum_out=SMVS[:, t:t + 1])
            else:
                def _sq(s, b=buf, tt=t - STILES):
                    s.activation(QW, b, AF.Square, accum_out=OWNSQ[:, tt + 32:tt + 33])
                    return s.activation(QW, b, AF.Square, bias=oneb,
                                        accum_out=OWNSUM[:, tt + 32:tt + 33])
            op("sa", _sq, "a")
            wait("v", lsem, t_ld)

            if t < STILES:
                def _st(v, b=buf, t=t):
                    v.tensor_reduce(SMN[:, t:t + 1], b, AX, A.min)
                    v.reduce_max(SMX[:, t:t + 1], b, AX)
                    return v.tensor_copy(sc1[:, 3:4], c3buf[:])

                op("v", _st, "v")
            else:
                Cn["v"] += 1
                em("v", lambda e: e.engine_nop().then_inc(sV, 1))

        def _wp1(v):
            v.tensor_tensor(NEGM, SMVS, SMVQ, A.subtract)
            v.tensor_scalar(NEGM, NEGM, -1024.0, -1.0 / 2048.0, A.add, A.mult)
            return v.tensor_copy(sc1[:, 3:4], c3buf[:])

        v_wp1 = op("v", _wp1, "v")
        wait("sa", "v", v_wp1)
        a_wp1 = op("sa", lambda s: s.activation(MSQ, NEGM, AF.Square), "a")
        wait("v", "a", a_wp1)

        def _wp2(v):
            v.scalar_tensor_tensor(VARE, SMVQ, 1.0 / 1024.0, MSQ,
                                   A.mult, A.subtract)
            return v.tensor_scalar(VARE, VARE, 1e-5, None, A.add)

        v_wp2 = op("v", _wp2, "v")
        wait("sa", "v", v_wp2)
        a_wp2 = op("sa", lambda s: s.activation(SQV, VARE, AF.Sqrt), "a")
        wait("v", "a", a_wp2)
        v_wp3 = op("v", lambda v: v.reciprocal(R0G, SQV), "v")
        wait("sa", "v", v_wp3)
        a_wp3 = op("sa", lambda s: s.activation(RRG, R0G, AF.Square), "a")
        wait("v", "a", a_wp3)

        def _wp4(v):
            v.tensor_tensor(BVG, RRG, VARE, A.mult)
            v.tensor_scalar(BVG, BVG, -0.5, 1.5, A.mult, A.add)
            v.tensor_tensor(RSTDG, R0G, BVG, A.mult)
            v.tensor_tensor(SMN, SMN, NEGM, A.add)
            v.tensor_tensor(SMN, SMN, RSTDG, A.mult)
            v.tensor_tensor(SMX, SMX, NEGM, A.add)
            v.tensor_tensor(SMX, SMX, RSTDG, A.mult)
            v.tensor_reduce(sc1[:, 0:1], SMN, AX, A.min)
            v.reduce_max(sc1[:, 1:2], SMX, AX)
            return v.tensor_copy(sc1[:, 3:4], c3buf[:])

        v_wp = op("v", _wp4, "v")

        wait("g", "v", v_wp)
        wait("g", "g", g_setup)

        def _dg1(g):
            g.tensor_copy(dsm[0:1, 3:4], NEGM[0:1, 0:1])
            g.tensor_copy(dsm[0:1, 4:5], RSTDG[0:1, 0:1])
            return g.tensor_tensor(dsm[0:1, 7:8], dsm[0:1, 3:4], dsm[0:1, 4:5], A.mult)

        g_d1 = op("g", _dg1, "g")
        dma_out("g", vec_d[0, :], sc1[:, 0:1])
        st_b = dma_out("g", vec_d[1, :], sc1[:, 1:2])
        wait("sy", "st", st_b)
        dma_in(MNR[:, 0:128], vec_d[0:1, :])
        dma_in(MXR[:, 0:128], vec_d[1:2, :])
        t_x0 = dma_in(XROW, xfull[0:1, :])
        wait("v", "ld", t_x0)

        def _dv1(v):
            v.tensor_reduce(dsm[0:1, 0:1], MNR[:, 0:128], AX, A.min)
            v.reduce_max(dsm[0:1, 1:2], MXR[:, 0:128], AX)
            return v.tensor_copy(sc1[:, 3:4], c3buf[:])

        v_d1 = op("v", _dv1, "v")
        wait("g", "v", v_d1)

        def _dg2(g):
            g.tensor_tensor(dsm[0:1, 5:6], dsm[0:1, 1:2], dsm[0:1, 0:1], A.subtract)
            return g.tensor_scalar(dsm[0:1, 5:6], dsm[0:1, 5:6], 1e-10, None, A.add)

        g_d2 = op("g", _dg2, "g")
        wait("v", "g", g_d2)
        v_d2 = op("v", lambda v: v.reciprocal(dsm[0:1, 2:3], dsm[0:1, 5:6]), "v")
        wait("g", "v", v_d2)

        def _dg3(g):
            g.tensor_tensor(dsm[0:1, 6:7], dsm[0:1, 0:1], dsm[0:1, 2:3], A.mult)
            return g.tensor_scalar(dsm[0:1, 6:7], dsm[0:1, 6:7], -1.0, None, A.mult)

        g_d3 = op("g", _dg3, "g")
        wait("sa", "g", g_d3)

        def _da1(s):
            s.activation(XNROW, XROW, AF.Identity,
                         bias=dsm[0:1, 7:8], scale=dsm[0:1, 4:5])
            return s.activation(ZROW, XNROW, AF.Identity,
                                bias=dsm[0:1, 6:7], scale=dsm[0:1, 2:3])

        a_d1 = op("sa", _da1, "a")
        wait("v", "a", a_d1)
        NNJ = (4, 16, 64, 256, 1024)

        def _dv3(v):
            for j, nn in enumerate(NNJ):
                v.tensor_reduce(dly[0:1, j:j + 1], ZROW[:, 0:nn], AX, A.add)
            return v.tensor_copy(sc1[:, 3:4], c3buf[:])

        v_d3 = op("v", _dv3, "v")
        wait("g", "v", v_d3)

        def _dg4(g):
            last = None
            for j, nn in enumerate(NNJ):
                last = g.tensor_scalar(dly2[0:1, j:j + 1], dly[0:1, j:j + 1],
                                       1.0 / nn, None, A.mult)
            return last

        g_d4 = op("g", _dg4, "g")
        wait("v", "g", g_d4)

        def _dv4(v):
            for j, nn in enumerate(NNJ):
                v.tensor_scalar(MROW, ZROW, dly2[0:1, j:j + 1], None, A.is_gt)
                v.tensor_reduce(dcnt[0:1, j:j + 1], MROW[:, 0:nn], AX, A.add)
            return v.tensor_copy(sc1[:, 3:4], c3buf[:])

        v_d4 = op("v", _dv4, "v")
        wait("g", "v", v_d4)
        g_d5 = op("g", lambda g: g.tensor_scalar(dcnt[0:1, 0:5], dcnt[0:1, 0:5],
                                                 1.0, None, A.max), "g")
        wait("sa", "g", g_d5)
        a_d2 = op("sa", lambda s: s.activation(dll[0:1, 0:5], dcnt[0:1, 0:5], AF.Ln), "a")
        wait("g", "a", a_d2)
        g_d6 = op("g", lambda g: g.tensor_tensor(dslp[0:1, 0:5], dll[0:1, 0:5],
                                                 acf[0:1, :], A.mult), "g")
        wait("v", "g", g_d6)

        def _dv5(v):
            v.tensor_reduce(dsm[0:1, 8:9], dslp[0:1, 0:5], AX, A.add)
            return v.tensor_copy(sc1[:, 3:4], c3buf[:])

        v_d5 = op("v", _dv5, "v")
        wait("g", "v", v_d5)
        g_d7 = op("g", lambda g: g.tensor_scalar(dsm[0:1, 9:10], dsm[0:1, 8:9],
                                                 1.0, 2.0, A.max, A.min), "g")
        wait("g", "g", g_d7)
        st_d = dma_out("g", vec_d[2, 0:1], dsm[0:1, 9:10])
        wait("sy", "st", st_d)
        t_db = dma_in(Db[:], vec_d[2:3, 0:1].partition_broadcast(128))

        wait("sa", "ld", t_db)
        a_f0 = op("sa", lambda s: s.activation(alf0[:], Db[:], AF.Square), "a")
        wait("g", "a", a_f0)

        def _fg1(g):
            g.tensor_scalar(gtm[:], alf0[:], 0.2, None, A.mult)
            g.tensor_tensor(alf[:], gtm[:], Db[:], A.add)
            g.tensor_scalar(bet[:], Db[:], 0.5, 1.0, A.mult, A.add)
            return g.tensor_scalar(gam[:], Db[:], 0.1, None, A.mult)

        g_f1 = op("g", _fg1, "g")
        wait("sa", "g", g_f1)

        def _fa1(s):
            s.activation(tmp5[:], kas[:], AF.Identity, bias=alf[:], scale=bet[:])
            s.activation(ub5[:], tmp5[:], AF.Abs, bias=HPIN)
            s.activation(fr1[:], ub5[:], AF.Sin, bias=HPIP, scale=-1.0)
            return s.activation(t5c[:], ka2s[:], AF.Identity, bias=alf[:], scale=gam[:])

        a_f1 = op("sa", _fa1, "a")
        wait("g", "a", a_f1)
        g_f2 = op("g", lambda g: g.tensor_tensor(t5c[:], t5c[:], kas[:], A.subtract), "g")
        wait("sa", "g", g_f2)
        a_f2 = op("sa", lambda s: s.activation(fr2[:], t5c[:], AF.Sin,
                                               bias=HPIP, scale=-1.0), "a")
        wait("g", "a", a_f2)

        def _fg3(g):
            g.tensor_tensor(cfr[:], fr1[:], fr2[:], A.mult)
            return g.tensor_tensor(cfr[:], cfr[:], cws[:], A.mult)

        g_cfr = op("g", _fg3, "g")

        wait("sy", "t", aw_last_t)
        t_m2 = dma_in(S1, ci["cAchi"])
        t_m2b = dma_in(S2, ci["cAclo"])
        m_last_t = None
        m_last_v = None
        for nm in ("c", "s"):
            AW = AWc if nm == "c" else AWs
            if nm == "s":
                wait("sy", "t", m_last_t)
                t_m2 = dma_in(S1, ci["cAshi"])
                t_m2b = dma_in(S2, ci["cAslo"])
            cfrb32 = cfr[:].rearrange("p (c o) -> p c o", o=1).broadcast_to([128, 5, 32])

            def _rbuild(v, AW=AW, cfrb32=cfrb32):
                return v.tensor_tensor(Rf, AW[:, :, 0:32], cfrb32, A.mult)

            if nm == "c":
                wait("v", "g", g_cfr)
            v_r = op("v", _rbuild, "v")
            op("v", lambda v: v.tensor_copy(Rh[:], Rf))
            v_rl = op("v", lambda v: v.scalar_tensor_tensor(
                Rl[:].rearrange("p c n -> p (c n)"),
                Rh[:].rearrange("p c n -> p (c n)"), -1.0,
                Rf.rearrange("p c n -> p (c n)"), A.mult, A.add), "v")
            wait("te", "v", v_rl)
            wait("te", "ld", t_m2b)
            for jc in range(8):
                jsl = slice(jc * 128, (jc + 1) * 128)
                mms = []
                for kc in range(5):
                    mms.append((S1[:, kc, jsl], Rh[:, kc, 0:24]))
                    mms.append((S1[:, kc, jsl], Rl[:, kc, 0:24]))
                    mms.append((S2[:, kc, jsl], Rh[:, kc, 0:24]))
                t_mk = None
                for i, (lh, rh) in enumerate(mms):
                    def _mm(te, lh=lh, rh=rh, st=(i == 0), sp=(i == len(mms) - 1)):
                        return te.matmul(pAW, lh, rh, start=st, stop=sp)
                    if i < len(mms) - 1:
                        op("te", _mm)
                    else:
                        t_mk = op("te", _mm, "t")
                m_last_t = t_mk
                wait("v", "t", t_mk)
                if nm == "c":
                    m_last_v = op("v", lambda v, jc=jc: v.tensor_copy(Mf[:, jc, :], pAW64), "v")
                else:
                    m_last_v = op("v", lambda v, jc=jc: v.tensor_tensor(
                        Mf[:, jc, :], Mf[:, jc, :], pAW64, A.add), "v")
                wait("te", "v", m_last_v)

        def _msplit(v):
            v.tensor_copy(MfC, Mf[:, :, 0:24])
            v.tensor_copy(Mhi[:], MfC)
            return v.scalar_tensor_tensor(
                Mlo[:].rearrange("p c n -> p (c n)"),
                Mhi[:].rearrange("p c n -> p (c n)"), -1.0,
                MfC.rearrange("p c n -> p (c n)"), A.mult, A.add)

        v_msp = op("v", _msplit, "v")

        c_last_v = v_msp
        for nm in ("c", "s"):
            wait("sy", "t", m_last_t)
            wait("sy", "v", c_last_v)
            t_c1 = dma_in(S1, ci["cAchi"] if nm == "c" else ci["cAshi"])
            t_c2 = dma_in(S2, ci["cAchi"] if nm == "c" else ci["cAshi"])
            wait("v", "ld", t_c2)
            cfrb1k = cfr[:].rearrange("p (c o) -> p c o", o=1).broadcast_to([128, 5, 1024])

            def _pbuild(v, cfrb1k=cfrb1k):
                return v.tensor_tensor(S2, S2, cfrb1k, A.mult)

            v_p = op("v", _pbuild, "v")
            wait("te", "v", v_p)
            wait("te", "ld", t_c1)
            for jc in range(8):
                jsl = slice(jc * 128, (jc + 1) * 128)
                t_ck = None
                for half in range(2):
                    hsl = slice(half * 512, (half + 1) * 512)
                    for kc in range(5):
                        def _mm(te, kc=kc, jsl=jsl, hsl=hsl, st=(kc == 0), sp=(kc == 4)):
                            return te.matmul(pdn[:, hsl], S1[:, kc, jsl],
                                             S2[:, kc, hsl], start=st, stop=sp)
                        if kc < 4 or half == 0:
                            op("te", _mm)
                        else:
                            t_ck = op("te", _mm, "t")
                wait("v", "t", t_ck)
                if nm == "c":
                    c_last_v = op("v", lambda v, jc=jc: v.tensor_copy(Chi[:, jc, :], pdn[:]), "v")
                else:
                    c_last_v = op("v", lambda v, jc=jc: v.tensor_add(Chi[:, jc, :], Chi[:, jc, :], pdn[:]), "v")
                wait("te", "v", c_last_v)

        barrier()

        prev_v = None
        prev_tr = None

        def wq_tile(rows_src, m, dst_fn, subw):
            nonlocal prev_v, prev_tr
            marks = []
            for s_i, w in enumerate(subw):
                o = sum(subw[:s_i])
                if prev_v is not None:
                    wait("sy", "v", prev_v)
                t_ld = dma_in(XT0[:, 0:w], rows_src[:, o:o + w])
                wait("sa", "ld", t_ld)
                if prev_v is not None:
                    wait("sa", "v", prev_v)
                a_sg = op("sa", lambda s, w=w: s.activation(QSG[:, 0:w], XT0[:, 0:w], AF.Sign), "a")
                wait("v", "ld", t_ld)
                wait("v", "a", a_sg)
                if prev_tr is not None:
                    wait("v", "t", prev_tr)
                v_mk = op("v", emit_quant(XT0[:, 0:w], QOUT[:, o:o + w], w), "v")
                marks.append(v_mk)
                prev_v = v_mk
            nblk = sum(subw) // 128
            wait("te", "v", marks[-1])
            for g0 in range(0, nblk, 8):
                gn = min(8, nblk - g0)
                t_tr = None
                for b in range(gn):
                    def _tr(te, b=b, g0=g0):
                        return te.transpose(pT[:, b * 128:(b + 1) * 128],
                                            QOUT[:, (g0 + b) * 128:(g0 + b + 1) * 128],
                                            ident[:])
                    if b < gn - 1:
                        op("te", _tr)
                    else:
                        t_tr = op("te", _tr, "t")
                prev_tr = t_tr
                wait("sa", "t", t_tr)
                a_ev = op("sa", lambda s, d=dst_fn(m, g0, gn), gn=gn:
                          s.copy(d, pT[:, 0:gn * 128].rearrange("p (c n) -> p c n", n=128)), "a")
                wait("te", "a", a_ev)

        for m in range(22):
            wq_tile(wg[m * 128:(m + 1) * 128, :], m,
                    lambda m, g0, gn: WGb[:, g0:g0 + gn, m * 128:(m + 1) * 128], [1024])
        for m in range(22):
            wq_tile(wu[m * 128:(m + 1) * 128, :], m,
                    lambda m, g0, gn: WUb[:, g0:g0 + gn, m * 128:(m + 1) * 128], [1024])
        for m in range(8):
            wq_tile(wd[m * 128:(m + 1) * 128, :], m,
                    lambda m, g0, gn: WDb[:, g0:g0 + gn, m * 128:(m + 1) * 128],
                    [1024, 1024, 768])

        barrier()

        def _owp1(v):
            v.tensor_tensor(NEGM64, OWNSUM, OWNSQ, A.subtract)
            v.tensor_scalar(NEGM64, NEGM64, -1024.0, -1.0 / 2048.0, A.add, A.mult)
            return v.tensor_copy(sc1[:, 3:4], c3buf[:])

        v_o1 = op("v", _owp1, "v")
        wait("sa", "v", v_o1)
        a_o1 = op("sa", lambda s: s.activation(MSQ64, NEGM64, AF.Square), "a")
        wait("v", "a", a_o1)

        def _owp2(v):
            v.scalar_tensor_tensor(VARE64, OWNSQ, 1.0 / 1024.0, MSQ64,
                                   A.mult, A.subtract)
            return v.tensor_scalar(VARE64, VARE64, 1e-5, None, A.add)

        v_o2 = op("v", _owp2, "v")
        wait("sa", "v", v_o2)
        a_o2 = op("sa", lambda s: s.activation(SQ64, VARE64, AF.Sqrt), "a")
        wait("v", "a", a_o2)
        v_o3 = op("v", lambda v: v.reciprocal(R064, SQ64), "v")
        wait("sa", "v", v_o3)
        a_o3 = op("sa", lambda s: s.activation(RR64, R064, AF.Square), "a")
        wait("v", "a", a_o3)

        def _owp4(v):
            v.tensor_tensor(B64, RR64, VARE64, A.mult)
            v.tensor_scalar(B64, B64, -0.5, 1.5, A.mult, A.add)
            return v.tensor_tensor(RSTD64, R064, B64, A.mult)

        v_own = op("v", _owp4, "v")

        ln_marks = []
        tr_last = []
        for t in range(TPC):
            buf = XT0 if t % 2 == 0 else XT1
            if t >= 2:
                wait("sy", "v", ln_marks[t - 2])
            t_ld = dma_in(buf, xb[t * 128:(t + 1) * 128, :])
            wait("v", "ld", t_ld)
            v_ln = op("v", lambda v, b=buf, t=t: v.tensor_scalar(
                XNF, b, NEGM64[:, 32 + t:33 + t], RSTD64[:, 32 + t:33 + t],
                A.add, A.mult), "v")
            ln_marks.append(v_ln)
            wait("sa", "v", v_ln)
            if t >= 1:
                wait("sa", "t", tr_last[t - 1])

            def _ahi(s):
                s.activation(QSG[:, 0:1024], XNF, AF.Sign)
                return s.activation(EV1, XNF, AF.Copy)

            a_hi = op("sa", _ahi, "a")
            wait("v", "a", a_hi)
            if t >= 1:
                wait("v", "t", tr_last[t - 1])
            op("v", emit_quant(XNF, QOUT[:, 0:1024], 1024))
            v_a = op("v", lambda v: v.scalar_tensor_tensor(EV2, EV1, -1.0, XNF, A.mult, A.add), "v")

            wait("te", "v", v_a)
            for src, dstd in ((QOUT[:, 0:1024], xqT_d), (EV1, xhT_d), (EV2, xlT_d)):
                t_tr = None
                for b in range(8):
                    def _tr(te, src=src, b=b):
                        return te.transpose(pT[:, b * 128:(b + 1) * 128],
                                            src[:, b * 128:(b + 1) * 128], ident[:])
                    if b < 7:
                        op("te", _tr)
                    else:
                        t_tr = op("te", _tr, "t")
                wait("sa", "t", t_tr)
                a_ev = op("sa", lambda s: s.copy(TSTG, pT[:]), "a")
                wait("te", "a", a_ev)
                wait("g", "a", a_ev)
                st_mk = dma_out("g", dstd[t], TSTG.rearrange("p (c n) -> p c n", n=128))
                wait("sa", "st", st_mk)
            tr_last.append(t_tr)

        barrier()

        chunks = [(0, 512), (512, 512), (1024, 512), (1536, 512), (2048, 512), (2560, 256)]
        sil_marks, h_marks, hq_marks, tr_marks, gy_marks = [], [], [], [], []
        tile_last_gy = []
        prev_g_st = None
        pending_tr = None

        def emit_tr(gk, io, iw):
            wait("te", "v", hq_marks[gk])
            nb = iw // 128
            tr_mk = None
            for b in range(nb):
                def _tr(te, b=b, io=io):
                    return te.transpose(pT[:, b * 128:(b + 1) * 128],
                                        QOUT[:, io + b * 128: io + (b + 1) * 128],
                                        ident[:])
                if b < nb - 1:
                    op("te", _tr)
                else:
                    tr_mk = op("te", _tr, "t")
            tr_marks.append(tr_mk)
            wait("sa", "t", tr_mk)
            a_ev = op("sa", lambda s, io=io, nb=nb:
                      s.copy(HQT[:, io // 128: io // 128 + nb, :],
                             pT[:, 0:nb * 128].rearrange("p (c n) -> p c n", n=128)), "a")
            wait("te", "a", a_ev)

        for t in range(TPC):
            xq = XQT0 if t % 2 == 0 else XQT1
            if t >= 2:
                wait("sy", "t", tile_last_gy[t - 2])
            t_ld = dma_in(xq, xqT_d[t])
            wait("te", "ld", t_ld)
            for ci_, (io, iw) in enumerate(chunks):
                gk = t * 6 + ci_
                pg = pg0 if gk % 2 == 0 else pg1
                py = py0 if gk % 2 == 0 else py1
                if gk >= 2:
                    wait("te", "a", sil_marks[gk - 2])
                    wait("te", "v", h_marks[gk - 2])
                for cc in range(8):
                    def _mmg(te, cc=cc, io=io, iw=iw, pg=pg, xq=xq,
                             st=(cc == 0), sp=(cc == 7)):
                        return te.matmul(pg[:, 0:iw], xq[:, cc, :],
                                         WGb[:, cc, io:io + iw], start=st, stop=sp)
                    op("te", _mmg)
                gy_mk = None
                for cc in range(8):
                    def _mmy(te, cc=cc, io=io, iw=iw, py=py, xq=xq,
                             st=(cc == 0), sp=(cc == 7)):
                        return te.matmul(py[:, 0:iw], xq[:, cc, :],
                                         WUb[:, cc, io:io + iw], start=st, stop=sp)
                    if cc < 7:
                        op("te", _mmy)
                    else:
                        gy_mk = op("te", _mmy, "t")
                gy_marks.append(gy_mk)
                if pending_tr is not None:
                    emit_tr(*pending_tr)
                    pending_tr = None
                wait("sa", "t", gy_mk)
                if gk >= 1:
                    wait("sa", "v", h_marks[gk - 1])
                sil_marks.append(op("sa", lambda s, pg=pg, iw=iw:
                                    s.activation(SIL[:, 0:iw], pg[:, 0:iw], AF.Silu), "a"))
                wait("v", "a", sil_marks[gk])
                h_marks.append(op("v", lambda v, py=py, iw=iw:
                                  v.tensor_mul(HB[:, 0:iw], SIL[:, 0:iw], py[:, 0:iw]), "v"))
                wait("sa", "v", h_marks[gk])
                if gk >= 1:
                    wait("sa", "v", hq_marks[gk - 1])
                a_sgn = op("sa", lambda s, iw=iw:
                           s.activation(QSG[:, 0:iw], HB[:, 0:iw], AF.Sign), "a")
                wait("v", "a", a_sgn)
                if gk >= 6:
                    wait("v", "t", tr_marks[gk - 6])
                hq_marks.append(op("v", emit_quant(HB[:, 0:iw], QOUT[:, io:io + iw], iw), "v"))
                pending_tr = (gk, io, iw)
            emit_tr(*pending_tr)
            pending_tr = None
            tile_last_gy.append(gy_marks[-1])
            t_dn = None
            for ic in range(22):
                for half in range(2):
                    def _mmd(te, ic=ic, half=half, st=(ic == 0), sp=(ic == 21)):
                        return te.matmul(pdn[:, half * 512:(half + 1) * 512],
                                         HQT[:, ic, :], WDb[:, ic, half * 512:(half + 1) * 512],
                                         start=st, stop=sp)
                    if ic == 21 and half == 1:
                        t_dn = op("te", _mmd, "t")
                    else:
                        op("te", _mmd)
            wait("sa", "t", t_dn)
            if prev_g_st is not None:
                wait("sa", "st", prev_g_st)
            a_g = op("sa", lambda s: s.activation(GSTGF, pdn[:], AF.Silu), "a")
            wait("g", "a", a_g)
            prev_g_st = dma_out("g", g_d[t], GSTGF)
            wait("te", "a", a_g)

        barrier()

        wait("sy", "st", prev_g_st)
        t_ble = dma_in(BLEB, ble.partition_broadcast(128))
        wait("v", "ld", t_ble)
        Cn["st"] += 16
        _wle_st = Cn["st"]
        em("g", lambda e: e.dma_start(out=WleB[:], in_=wle).then_inc(sSt, 16))
        wait("te", "st", _wle_st)

        def _gmask(g):
            g.memset(maskb[:], 0.0)
            return g.memset(maskb[:, 0:12], 1.0)

        g_mask = op("g", _gmask, "g")
        wait("v", "g", g_mask)
        prev_cb = None
        prev_sty = None
        for t in range(TPC):
            if prev_cb is not None:
                wait("sy", "v", prev_cb)
                wait("sy", "st", prev_sty)
            dma_in(XHT, xhT_d[t])
            t2 = dma_in(XLT, xlT_d[t])
            t3 = dma_in(XT0, xb[t * 128:(t + 1) * 128, :])
            t4 = dma_in(GTF, g_d[t])
            wait("te", "ld", t2)
            for half in range(2):
                hsl = slice(half * 512, (half + 1) * 512)
                for cc in range(8):
                    def _mmx(te, cc=cc, hsl=hsl, st=(cc == 0)):
                        return te.matmul(pdn[:, hsl], XHT[:, cc, :],
                                         Chi[:, cc, hsl], start=st, stop=False)
                    op("te", _mmx)
            lat_mms = ([(XHT[:, cc, :], Mhi[:, cc, :]) for cc in range(8)]
                       + [(XHT[:, cc, :], Mlo[:, cc, :]) for cc in range(8)]
                       + [(XLT[:, cc, :], Mhi[:, cc, :]) for cc in range(8)])
            t_lat = None
            for i, (lh, rh) in enumerate(lat_mms):
                def _mml(te, lh=lh, rh=rh, st=(i == 0), sp=(i == 23)):
                    return te.matmul(latP, lh, rh, start=st, stop=sp)
                if i < 23:
                    op("te", _mml)
                else:
                    t_lat = op("te", _mml, "t")
            wait("v", "t", t_lat)
            v_bits = op("v", lambda v: v.tensor_tensor(
                BITSF, pg0[:, 0:64], negb[:], A.is_gt), "v")
            wait("sa", "v", v_bits)
            a_bc = op("sa", lambda s: s.copy(bits[:], BITSF[:, 0:24]), "a")
            wait("te", "a", a_bc)
            t_bt = op("te", lambda te: te.transpose(psA, bits[:], ident[:]), "t")
            wait("sa", "t", t_bt)
            a_bt = op("sa", lambda s: s.copy(bitsT[:], psA), "a")
            wait("te", "a", a_bt)
            t_syn = op("te", lambda te: te.matmul(synP, bitsT[:], HTs[:], start=True, stop=True), "t")
            wait("v", "t", t_syn)

            def _golay(v):
                v.tensor_scalar(synpm, py0[:, 0:64], 0.5, None, A.is_gt)
                v.tensor_reduce(sc1[:, 2:3], synpm[:, 0:12], AX, A.add)
                v.tensor_copy(sc1[:, 3:4], c3buf[:])
                v.tensor_copy(GW, sc1[:, 2:3].broadcast_to([128, 64]))
                v.tensor_scalar(GU, GW, 3.5, None, A.is_lt)
                v.tensor_tensor(GA, synpm, GU, A.mult)
                v.tensor_tensor(GF, GA, maskb, A.mult)
                v.tensor_tensor(GD, BITSF, GF, A.mult)
                v.tensor_tensor(GE, BITSF, GF, A.add)
                return v.scalar_tensor_tensor(CORRF, GD, -2.0, GE, A.mult, A.add)

            v_go = op("v", _golay, "v")
            wait("sa", "v", v_go)

            def _acorr(s):
                s.copy(corr[:, 0:12], CORRF[:, 0:12])
                return s.copy(corr[:, 12:24], BITSF[:, 12:24])

            a_co = op("sa", _acorr, "a")
            wait("te", "a", a_co)
            t_ct = op("te", lambda te: te.transpose(psB, corr[:], ident[:]), "t")
            wait("sa", "t", t_ct)
            a_ct = op("sa", lambda s: s.copy(corrT[:], psB), "a")
            wait("te", "a", a_ct)
            t_wle2 = None
            for half in range(2):
                def _mmw(te, half=half):
                    return te.matmul(pdn[:, half * 512:(half + 1) * 512], corrT[:],
                                     WleB[:, half * 512:(half + 1) * 512],
                                     start=False, stop=True)
                if half == 0:
                    op("te", _mmw)
                else:
                    t_wle2 = op("te", _mmw, "t")
            wait("v", "t", t_wle2)
            wait("v", "ld", max(t3, t4))

            def _comb(v):
                v.tensor_add(SSB, pdn[:], BLEB)
                v.tensor_mul(SSB, SSB, GTF)
                return v.tensor_add(OSB, SSB, XT0)

            prev_cb = op("v", _comb, "v")
            wait("te", "v", prev_cb)
            wait("g", "v", prev_cb)
            prev_sty = dma_out("g", yout[t * 128:(t + 1) * 128, :], OSB)

        wait("g", "st", Cn["st"])

        @block.sync
        def _(e):
            for fn in Q["sy"]:
                fn(e)

        @block.vector
        def _(e):
            for fn in Q["v"]:
                fn(e)

        @block.scalar
        def _(e):
            for fn in Q["sa"]:
                fn(e)

        @block.tensor
        def _(e):
            for fn in Q["te"]:
                fn(e)

        @block.gpsimd
        def _(e):
            for fn in Q["g"]:
                fn(e)

    mybir.codegen_inst_isa_subclasses(nc)
    return nc


def kernel(L0_attn_x, L0_attn_W_el, L0_attn_b_el, L0_attn_W_le, L0_attn_b_le,
           L0_ffn_w_gate, L0_ffn_w_up, L0_ffn_w_down):
    x = np.asarray(L0_attn_x)
    xf = np.ascontiguousarray(x.reshape(ROWS, E))
    if "nc" not in _NC_CACHE:
        _NC_CACHE["nc"] = build_program()
    nc = _NC_CACHE["nc"]
    base = {
        "xfull": xf,
        "wg": np.ascontiguousarray(np.asarray(L0_ffn_w_gate)),
        "wu": np.ascontiguousarray(np.asarray(L0_ffn_w_up)),
        "wd": np.ascontiguousarray(np.asarray(L0_ffn_w_down)),
        "wel": np.ascontiguousarray(np.asarray(L0_attn_W_el)),
        "bel": np.ascontiguousarray(np.asarray(L0_attn_b_el).reshape(1, 24)),
        "wle": np.ascontiguousarray(np.asarray(L0_attn_W_le)),
        "ble": np.ascontiguousarray(np.asarray(L0_attn_b_le).reshape(1, E)),
    }
    for name, arr in _CONSTS.items():
        base[name] = arr
    in_maps = []
    for c in range(N_CORES):
        m = dict(base)
        m["xb"] = np.ascontiguousarray(x[c])
        in_maps.append(m)
    res = run_bass_kernel_spmd(nc, in_maps, core_ids=list(range(N_CORES)))
    out = np.stack([res.results[c]["yout"] for c in range(N_CORES)], axis=0)
    return out.astype(np.float32, copy=False)


# version 12
# speedup vs baseline: 1.2905x; 1.0016x over previous
import math

import numpy as np
import ml_dtypes

import concourse.bass as bass
import concourse.mybir as mybir
from concourse.bass_utils import run_bass_kernel_spmd
from concourse.dve_op import DveOp
from concourse.dve_spec import Spec, Src0, C0, C1, C2, C3, lower as dve_lower, sq
from concourse.dve_table_gen import dve_ver_for
import concourse.dve_ops as dve_ops_mod

F32 = mybir.dt.float32
BF16 = mybir.dt.bfloat16
A = mybir.AluOpType
AF = mybir.ActivationFunctionType
AX = mybir.AxisListType.X
BF = ml_dtypes.bfloat16

B_, SEQ, E = 8, 4096, 1024
ROWS = B_ * SEQ
TPC = SEQ // 128
STILES = ROWS // 128
I_FFN = 2816
HALFPI = float(np.float32(math.pi / 2.0))
SIXTH = float(np.float32(1.0 / 6.0))
N_CORES = 8


def _register_op(name, spec, reference):
    for op_ in dve_ops_mod.OPS:
        if op_.name == name:
            return op_
    ver = dve_ver_for("TRN2")
    sha = dve_lower(spec, ver=ver, perf_en=False, name=name).sha(ver)
    op_ = DveOp(name, spec, subdim=False, uops_sha={ver: sha}, reference=reference)
    dve_ops_mod.OPS.append(op_)
    row = max(dve_ops_mod._SUB_OPCODE_FOR_NAME.values()) + 1
    assert row < 0x20, "custom-DVE opcode rows exhausted"
    dve_ops_mod._SUB_OPCODE_FOR_NAME[name] = row
    dve_ops_mod.CUSTOM_DVE_REFS[name] = reference
    return op_


def _fp4a_ref(in0, in1, s0, s1, imm2):
    y = in0 * in0
    return (y > s0) * 1.0 + (y > s1) + (y > imm2) + (y > in1)


def _fp4b_ref(in0, in1, s0, s1, imm2):
    y = in0 * in0
    return 2.0 * ((y > s0) * 1.0 + (y > s1) + 2.0 * (y > imm2))


_y = sq(Src0)
FP4A = _register_op(
    "ANT_FP4A", Spec(body=(_y > C0) + (_y > C1) + ((_y > C2) + (_y > C3))), _fp4a_ref
)
_c7 = _y > C2
_inner = ((_y > C0) + (_y > C1)) + (_c7 + _c7)
FP4B = _register_op("ANT_FP4B", Spec(body=_inner + _inner), _fp4b_ref)


def _build_consts():
    k = np.arange(513, dtype=np.float64)
    n = np.arange(1024, dtype=np.float64)
    ang = 2.0 * np.pi * np.outer(k, n) / 1024.0
    Acp = np.zeros((640, 1024), dtype=np.float64)
    Asp = np.zeros((640, 1024), dtype=np.float64)
    Acp[:513] = np.cos(ang)
    Asp[:513] = np.sin(ang)
    ac_ch = np.ascontiguousarray(Acp.reshape(5, 128, 1024).transpose(1, 0, 2)).astype(np.float32)
    as_ch = np.ascontiguousarray(Asp.reshape(5, 128, 1024).transpose(1, 0, 2)).astype(np.float32)
    Ac_hi = ac_ch.astype(BF)
    As_hi = as_ch.astype(BF)
    AcT = np.ascontiguousarray(Acp.T.reshape(8, 128, 640).transpose(1, 0, 2)).astype(np.float32)
    AsT = np.ascontiguousarray(Asp.T.reshape(8, 128, 640).transpose(1, 0, 2)).astype(np.float32)
    AcT_hi = AcT.astype(BF)
    AsT_hi = AsT.astype(BF)
    AcT_lo = (AcT - AcT_hi.astype(np.float32)).astype(BF)
    AsT_lo = (AsT - AsT_hi.astype(np.float32)).astype(BF)
    cw = np.zeros(640, dtype=np.float64)
    cw[:513] = 2.0 / 1024.0
    cw[0] = 1.0 / 1024.0
    cw[512] = 1.0 / 1024.0
    cconst = np.ascontiguousarray(cw.reshape(5, 128).T.astype(np.float32))
    kk = np.fft.fftfreq(1024).astype(np.float32)[:513].copy()
    kk[512] = np.float32(-0.5)
    ka = (np.abs(kk) + np.float32(1e-8)).astype(np.float32)
    ka2 = (ka * ka).astype(np.float32)
    kap = np.zeros(640, dtype=np.float32)
    ka2p = np.zeros(640, dtype=np.float32)
    kap[:513] = ka
    ka2p[:513] = ka2
    ka_ch = np.ascontiguousarray(kap.reshape(5, 128).T)
    ka2_ch = np.ascontiguousarray(ka2p.reshape(5, 128).T)
    lx = np.log(np.array([2.0, 4.0, 8.0, 16.0, 32.0], dtype=np.float32)).astype(np.float64)
    den = 5.0 * np.sum(lx * lx) - np.sum(lx) ** 2
    acoef = np.ascontiguousarray((-(5.0 * lx - np.sum(lx)) / den).astype(np.float32)[None, :])
    ident = np.eye(128, dtype=np.float32).astype(BF)
    _A = np.array(
        [[1,1,1,0,1,1,1,0,0,0,1,1],[1,1,0,1,1,1,0,0,0,1,1,0],[1,0,1,1,1,0,0,0,1,1,0,1],
         [0,1,1,1,0,0,0,1,1,0,1,1],[0,0,0,0,1,1,1,1,1,1,1,1],[1,1,1,1,0,0,0,0,0,0,0,0],
         [0,0,0,0,1,1,1,1,0,0,0,0],[1,1,0,0,1,1,0,0,1,1,0,0],[1,0,1,0,1,0,1,0,1,0,1,0],
         [0,1,0,1,0,1,0,1,0,1,0,1],[1,0,0,1,1,0,0,1,1,0,0,1],[0,1,1,0,0,1,1,0,0,1,1,0]],
        dtype=np.float32)
    H = np.concatenate([_A.T, np.eye(12, dtype=np.float32)], axis=1)
    HT = np.ascontiguousarray(H.T).astype(BF)
    Ac_lo = (ac_ch - Ac_hi.astype(np.float32)).astype(BF)
    As_lo = (as_ch - As_hi.astype(np.float32)).astype(BF)
    return dict(cAchi=Ac_hi, cAshi=As_hi, cAclo=Ac_lo, cAslo=As_lo,
                cAcThi=AcT_hi, cAcTlo=AcT_lo, cAsThi=AsT_hi, cAsTlo=AsT_lo,
                ccw=cconst, cka=ka_ch, cka2=ka2_ch, cacoef=acoef,
                cident=ident, cHT=HT)


_CONSTS = _build_consts()
_NC_CACHE = {}


def _v3(ap, n=16):
    return ap.rearrange("p (s n) -> p s n", n=n)


def _b16(small_ap, S):
    return small_ap.rearrange("p (s o) -> p s o", o=1).broadcast_to([128, S, 16])


def build_program():
    nc = bass.Bass("TRN2", target_bir_lowering=False, debug=False)

    xb = nc.dram_tensor("xb", [SEQ, E], F32, kind="ExternalInput").ap()
    xfull = nc.dram_tensor("xfull", [ROWS, E], F32, kind="ExternalInput").ap()
    wg = nc.dram_tensor("wg", [I_FFN, E], F32, kind="ExternalInput").ap()
    wu = nc.dram_tensor("wu", [I_FFN, E], F32, kind="ExternalInput").ap()
    wd = nc.dram_tensor("wd", [E, I_FFN], F32, kind="ExternalInput").ap()
    wel = nc.dram_tensor("wel", [E, 24], F32, kind="ExternalInput").ap()
    bel = nc.dram_tensor("bel", [1, 24], F32, kind="ExternalInput").ap()
    wle = nc.dram_tensor("wle", [24, E], F32, kind="ExternalInput").ap()
    ble = nc.dram_tensor("ble", [1, E], F32, kind="ExternalInput").ap()
    yout = nc.dram_tensor("yout", [SEQ, E], F32, kind="ExternalOutput").ap()
    ci = {}
    for name, arr in _CONSTS.items():
        dt = BF16 if arr.dtype == BF else F32
        ci[name] = nc.dram_tensor(name, list(arr.shape), dt, kind="ExternalInput").ap()

    xqT_d = nc.dram_tensor("xqT_d", [TPC, 128, 8, 128], BF16, kind="Internal").ap()
    xhT_d = nc.dram_tensor("xhT_d", [TPC, 128, 8, 128], BF16, kind="Internal").ap()
    xlT_d = nc.dram_tensor("xlT_d", [TPC, 128, 8, 128], BF16, kind="Internal").ap()
    g_d = nc.dram_tensor("g_d", [TPC, 128, E], F32, kind="Internal").ap()
    vec_d = nc.dram_tensor("vec_d", [4, 128], F32, kind="Internal").ap()

    from contextlib import ExitStack
    es = ExitStack()
    with es:
        WGb = es.enter_context(nc.sbuf_tensor("WGb", [128, 8, I_FFN], BF16))
        WUb = es.enter_context(nc.sbuf_tensor("WUb", [128, 8, I_FFN], BF16))
        WDb = es.enter_context(nc.sbuf_tensor("WDb", [128, 22, E], BF16))
        Chi = es.enter_context(nc.sbuf_tensor("Chi", [128, 8, E], BF16))
        Mhi = es.enter_context(nc.sbuf_tensor("Mhi", [128, 8, 24], BF16))
        Mlo = es.enter_context(nc.sbuf_tensor("Mlo", [128, 8, 24], BF16))
        AF_ = es.enter_context(nc.sbuf_tensor("AF_", [128, 6144], F32))
        ABF = es.enter_context(nc.sbuf_tensor("ABF", [128, 13824], BF16))
        ident = es.enter_context(nc.sbuf_tensor("ident", [128, 128], BF16))
        HTs = es.enter_context(nc.sbuf_tensor("HTs", [24, 12], BF16))
        negb = es.enter_context(nc.sbuf_tensor("negb", [128, 64], F32))
        c3buf = es.enter_context(nc.sbuf_tensor("c3buf", [128, 1], F32))
        sc1 = es.enter_context(nc.sbuf_tensor("sc1", [128, 4], F32))
        cstb = es.enter_context(nc.sbuf_tensor("cstb", [128, 8], F32))
        dsm = es.enter_context(nc.sbuf_tensor("dsm", [1, 16], F32))
        dly = es.enter_context(nc.sbuf_tensor("dly", [1, 8], F32))
        dly2 = es.enter_context(nc.sbuf_tensor("dly2", [1, 8], F32))
        dcnt = es.enter_context(nc.sbuf_tensor("dcnt", [1, 8], F32))
        dll = es.enter_context(nc.sbuf_tensor("dll", [1, 8], F32))
        dslp = es.enter_context(nc.sbuf_tensor("dslp", [1, 8], F32))
        acf = es.enter_context(nc.sbuf_tensor("acf", [1, 5], F32))
        Db = es.enter_context(nc.sbuf_tensor("Db", [128, 1], F32))
        alf0 = es.enter_context(nc.sbuf_tensor("alf0", [128, 1], F32))
        gtm = es.enter_context(nc.sbuf_tensor("gtm", [128, 1], F32))
        alf = es.enter_context(nc.sbuf_tensor("alf", [128, 1], F32))
        bet = es.enter_context(nc.sbuf_tensor("bet", [128, 1], F32))
        gam = es.enter_context(nc.sbuf_tensor("gam", [128, 1], F32))
        kas = es.enter_context(nc.sbuf_tensor("kas", [128, 5], F32))
        ka2s = es.enter_context(nc.sbuf_tensor("ka2s", [128, 5], F32))
        cws = es.enter_context(nc.sbuf_tensor("cws", [128, 5], F32))
        tmp5 = es.enter_context(nc.sbuf_tensor("tmp5", [128, 5], F32))
        t5c = es.enter_context(nc.sbuf_tensor("t5c", [128, 5], F32))
        ub5 = es.enter_context(nc.sbuf_tensor("ub5", [128, 5], F32))
        fr1 = es.enter_context(nc.sbuf_tensor("fr1", [128, 5], F32))
        fr2 = es.enter_context(nc.sbuf_tensor("fr2", [128, 5], F32))
        cfr = es.enter_context(nc.sbuf_tensor("cfr", [128, 5], F32))
        WelS = es.enter_context(nc.sbuf_tensor("WelS", [128, 8, 24], F32))
        WelH = es.enter_context(nc.sbuf_tensor("WelH", [128, 8, 24], BF16))
        WelL = es.enter_context(nc.sbuf_tensor("WelL", [128, 8, 24], BF16))
        AWc = es.enter_context(nc.sbuf_tensor("AWc", [128, 5, 64], F32))
        AWs = es.enter_context(nc.sbuf_tensor("AWs", [128, 5, 64], F32))
        Rh = es.enter_context(nc.sbuf_tensor("Rh", [128, 5, 32], BF16))
        Rl = es.enter_context(nc.sbuf_tensor("Rl", [128, 5, 32], BF16))
        Mf = es.enter_context(nc.sbuf_tensor("Mf", [128, 8, 64], F32))
        bits = es.enter_context(nc.sbuf_tensor("bits", [128, 24], BF16))
        corr = es.enter_context(nc.sbuf_tensor("corr", [128, 24], BF16))
        pg0 = es.enter_context(nc.psum_tensor("pg0", [128, 512], F32))
        pg1 = es.enter_context(nc.psum_tensor("pg1", [128, 512], F32))
        py0 = es.enter_context(nc.psum_tensor("py0", [128, 512], F32))
        py1 = es.enter_context(nc.psum_tensor("py1", [128, 512], F32))
        pdn = es.enter_context(nc.psum_tensor("pdn", [128, 1024], F32))
        pT = es.enter_context(nc.psum_tensor("pT", [128, 1024], BF16))
        sLd = es.enter_context(nc.semaphore("sLd"))
        sL2 = es.enter_context(nc.semaphore("sL2"))
        sSt = es.enter_context(nc.semaphore("sSt"))
        sV = es.enter_context(nc.semaphore("sV"))
        sA = es.enter_context(nc.semaphore("sA"))
        sT = es.enter_context(nc.semaphore("sT"))
        sSy = es.enter_context(nc.semaphore("sSy"))
        sG = es.enter_context(nc.semaphore("sG"))
        block = es.enter_context(nc.Block())
        XT0 = AF_[:, 0:1024]
        XT1 = AF_[:, 1024:2048]
        QW = AF_[:, 2048:3072]
        SIL = AF_[:, 3072:3584]
        HB = AF_[:, 3584:4096]
        SMVS = AF_[:, 4096:4352]
        SMVQ = AF_[:, 4352:4608]
        SMN = AF_[:, 4608:4864]
        SMX = AF_[:, 4864:5120]
        XNF = AF_[:, 5120:6144]
        BLEB = AF_[:, 5120:6144]
        SSB = AF_[:, 4096:5120]
        OSB = AF_[:, 3072:4096]
        NEGM = AF_[:, 2048:2304]
        MSQ = AF_[:, 2304:2560]
        VARE = AF_[:, 2560:2816]
        SQV = AF_[:, 2816:3072]
        R0G = AF_[:, 3072:3328]
        RRG = AF_[:, 3328:3584]
        BVG = AF_[:, 3584:3840]
        RSTDG = AF_[:, 3840:4096]
        OWNSUM = AF_[:, 5120:5184]
        OWNSQ = AF_[:, 5184:5248]
        NEGM64 = AF_[:, 3072:3136]
        RSTD64 = AF_[:, 3136:3200]
        MSQ64 = AF_[:, 5248:5312]
        VARE64 = AF_[:, 5312:5376]
        SQ64 = AF_[:, 5376:5440]
        R064 = AF_[:, 5440:5504]
        RR64 = AF_[:, 5504:5568]
        B64 = AF_[:, 5568:5632]
        MfC = AF_[:, 2048:2240].rearrange("p (c n) -> p c n", n=24)
        Rf = AF_[:, 2048:2208].rearrange("p (c n) -> p c n", n=32)
        qm = AF_[:, 4096:4160]
        qsb = AF_[:, 4160:4224]
        qrb = AF_[:, 4224:4288]
        qs2 = AF_[:, 4288:4352]
        XROW = AF_[0:1, 0:1024]
        XNROW = AF_[0:1, 1024:2048]
        ZROW = AF_[0:1, 3072:4096]
        MROW = AF_[0:1, 0:1024]
        MNR = AF_[0:1, 2048:2176]
        MXR = AF_[0:1, 2176:2304]

        S1 = ABF[:, 0:5120].rearrange("p (c n) -> p c n", n=1024)
        S2 = ABF[:, 5120:10240].rearrange("p (c n) -> p c n", n=1024)
        S1f = ABF[:, 0:5120]
        S2f = ABF[:, 5120:10240]
        AcT_hi_v = S1f.rearrange("p (c n) -> p c n", n=640)
        AcT_lo_v = S2f.rearrange("p (c n) -> p c n", n=640)
        QGA = ABF[:, 0:1024]
        QGB = ABF[:, 1024:2048]
        QSG = ABF[:, 2048:3072]
        QOUT = ABF[:, 3072:5888]
        EV1 = ABF[:, 5888:6912]
        EV2 = ABF[:, 6912:7936]
        TSTG = ABF[:, 7936:8960]
        GSTGF = AF_[:, 5120:6144]
        HQT = ABF[:, 8960:11776].rearrange("p (c n) -> p c n", n=128)
        XQT0 = ABF[:, 11776:12800].rearrange("p (c n) -> p c n", n=128)
        XQT1 = ABF[:, 12800:13824].rearrange("p (c n) -> p c n", n=128)
        XHT = ABF[:, 5888:6912].rearrange("p (c n) -> p c n", n=128)
        XLT = ABF[:, 6912:7936].rearrange("p (c n) -> p c n", n=128)
        GTF = AF_[:, 2048:3072]
        GW = AF_[:, 1024:1088]
        GU = AF_[:, 1088:1152]
        GA = AF_[:, 1152:1216]
        GB = AF_[:, 1024:1088]
        GF0 = AF_[:, 1088:1152]
        GF = AF_[:, 1152:1216]
        BITSF = AF_[:, 1216:1280]
        GD = AF_[:, 1024:1088]
        GE = AF_[:, 1088:1152]
        CORRF = AF_[:, 1152:1216]
        synpm = AF_[:, 1280:1344]
        maskb = AF_[:, 1344:1408]
        WleB = ABF[0:24, 8960:9984]
        bitsT = ABF[0:24, 9984:10112]
        corrT = ABF[0:24, 10112:10240]
        NB05 = cstb[:, 0:1]
        NB5 = cstb[:, 1:2]
        HPIN = cstb[:, 2:3]
        HPIP = cstb[:, 3:4]
        oneb = cstb[:, 4:5]

        pAW = pg0[:, 0:24]
        pAW64 = pg0[:, 0:64]
        latP = pg0[:, 0:24]
        synP = py0[:, 0:12]
        psA = pT[0:24, 0:128]
        psB = pT[0:24, 128:256]

        Q = {"sy": [], "v": [], "sa": [], "te": [], "g": []}
        Cn = {"ld": 0, "ld2": 0, "st": 0, "v": 0, "a": 0, "t": 0, "sy": 0, "g": 0}
        SEM = {"ld": sLd, "ld2": sL2, "st": sSt, "v": sV, "a": sA, "t": sT,
               "sy": sSy, "g": sG}

        def em(eng, fn):
            Q[eng].append(fn)

        def dma_in(dst, src):
            Cn["ld"] += 16
            tgt = Cn["ld"]
            em("sy", lambda e, d=dst, s=src: e.dma_start(out=d, in_=s).then_inc(sLd, 16))
            return tgt

        def dma_in2(dst, src):
            Cn["ld2"] += 16
            tgt = Cn["ld2"]
            em("g", lambda e, d=dst, s=src: e.dma_start(out=d, in_=s).then_inc(sL2, 16))
            return tgt

        def wait(eng, sem_key, val):
            em(eng, lambda e, k=sem_key, vv=val: e.wait_ge(SEM[k], vv))

        def op(eng, fn, inc_key=None):
            if inc_key is None:
                em(eng, fn)
                return None
            Cn[inc_key] += 1
            tgt = Cn[inc_key]

            def _f(e, f=fn, k=inc_key):
                f(e).then_inc(SEM[k], 1)

            em(eng, _f)
            return tgt

        def dma_out(eng_key, dst, src):
            Cn["st"] += 16
            tgt = Cn["st"]
            em(eng_key, lambda e, d=dst, s=src: e.dma_start(out=d, in_=s).then_inc(sSt, 16))
            return tgt

        def barrier():
            dp = {
                "sy": lambda e: e.nop(),
                "v": lambda e: e.engine_nop(),
                "sa": lambda e: e.copy(cstb[:, 7:8], c3buf[:]),
                "te": lambda e: e.ldweights(ident[:]),
                "g": lambda e: e.engine_nop(),
            }
            marks = {}
            for ek, sk in (("sy", "sy"), ("v", "v"), ("sa", "a"), ("te", "t"), ("g", "g")):
                Cn[sk] += 1
                marks[ek] = Cn[sk]
                em(ek, lambda e, k=sk, f=dp[ek]: f(e).then_inc(SEM[k], 1))
            for ek in ("sy", "v", "sa", "te", "g"):
                for ok, sk in (("sy", "sy"), ("v", "v"), ("sa", "a"), ("te", "t"), ("g", "g")):
                    if ok != ek:
                        wait(ek, sk, marks[ok])

        def emit_quant(src, dst, W):
            S = W // 16

            def _q(v, s=src, d=dst, W=W, S=S):
                v.tensor_reduce(qm[:, 0:S], _v3(s), AX, A.max, apply_absolute_value=True)
                v.tensor_copy(sc1[:, 3:4], c3buf[:])
                v.tensor_scalar(qsb[:, 0:64], qm[:, 0:64], SIXTH, 1e-12, A.mult, A.max)
                v.reciprocal(qrb[:, 0:64], qsb[:, 0:64])
                v.tensor_scalar(qs2[:, 0:64], qsb[:, 0:64], 0.5, None, A.mult)
                v.tensor_tensor(_v3(QW[:, :W]), _v3(s), _b16(qrb[:, 0:S], S), A.mult)
                v._custom_dve(FP4A, out=QGA[:, :W], in0=QW[:, :W], in1=c3buf[:],
                              s0=0.0625, s1=0.5625, imm2=1.5625)
                v._custom_dve(FP4B, out=QGB[:, :W], in0=QW[:, :W],
                              s0=6.25, s1=12.25, imm2=25.0)
                v.tensor_tensor(QGA[:, :W], QGA[:, :W], QGB[:, :W], A.add)
                v.tensor_tensor(QGA[:, :W], QGA[:, :W], QSG[:, :W], A.mult)
                return v.tensor_tensor(_v3(d), _v3(QGA[:, :W]), _b16(qs2[:, 0:S], S), A.mult)

            return _q

        for dst, key in ((ident[:], "cident"), (HTs[:], "cHT"), (acf[:], "cacoef"),
                         (kas[:], "cka"), (ka2s[:], "cka2"), (cws[:], "ccw")):
            dma_in(dst, ci[key])
        dma_in(WelS[:], wel.rearrange("(c p) e -> p c e", p=128))
        t_bel = dma_in(negb[:, 0:24], bel.partition_broadcast(128))
        t_act1 = dma_in(AcT_hi_v, ci["cAcThi"])
        t_act2 = dma_in(AcT_lo_v, ci["cAcTlo"])
        consts_ld = Cn["ld"]

        wait("v", "ld", consts_ld)
        op("v", lambda v: v.memset(c3buf[:], 3.0625))
        wait("g", "ld", t_bel)

        def _gsetup(g):
            g.memset(negb[:, 24:64], 0.0)
            g.tensor_scalar(negb[:, 0:24], negb[:, 0:24], -1.0, None, A.mult)
            g.memset(cstb[:, 0:1], -0.5)
            g.memset(cstb[:, 1:2], -5.0)
            g.memset(cstb[:, 2:3], -HALFPI)
            g.memset(cstb[:, 4:5], 1.0)
            return g.memset(cstb[:, 3:4], HALFPI)

        g_setup = op("g", _gsetup, "g")
        op("v", lambda v: v.tensor_copy(WelH[:], WelS[:]))
        v_wel = op("v", lambda v: v.scalar_tensor_tensor(
            WelL[:].rearrange("p c n -> p (c n)"),
            WelH[:].rearrange("p c n -> p (c n)"), -1.0,
            WelS[:].rearrange("p c n -> p (c n)"), A.mult, A.add), "v")

        wait("te", "v", v_wel)
        wait("te", "ld", t_act2)
        aw_last_t = None
        for nm, AW in (("c", AWc), ("s", AWs)):
            if nm == "s":
                wait("sy", "t", aw_last_t)
                t_ast1 = dma_in(AcT_hi_v, ci["cAsThi"])
                t_ast2 = dma_in(AcT_lo_v, ci["cAsTlo"])
                wait("te", "ld", t_ast2)
            for kc in range(5):
                ksl = slice(kc * 128, (kc + 1) * 128)
                mms = []
                for cc in range(8):
                    mms.append((AcT_hi_v[:, cc, ksl], WelH[:, cc, :]))
                    mms.append((AcT_hi_v[:, cc, ksl], WelL[:, cc, :]))
                    mms.append((AcT_lo_v[:, cc, ksl], WelH[:, cc, :]))
                t_mark = None
                for i, (lh, rh) in enumerate(mms):
                    def _mm(te, lh=lh, rh=rh, st=(i == 0), sp=(i == len(mms) - 1)):
                        return te.matmul(pAW, lh, rh, start=st, stop=sp)
                    if i < len(mms) - 1:
                        op("te", _mm)
                    else:
                        t_mark = op("te", _mm, "t")
                aw_last_t = t_mark
                wait("sa", "t", t_mark)
                a_mark = op("sa", lambda s, d=AW[:, kc, :]: s.copy(d, pAW64), "a")
                wait("te", "a", a_mark)

        op("v", lambda v: v.memset(OWNSUM[:, 0:32], 1.0))
        op("v", lambda v: v.memset(OWNSQ[:, 0:32], 1.0))
        v_stats_base = Cn["v"]
        a_stats_base = Cn["a"]
        NT_ALL = STILES + TPC
        for t in range(NT_ALL):
            buf = XT0 if t % 2 == 0 else XT1
            qk = "sy" if t % 2 == 0 else "g"
            lsem = "ld" if t % 2 == 0 else "ld2"
            if t >= 2:
                wait(qk, "v", v_stats_base + (t - 1))
                wait(qk, "a", a_stats_base + (t - 1))
            if t < STILES:
                srcap = xfull[t * 128:(t + 1) * 128, :]
            else:
                tt = t - STILES
                srcap = xb[tt * 128:(tt + 1) * 128, :]
            t_ld = dma_in(buf, srcap) if t % 2 == 0 else dma_in2(buf, srcap)
            wait("sa", lsem, t_ld)
            if t < STILES:
                def _sq(s, b=buf, t=t):
                    s.activation(QW, b, AF.Square, accum_out=SMVQ[:, t:t + 1])
                    return s.activation(QW, b, AF.Square, bias=oneb,
                                        accum_out=SMVS[:, t:t + 1])
            else:
                def _sq(s, b=buf, tt=t - STILES):
                    s.activation(QW, b, AF.Square, accum_out=OWNSQ[:, tt + 32:tt + 33])
                    return s.activation(QW, b, AF.Square, bias=oneb,
                                        accum_out=OWNSUM[:, tt + 32:tt + 33])
            op("sa", _sq, "a")
            wait("v", lsem, t_ld)

            if t < STILES:
                def _st(v, b=buf, t=t):
                    v.tensor_reduce(SMN[:, t:t + 1], b, AX, A.min)
                    v.reduce_max(SMX[:, t:t + 1], b, AX)
                    return v.tensor_copy(sc1[:, 3:4], c3buf[:])

                op("v", _st, "v")
            else:
                Cn["v"] += 1
                em("v", lambda e: e.engine_nop().then_inc(sV, 1))

        def _wp1(v):
            v.tensor_tensor(NEGM, SMVS, SMVQ, A.subtract)
            v.tensor_scalar(NEGM, NEGM, -1024.0, -1.0 / 2048.0, A.add, A.mult)
            return v.tensor_copy(sc1[:, 3:4], c3buf[:])

        v_wp1 = op("v", _wp1, "v")
        wait("sa", "v", v_wp1)
        a_wp1 = op("sa", lambda s: s.activation(MSQ, NEGM, AF.Square), "a")
        wait("v", "a", a_wp1)

        def _wp2(v):
            v.scalar_tensor_tensor(VARE, SMVQ, 1.0 / 1024.0, MSQ,
                                   A.mult, A.subtract)
            return v.tensor_scalar(VARE, VARE, 1e-5, None, A.add)

        v_wp2 = op("v", _wp2, "v")
        wait("sa", "v", v_wp2)
        a_wp2 = op("sa", lambda s: s.activation(SQV, VARE, AF.Sqrt), "a")
        wait("v", "a", a_wp2)
        v_wp3 = op("v", lambda v: v.reciprocal(R0G, SQV), "v")
        wait("sa", "v", v_wp3)
        a_wp3 = op("sa", lambda s: s.activation(RRG, R0G, AF.Square), "a")
        wait("v", "a", a_wp3)

        def _wp4(v):
            v.tensor_tensor(BVG, RRG, VARE, A.mult)
            v.tensor_scalar(BVG, BVG, -0.5, 1.5, A.mult, A.add)
            v.tensor_tensor(RSTDG, R0G, BVG, A.mult)
            v.tensor_tensor(SMN, SMN, NEGM, A.add)
            v.tensor_tensor(SMN, SMN, RSTDG, A.mult)
            v.tensor_tensor(SMX, SMX, NEGM, A.add)
            v.tensor_tensor(SMX, SMX, RSTDG, A.mult)
            v.tensor_reduce(sc1[:, 0:1], SMN, AX, A.min)
            v.reduce_max(sc1[:, 1:2], SMX, AX)
            return v.tensor_copy(sc1[:, 3:4], c3buf[:])

        v_wp = op("v", _wp4, "v")

        wait("g", "v", v_wp)
        wait("g", "g", g_setup)

        def _dg1(g):
            g.tensor_copy(dsm[0:1, 3:4], NEGM[0:1, 0:1])
            g.tensor_copy(dsm[0:1, 4:5], RSTDG[0:1, 0:1])
            return g.tensor_tensor(dsm[0:1, 7:8], dsm[0:1, 3:4], dsm[0:1, 4:5], A.mult)

        g_d1 = op("g", _dg1, "g")
        dma_out("g", vec_d[0, :], sc1[:, 0:1])
        st_b = dma_out("g", vec_d[1, :], sc1[:, 1:2])
        wait("sy", "st", st_b)
        dma_in(MNR[:, 0:128], vec_d[0:1, :])
        dma_in(MXR[:, 0:128], vec_d[1:2, :])
        t_x0 = dma_in(XROW, xfull[0:1, :])
        wait("v", "ld", t_x0)

        def _dv1(v):
            v.tensor_reduce(dsm[0:1, 0:1], MNR[:, 0:128], AX, A.min)
            v.reduce_max(dsm[0:1, 1:2], MXR[:, 0:128], AX)
            return v.tensor_copy(sc1[:, 3:4], c3buf[:])

        v_d1 = op("v", _dv1, "v")
        wait("g", "v", v_d1)

        def _dg2(g):
            g.tensor_tensor(dsm[0:1, 5:6], dsm[0:1, 1:2], dsm[0:1, 0:1], A.subtract)
            return g.tensor_scalar(dsm[0:1, 5:6], dsm[0:1, 5:6], 1e-10, None, A.add)

        g_d2 = op("g", _dg2, "g")
        wait("v", "g", g_d2)
        v_d2 = op("v", lambda v: v.reciprocal(dsm[0:1, 2:3], dsm[0:1, 5:6]), "v")
        wait("g", "v", v_d2)

        def _dg3(g):
            g.tensor_tensor(dsm[0:1, 6:7], dsm[0:1, 0:1], dsm[0:1, 2:3], A.mult)
            return g.tensor_scalar(dsm[0:1, 6:7], dsm[0:1, 6:7], -1.0, None, A.mult)

        g_d3 = op("g", _dg3, "g")
        wait("sa", "g", g_d3)

        def _da1(s):
            s.activation(XNROW, XROW, AF.Identity,
                         bias=dsm[0:1, 7:8], scale=dsm[0:1, 4:5])
            return s.activation(ZROW, XNROW, AF.Identity,
                                bias=dsm[0:1, 6:7], scale=dsm[0:1, 2:3])

        a_d1 = op("sa", _da1, "a")
        wait("v", "a", a_d1)
        NNJ = (4, 16, 64, 256, 1024)

        def _dv3(v):
            for j, nn in enumerate(NNJ):
                v.tensor_reduce(dly[0:1, j:j + 1], ZROW[:, 0:nn], AX, A.add)
            return v.tensor_copy(sc1[:, 3:4], c3buf[:])

        v_d3 = op("v", _dv3, "v")
        wait("g", "v", v_d3)

        def _dg4(g):
            last = None
            for j, nn in enumerate(NNJ):
                last = g.tensor_scalar(dly2[0:1, j:j + 1], dly[0:1, j:j + 1],
                                       1.0 / nn, None, A.mult)
            return last

        g_d4 = op("g", _dg4, "g")
        wait("v", "g", g_d4)

        def _dv4(v):
            for j, nn in enumerate(NNJ):
                v.tensor_scalar(MROW, ZROW, dly2[0:1, j:j + 1], None, A.is_gt)
                v.tensor_reduce(dcnt[0:1, j:j + 1], MROW[:, 0:nn], AX, A.add)
            return v.tensor_copy(sc1[:, 3:4], c3buf[:])

        v_d4 = op("v", _dv4, "v")
        wait("g", "v", v_d4)
        g_d5 = op("g", lambda g: g.tensor_scalar(dcnt[0:1, 0:5], dcnt[0:1, 0:5],
                                                 1.0, None, A.max), "g")
        wait("sa", "g", g_d5)
        a_d2 = op("sa", lambda s: s.activation(dll[0:1, 0:5], dcnt[0:1, 0:5], AF.Ln), "a")
        wait("g", "a", a_d2)
        g_d6 = op("g", lambda g: g.tensor_tensor(dslp[0:1, 0:5], dll[0:1, 0:5],
                                                 acf[0:1, :], A.mult), "g")
        wait("v", "g", g_d6)

        def _dv5(v):
            v.tensor_reduce(dsm[0:1, 8:9], dslp[0:1, 0:5], AX, A.add)
            return v.tensor_copy(sc1[:, 3:4], c3buf[:])

        v_d5 = op("v", _dv5, "v")
        wait("g", "v", v_d5)
        g_d7 = op("g", lambda g: g.tensor_scalar(dsm[0:1, 9:10], dsm[0:1, 8:9],
                                                 1.0, 2.0, A.max, A.min), "g")
        wait("g", "g", g_d7)
        st_d = dma_out("g", vec_d[2, 0:1], dsm[0:1, 9:10])
        wait("sy", "st", st_d)
        t_db = dma_in(Db[:], vec_d[2:3, 0:1].partition_broadcast(128))

        wait("sa", "ld", t_db)
        a_f0 = op("sa", lambda s: s.activation(alf0[:], Db[:], AF.Square), "a")
        wait("g", "a", a_f0)

        def _fg1(g):
            g.tensor_scalar(gtm[:], alf0[:], 0.2, None, A.mult)
            g.tensor_tensor(alf[:], gtm[:], Db[:], A.add)
            g.tensor_scalar(bet[:], Db[:], 0.5, 1.0, A.mult, A.add)
            return g.tensor_scalar(gam[:], Db[:], 0.1, None, A.mult)

        g_f1 = op("g", _fg1, "g")
        wait("sa", "g", g_f1)

        def _fa1(s):
            s.activation(tmp5[:], kas[:], AF.Identity, bias=alf[:], scale=bet[:])
            s.activation(ub5[:], tmp5[:], AF.Abs, bias=HPIN)
            s.activation(fr1[:], ub5[:], AF.Sin, bias=HPIP, scale=-1.0)
            return s.activation(t5c[:], ka2s[:], AF.Identity, bias=alf[:], scale=gam[:])

        a_f1 = op("sa", _fa1, "a")
        wait("g", "a", a_f1)
        g_f2 = op("g", lambda g: g.tensor_tensor(t5c[:], t5c[:], kas[:], A.subtract), "g")
        wait("sa", "g", g_f2)
        a_f2 = op("sa", lambda s: s.activation(fr2[:], t5c[:], AF.Sin,
                                               bias=HPIP, scale=-1.0), "a")
        wait("g", "a", a_f2)

        def _fg3(g):
            g.tensor_tensor(cfr[:], fr1[:], fr2[:], A.mult)
            return g.tensor_tensor(cfr[:], cfr[:], cws[:], A.mult)

        g_cfr = op("g", _fg3, "g")

        wait("sy", "t", aw_last_t)
        t_m2 = dma_in(S1, ci["cAchi"])
        t_m2b = dma_in(S2, ci["cAclo"])
        m_last_t = None
        m_last_v = None
        for nm in ("c", "s"):
            AW = AWc if nm == "c" else AWs
            if nm == "s":
                wait("sy", "t", m_last_t)
                t_m2 = dma_in(S1, ci["cAshi"])
                t_m2b = dma_in(S2, ci["cAslo"])
            cfrb32 = cfr[:].rearrange("p (c o) -> p c o", o=1).broadcast_to([128, 5, 32])

            def _rbuild(v, AW=AW, cfrb32=cfrb32):
                return v.tensor_tensor(Rf, AW[:, :, 0:32], cfrb32, A.mult)

            if nm == "c":
                wait("v", "g", g_cfr)
            v_r = op("v", _rbuild, "v")
            op("v", lambda v: v.tensor_copy(Rh[:], Rf))
            v_rl = op("v", lambda v: v.scalar_tensor_tensor(
                Rl[:].rearrange("p c n -> p (c n)"),
                Rh[:].rearrange("p c n -> p (c n)"), -1.0,
                Rf.rearrange("p c n -> p (c n)"), A.mult, A.add), "v")
            wait("te", "v", v_rl)
            wait("te", "ld", t_m2b)
            for jc in range(8):
                jsl = slice(jc * 128, (jc + 1) * 128)
                mms = []
                for kc in range(5):
                    mms.append((S1[:, kc, jsl], Rh[:, kc, 0:24]))
                    mms.append((S1[:, kc, jsl], Rl[:, kc, 0:24]))
                    mms.append((S2[:, kc, jsl], Rh[:, kc, 0:24]))
                t_mk = None
                for i, (lh, rh) in enumerate(mms):
                    def _mm(te, lh=lh, rh=rh, st=(i == 0), sp=(i == len(mms) - 1)):
                        return te.matmul(pAW, lh, rh, start=st, stop=sp)
                    if i < len(mms) - 1:
                        op("te", _mm)
                    else:
                        t_mk = op("te", _mm, "t")
                m_last_t = t_mk
                wait("v", "t", t_mk)
                if nm == "c":
                    m_last_v = op("v", lambda v, jc=jc: v.tensor_copy(Mf[:, jc, :], pAW64), "v")
                else:
                    m_last_v = op("v", lambda v, jc=jc: v.tensor_tensor(
                        Mf[:, jc, :], Mf[:, jc, :], pAW64, A.add), "v")
                wait("te", "v", m_last_v)

        def _msplit(v):
            v.tensor_copy(MfC, Mf[:, :, 0:24])
            v.tensor_copy(Mhi[:], MfC)
            return v.scalar_tensor_tensor(
                Mlo[:].rearrange("p c n -> p (c n)"),
                Mhi[:].rearrange("p c n -> p (c n)"), -1.0,
                MfC.rearrange("p c n -> p (c n)"), A.mult, A.add)

        v_msp = op("v", _msplit, "v")

        c_last_v = v_msp
        for nm in ("c", "s"):
            wait("sy", "t", m_last_t)
            wait("sy", "v", c_last_v)
            t_c1 = dma_in(S1, ci["cAchi"] if nm == "c" else ci["cAshi"])
            t_c2 = dma_in(S2, ci["cAchi"] if nm == "c" else ci["cAshi"])
            wait("v", "ld", t_c2)
            cfrb1k = cfr[:].rearrange("p (c o) -> p c o", o=1).broadcast_to([128, 5, 1024])

            def _pbuild(v, cfrb1k=cfrb1k):
                return v.tensor_tensor(S2, S2, cfrb1k, A.mult)

            v_p = op("v", _pbuild, "v")
            wait("te", "v", v_p)
            wait("te", "ld", t_c1)
            for jc in range(8):
                jsl = slice(jc * 128, (jc + 1) * 128)
                t_ck = None
                for half in range(2):
                    hsl = slice(half * 512, (half + 1) * 512)
                    for kc in range(5):
                        def _mm(te, kc=kc, jsl=jsl, hsl=hsl, st=(kc == 0), sp=(kc == 4)):
                            return te.matmul(pdn[:, hsl], S1[:, kc, jsl],
                                             S2[:, kc, hsl], start=st, stop=sp)
                        if kc < 4 or half == 0:
                            op("te", _mm)
                        else:
                            t_ck = op("te", _mm, "t")
                wait("v", "t", t_ck)
                if nm == "c":
                    c_last_v = op("v", lambda v, jc=jc: v.tensor_copy(Chi[:, jc, :], pdn[:]), "v")
                else:
                    c_last_v = op("v", lambda v, jc=jc: v.tensor_add(Chi[:, jc, :], Chi[:, jc, :], pdn[:]), "v")
                wait("te", "v", c_last_v)

        barrier()

        prev_v = None
        prev_tr = None

        def wq_tile(rows_src, m, dst_fn, subw):
            nonlocal prev_v, prev_tr
            marks = []
            for s_i, w in enumerate(subw):
                o = sum(subw[:s_i])
                if prev_v is not None:
                    wait("sy", "v", prev_v)
                t_ld = dma_in(XT0[:, 0:w], rows_src[:, o:o + w])
                wait("sa", "ld", t_ld)
                if prev_v is not None:
                    wait("sa", "v", prev_v)
                a_sg = op("sa", lambda s, w=w: s.activation(QSG[:, 0:w], XT0[:, 0:w], AF.Sign), "a")
                wait("v", "ld", t_ld)
                wait("v", "a", a_sg)
                if prev_tr is not None:
                    wait("v", "t", prev_tr)
                v_mk = op("v", emit_quant(XT0[:, 0:w], QOUT[:, o:o + w], w), "v")
                marks.append(v_mk)
                prev_v = v_mk
            nblk = sum(subw) // 128
            wait("te", "v", marks[-1])
            for g0 in range(0, nblk, 8):
                gn = min(8, nblk - g0)
                t_tr = None
                for b in range(gn):
                    def _tr(te, b=b, g0=g0):
                        return te.transpose(pT[:, b * 128:(b + 1) * 128],
                                            QOUT[:, (g0 + b) * 128:(g0 + b + 1) * 128],
                                            ident[:])
                    if b < gn - 1:
                        op("te", _tr)
                    else:
                        t_tr = op("te", _tr, "t")
                prev_tr = t_tr
                wait("sa", "t", t_tr)
                a_ev = op("sa", lambda s, d=dst_fn(m, g0, gn), gn=gn:
                          s.copy(d, pT[:, 0:gn * 128].rearrange("p (c n) -> p c n", n=128)), "a")
                wait("te", "a", a_ev)

        for m in range(22):
            wq_tile(wg[m * 128:(m + 1) * 128, :], m,
                    lambda m, g0, gn: WGb[:, g0:g0 + gn, m * 128:(m + 1) * 128], [1024])
        for m in range(22):
            wq_tile(wu[m * 128:(m + 1) * 128, :], m,
                    lambda m, g0, gn: WUb[:, g0:g0 + gn, m * 128:(m + 1) * 128], [1024])
        for m in range(8):
            wq_tile(wd[m * 128:(m + 1) * 128, :], m,
                    lambda m, g0, gn: WDb[:, g0:g0 + gn, m * 128:(m + 1) * 128],
                    [1024, 1024, 768])

        barrier()

        def _owp1(v):
            v.tensor_tensor(NEGM64, OWNSUM, OWNSQ, A.subtract)
            v.tensor_scalar(NEGM64, NEGM64, -1024.0, -1.0 / 2048.0, A.add, A.mult)
            return v.tensor_copy(sc1[:, 3:4], c3buf[:])

        v_o1 = op("v", _owp1, "v")
        wait("sa", "v", v_o1)
        a_o1 = op("sa", lambda s: s.activation(MSQ64, NEGM64, AF.Square), "a")
        wait("v", "a", a_o1)

        def _owp2(v):
            v.scalar_tensor_tensor(VARE64, OWNSQ, 1.0 / 1024.0, MSQ64,
                                   A.mult, A.subtract)
            return v.tensor_scalar(VARE64, VARE64, 1e-5, None, A.add)

        v_o2 = op("v", _owp2, "v")
        wait("sa", "v", v_o2)
        a_o2 = op("sa", lambda s: s.activation(SQ64, VARE64, AF.Sqrt), "a")
        wait("v", "a", a_o2)
        v_o3 = op("v", lambda v: v.reciprocal(R064, SQ64), "v")
        wait("sa", "v", v_o3)
        a_o3 = op("sa", lambda s: s.activation(RR64, R064, AF.Square), "a")
        wait("v", "a", a_o3)

        def _owp4(v):
            v.tensor_tensor(B64, RR64, VARE64, A.mult)
            v.tensor_scalar(B64, B64, -0.5, 1.5, A.mult, A.add)
            return v.tensor_tensor(RSTD64, R064, B64, A.mult)

        v_own = op("v", _owp4, "v")

        ln_marks = []
        tr_last = []
        last_st = [None, None]
        PB_QO = (ABF[:, 3072:4096], ABF[:, 8960:9984])
        PB_E1 = (EV1, ABF[:, 9984:11008])
        PB_E2 = (EV2, ABF[:, 11008:12032])
        PB_TS = (TSTG, ABF[:, 12032:13056])
        for t in range(TPC):
            buf = XT0 if t % 2 == 0 else XT1
            p = t % 2
            qo, e1, e2, ts = PB_QO[p], PB_E1[p], PB_E2[p], PB_TS[p]
            if t >= 2:
                wait("sy", "v", ln_marks[t - 2])
            t_ld = dma_in(buf, xb[t * 128:(t + 1) * 128, :])
            wait("v", "ld", t_ld)
            v_ln = op("v", lambda v, b=buf, t=t: v.tensor_scalar(
                XNF, b, NEGM64[:, 32 + t:33 + t], RSTD64[:, 32 + t:33 + t],
                A.add, A.mult), "v")
            ln_marks.append(v_ln)
            wait("sa", "v", v_ln)
            if t >= 2:
                wait("sa", "t", tr_last[t - 2])

            def _ahi(s, e1=e1):
                s.activation(QSG[:, 0:1024], XNF, AF.Sign)
                return s.activation(e1, XNF, AF.Copy)

            a_hi = op("sa", _ahi, "a")
            wait("v", "a", a_hi)
            if t >= 2:
                wait("v", "t", tr_last[t - 2])
            op("v", emit_quant(XNF, qo, 1024))
            v_a = op("v", lambda v, e1=e1, e2=e2: v.scalar_tensor_tensor(
                e2, e1, -1.0, XNF, A.mult, A.add), "v")

            wait("te", "v", v_a)
            first_ev = True
            for src, dstd in ((qo, xqT_d), (e1, xhT_d), (e2, xlT_d)):
                t_tr = None
                for b in range(8):
                    def _tr(te, src=src, b=b):
                        return te.transpose(pT[:, b * 128:(b + 1) * 128],
                                            src[:, b * 128:(b + 1) * 128], ident[:])
                    if b < 7:
                        op("te", _tr)
                    else:
                        t_tr = op("te", _tr, "t")
                wait("sa", "t", t_tr)
                if first_ev and last_st[p] is not None:
                    wait("sa", "st", last_st[p])
                first_ev = False
                a_ev = op("sa", lambda s, ts=ts: s.copy(ts, pT[:]), "a")
                wait("te", "a", a_ev)
                wait("g", "a", a_ev)
                st_mk = dma_out("g", dstd[t], ts.rearrange("p (c n) -> p c n", n=128))
                if dstd is not xlT_d:
                    wait("sa", "st", st_mk)
                else:
                    last_st[p] = st_mk
            tr_last.append(t_tr)

        wait("sy", "st", max(m for m in last_st if m is not None))

        barrier()

        chunks = [(0, 512), (512, 512), (1024, 512), (1536, 512), (2048, 512), (2560, 256)]
        sil_marks, h_marks, hq_marks, tr_marks, gy_marks = [], [], [], [], []
        tile_last_gy = []
        prev_g_st = None
        pending_tr = None
        pending_ag = None

        def emit_tr(gk, io, iw):
            wait("te", "v", hq_marks[gk])
            nb = iw // 128
            tr_mk = None
            for b in range(nb):
                def _tr(te, b=b, io=io):
                    return te.transpose(pT[:, b * 128:(b + 1) * 128],
                                        QOUT[:, io + b * 128: io + (b + 1) * 128],
                                        ident[:])
                if b < nb - 1:
                    op("te", _tr)
                else:
                    tr_mk = op("te", _tr, "t")
            tr_marks.append(tr_mk)
            wait("sa", "t", tr_mk)
            a_ev = op("sa", lambda s, io=io, nb=nb:
                      s.copy(HQT[:, io // 128: io // 128 + nb, :],
                             pT[:, 0:nb * 128].rearrange("p (c n) -> p c n", n=128)), "a")
            wait("te", "a", a_ev)

        for t in range(TPC):
            xq = XQT0 if t % 2 == 0 else XQT1
            if t >= 2:
                wait("sy", "t", tile_last_gy[t - 2])
            t_ld = dma_in(xq, xqT_d[t])
            wait("te", "ld", t_ld)
            for ci_, (io, iw) in enumerate(chunks):
                gk = t * 6 + ci_
                pg = pg0 if gk % 2 == 0 else pg1
                py = py0 if gk % 2 == 0 else py1
                if gk >= 2:
                    wait("te", "a", sil_marks[gk - 2])
                    wait("te", "v", h_marks[gk - 2])
                for cc in range(8):
                    def _mmg(te, cc=cc, io=io, iw=iw, pg=pg, xq=xq,
                             st=(cc == 0), sp=(cc == 7)):
                        return te.matmul(pg[:, 0:iw], xq[:, cc, :],
                                         WGb[:, cc, io:io + iw], start=st, stop=sp)
                    op("te", _mmg)
                gy_mk = None
                for cc in range(8):
                    def _mmy(te, cc=cc, io=io, iw=iw, py=py, xq=xq,
                             st=(cc == 0), sp=(cc == 7)):
                        return te.matmul(py[:, 0:iw], xq[:, cc, :],
                                         WUb[:, cc, io:io + iw], start=st, stop=sp)
                    if cc < 7:
                        op("te", _mmy)
                    else:
                        gy_mk = op("te", _mmy, "t")
                gy_marks.append(gy_mk)
                if pending_tr is not None:
                    emit_tr(*pending_tr)
                    pending_tr = None
                wait("sa", "t", gy_mk)
                if gk >= 1:
                    wait("sa", "v", h_marks[gk - 1])
                sil_marks.append(op("sa", lambda s, pg=pg, iw=iw:
                                    s.activation(SIL[:, 0:iw], pg[:, 0:iw], AF.Silu), "a"))
                wait("v", "a", sil_marks[gk])
                h_marks.append(op("v", lambda v, py=py, iw=iw:
                                  v.tensor_mul(HB[:, 0:iw], SIL[:, 0:iw], py[:, 0:iw]), "v"))
                wait("sa", "v", h_marks[gk])
                if gk >= 1:
                    wait("sa", "v", hq_marks[gk - 1])
                a_sgn = op("sa", lambda s, iw=iw:
                           s.activation(QSG[:, 0:iw], HB[:, 0:iw], AF.Sign), "a")
                wait("v", "a", a_sgn)
                if gk >= 6:
                    wait("v", "t", tr_marks[gk - 6])
                hq_marks.append(op("v", emit_quant(HB[:, 0:iw], QOUT[:, io:io + iw], iw), "v"))
                pending_tr = (gk, io, iw)
            emit_tr(*pending_tr)
            pending_tr = None
            tile_last_gy.append(gy_marks[-1])
            if pending_ag is not None:
                wait("te", "a", pending_ag)
            t_dn = None
            for ic in range(22):
                for half in range(2):
                    def _mmd(te, ic=ic, half=half, st=(ic == 0), sp=(ic == 21)):
                        return te.matmul(pdn[:, half * 512:(half + 1) * 512],
                                         HQT[:, ic, :], WDb[:, ic, half * 512:(half + 1) * 512],
                                         start=st, stop=sp)
                    if ic == 21 and half == 1:
                        t_dn = op("te", _mmd, "t")
                    else:
                        op("te", _mmd)
            wait("sa", "t", t_dn)
            if prev_g_st is not None:
                wait("sa", "st", prev_g_st)
            a_g = op("sa", lambda s: s.activation(GSTGF, pdn[:], AF.Silu), "a")
            wait("g", "a", a_g)
            prev_g_st = dma_out("g", g_d[t], GSTGF)
            pending_ag = a_g

        barrier()

        wait("sy", "st", prev_g_st)
        t_ble = dma_in(BLEB, ble.partition_broadcast(128))
        wait("v", "ld", t_ble)
        Cn["st"] += 16
        _wle_st = Cn["st"]
        em("g", lambda e: e.dma_start(out=WleB[:], in_=wle).then_inc(sSt, 16))
        wait("te", "st", _wle_st)

        def _gmask(g):
            g.memset(maskb[:], 0.0)
            return g.memset(maskb[:, 0:12], 1.0)

        g_mask = op("g", _gmask, "g")
        wait("v", "g", g_mask)
        prev_cb = None
        prev_sty = None
        for t in range(TPC):
            if prev_cb is not None:
                wait("sy", "v", prev_cb)
                wait("sy", "st", prev_sty)
            dma_in(XHT, xhT_d[t])
            t2 = dma_in(XLT, xlT_d[t])
            t3 = dma_in(XT0, xb[t * 128:(t + 1) * 128, :])
            t4 = dma_in(GTF, g_d[t])
            wait("te", "ld", t2)
            for half in range(2):
                hsl = slice(half * 512, (half + 1) * 512)
                for cc in range(8):
                    def _mmx(te, cc=cc, hsl=hsl, st=(cc == 0)):
                        return te.matmul(pdn[:, hsl], XHT[:, cc, :],
                                         Chi[:, cc, hsl], start=st, stop=False)
                    op("te", _mmx)
            lat_mms = ([(XHT[:, cc, :], Mhi[:, cc, :]) for cc in range(8)]
                       + [(XHT[:, cc, :], Mlo[:, cc, :]) for cc in range(8)]
                       + [(XLT[:, cc, :], Mhi[:, cc, :]) for cc in range(8)])
            t_lat = None
            for i, (lh, rh) in enumerate(lat_mms):
                def _mml(te, lh=lh, rh=rh, st=(i == 0), sp=(i == 23)):
                    return te.matmul(latP, lh, rh, start=st, stop=sp)
                if i < 23:
                    op("te", _mml)
                else:
                    t_lat = op("te", _mml, "t")
            wait("v", "t", t_lat)
            v_bits = op("v", lambda v: v.tensor_tensor(
                BITSF, pg0[:, 0:64], negb[:], A.is_gt), "v")
            wait("sa", "v", v_bits)
            a_bc = op("sa", lambda s: s.copy(bits[:], BITSF[:, 0:24]), "a")
            wait("te", "a", a_bc)
            t_bt = op("te", lambda te: te.transpose(psA, bits[:], ident[:]), "t")
            wait("sa", "t", t_bt)
            a_bt = op("sa", lambda s: s.copy(bitsT[:], psA), "a")
            wait("te", "a", a_bt)
            t_syn = op("te", lambda te: te.matmul(synP, bitsT[:], HTs[:], start=True, stop=True), "t")
            wait("v", "t", t_syn)

            def _golay(v):
                v.tensor_scalar(synpm, py0[:, 0:64], 0.5, None, A.is_gt)
                v.tensor_reduce(sc1[:, 2:3], synpm[:, 0:12], AX, A.add)
                v.tensor_copy(sc1[:, 3:4], c3buf[:])
                v.tensor_copy(GW, sc1[:, 2:3].broadcast_to([128, 64]))
                v.tensor_scalar(GU, GW, 3.5, None, A.is_lt)
                v.tensor_tensor(GA, synpm, GU, A.mult)
                v.tensor_tensor(GF, GA, maskb, A.mult)
                v.tensor_tensor(GD, BITSF, GF, A.mult)
                v.tensor_tensor(GE, BITSF, GF, A.add)
                return v.scalar_tensor_tensor(CORRF, GD, -2.0, GE, A.mult, A.add)

            v_go = op("v", _golay, "v")
            wait("sa", "v", v_go)

            def _acorr(s):
                s.copy(corr[:, 0:12], CORRF[:, 0:12])
                return s.copy(corr[:, 12:24], BITSF[:, 12:24])

            a_co = op("sa", _acorr, "a")
            wait("te", "a", a_co)
            t_ct = op("te", lambda te: te.transpose(psB, corr[:], ident[:]), "t")
            wait("sa", "t", t_ct)
            a_ct = op("sa", lambda s: s.copy(corrT[:], psB), "a")
            wait("te", "a", a_ct)
            t_wle2 = None
            for half in range(2):
                def _mmw(te, half=half):
                    return te.matmul(pdn[:, half * 512:(half + 1) * 512], corrT[:],
                                     WleB[:, half * 512:(half + 1) * 512],
                                     start=False, stop=True)
                if half == 0:
                    op("te", _mmw)
                else:
                    t_wle2 = op("te", _mmw, "t")
            wait("v", "t", t_wle2)
            wait("v", "ld", max(t3, t4))

            def _comb(v):
                v.tensor_add(SSB, pdn[:], BLEB)
                v.tensor_mul(SSB, SSB, GTF)
                return v.tensor_add(OSB, SSB, XT0)

            prev_cb = op("v", _comb, "v")
            wait("te", "v", prev_cb)
            wait("g", "v", prev_cb)
            prev_sty = dma_out("g", yout[t * 128:(t + 1) * 128, :], OSB)

        wait("g", "st", Cn["st"])

        @block.sync
        def _(e):
            for fn in Q["sy"]:
                fn(e)

        @block.vector
        def _(e):
            for fn in Q["v"]:
                fn(e)

        @block.scalar
        def _(e):
            for fn in Q["sa"]:
                fn(e)

        @block.tensor
        def _(e):
            for fn in Q["te"]:
                fn(e)

        @block.gpsimd
        def _(e):
            for fn in Q["g"]:
                fn(e)

    mybir.codegen_inst_isa_subclasses(nc)
    return nc


def kernel(L0_attn_x, L0_attn_W_el, L0_attn_b_el, L0_attn_W_le, L0_attn_b_le,
           L0_ffn_w_gate, L0_ffn_w_up, L0_ffn_w_down):
    x = np.asarray(L0_attn_x)
    xf = np.ascontiguousarray(x.reshape(ROWS, E))
    if "nc" not in _NC_CACHE:
        _NC_CACHE["nc"] = build_program()
    nc = _NC_CACHE["nc"]
    base = {
        "xfull": xf,
        "wg": np.ascontiguousarray(np.asarray(L0_ffn_w_gate)),
        "wu": np.ascontiguousarray(np.asarray(L0_ffn_w_up)),
        "wd": np.ascontiguousarray(np.asarray(L0_ffn_w_down)),
        "wel": np.ascontiguousarray(np.asarray(L0_attn_W_el)),
        "bel": np.ascontiguousarray(np.asarray(L0_attn_b_el).reshape(1, 24)),
        "wle": np.ascontiguousarray(np.asarray(L0_attn_W_le)),
        "ble": np.ascontiguousarray(np.asarray(L0_attn_b_le).reshape(1, E)),
    }
    for name, arr in _CONSTS.items():
        base[name] = arr
    in_maps = []
    for c in range(N_CORES):
        m = dict(base)
        m["xb"] = np.ascontiguousarray(x[c])
        in_maps.append(m)
    res = run_bass_kernel_spmd(nc, in_maps, core_ids=list(range(N_CORES)))
    out = np.stack([res.results[c]["yout"] for c in range(N_CORES)], axis=0)
    return out.astype(np.float32, copy=False)


# version 14
# speedup vs baseline: 1.3628x; 1.0561x over previous
import math

import numpy as np
import ml_dtypes

import concourse.bass as bass
import concourse.mybir as mybir
from concourse.bass_utils import run_bass_kernel_spmd
from concourse.dve_op import DveOp
from concourse.dve_spec import Spec, Src0, C0, C1, C2, C3, lower as dve_lower, sq
from concourse.dve_table_gen import dve_ver_for
import concourse.dve_ops as dve_ops_mod

F32 = mybir.dt.float32
BF16 = mybir.dt.bfloat16
A = mybir.AluOpType
AF = mybir.ActivationFunctionType
AX = mybir.AxisListType.X
BF = ml_dtypes.bfloat16

B_, SEQ, E = 8, 4096, 1024
ROWS = B_ * SEQ
TPC = SEQ // 128
STILES = ROWS // 128
I_FFN = 2816
HALFPI = float(np.float32(math.pi / 2.0))
SIXTH = float(np.float32(1.0 / 6.0))
N_CORES = 8


def _register_op(name, spec, reference):
    for op_ in dve_ops_mod.OPS:
        if op_.name == name:
            return op_
    ver = dve_ver_for("TRN2")
    sha = dve_lower(spec, ver=ver, perf_en=False, name=name).sha(ver)
    op_ = DveOp(name, spec, subdim=False, uops_sha={ver: sha}, reference=reference)
    dve_ops_mod.OPS.append(op_)
    row = max(dve_ops_mod._SUB_OPCODE_FOR_NAME.values()) + 1
    assert row < 0x20, "custom-DVE opcode rows exhausted"
    dve_ops_mod._SUB_OPCODE_FOR_NAME[name] = row
    dve_ops_mod.CUSTOM_DVE_REFS[name] = reference
    return op_


def _fp4a_ref(in0, in1, s0, s1, imm2):
    y = in0 * in0
    return (y > s0) * 1.0 + (y > s1) + (y > imm2) + (y > in1)


def _fp4b_ref(in0, in1, s0, s1, imm2):
    y = in0 * in0
    return 2.0 * ((y > s0) * 1.0 + (y > s1) + 2.0 * (y > imm2))


_y = sq(Src0)
FP4A = _register_op(
    "ANT_FP4A", Spec(body=(_y > C0) + (_y > C1) + ((_y > C2) + (_y > C3))), _fp4a_ref
)
_c7 = _y > C2
_inner = ((_y > C0) + (_y > C1)) + (_c7 + _c7)
FP4B = _register_op("ANT_FP4B", Spec(body=_inner + _inner), _fp4b_ref)


def _build_consts():
    k = np.arange(513, dtype=np.float64)
    n = np.arange(1024, dtype=np.float64)
    ang = 2.0 * np.pi * np.outer(k, n) / 1024.0
    Acp = np.zeros((640, 1024), dtype=np.float64)
    Asp = np.zeros((640, 1024), dtype=np.float64)
    Acp[:513] = np.cos(ang)
    Asp[:513] = np.sin(ang)
    ac_ch = np.ascontiguousarray(Acp.reshape(5, 128, 1024).transpose(1, 0, 2)).astype(np.float32)
    as_ch = np.ascontiguousarray(Asp.reshape(5, 128, 1024).transpose(1, 0, 2)).astype(np.float32)
    Ac_hi = ac_ch.astype(BF)
    As_hi = as_ch.astype(BF)
    AcT = np.ascontiguousarray(Acp.T.reshape(8, 128, 640).transpose(1, 0, 2)).astype(np.float32)
    AsT = np.ascontiguousarray(Asp.T.reshape(8, 128, 640).transpose(1, 0, 2)).astype(np.float32)
    AcT_hi = AcT.astype(BF)
    AsT_hi = AsT.astype(BF)
    AcT_lo = (AcT - AcT_hi.astype(np.float32)).astype(BF)
    AsT_lo = (AsT - AsT_hi.astype(np.float32)).astype(BF)
    cw = np.zeros(640, dtype=np.float64)
    cw[:513] = 2.0 / 1024.0
    cw[0] = 1.0 / 1024.0
    cw[512] = 1.0 / 1024.0
    cconst = np.ascontiguousarray(cw.reshape(5, 128).T.astype(np.float32))
    kk = np.fft.fftfreq(1024).astype(np.float32)[:513].copy()
    kk[512] = np.float32(-0.5)
    ka = (np.abs(kk) + np.float32(1e-8)).astype(np.float32)
    ka2 = (ka * ka).astype(np.float32)
    kap = np.zeros(640, dtype=np.float32)
    ka2p = np.zeros(640, dtype=np.float32)
    kap[:513] = ka
    ka2p[:513] = ka2
    ka_ch = np.ascontiguousarray(kap.reshape(5, 128).T)
    ka2_ch = np.ascontiguousarray(ka2p.reshape(5, 128).T)
    lx = np.log(np.array([2.0, 4.0, 8.0, 16.0, 32.0], dtype=np.float32)).astype(np.float64)
    den = 5.0 * np.sum(lx * lx) - np.sum(lx) ** 2
    acoef = np.ascontiguousarray((-(5.0 * lx - np.sum(lx)) / den).astype(np.float32)[None, :])
    ident = np.eye(128, dtype=np.float32).astype(BF)
    _A = np.array(
        [[1,1,1,0,1,1,1,0,0,0,1,1],[1,1,0,1,1,1,0,0,0,1,1,0],[1,0,1,1,1,0,0,0,1,1,0,1],
         [0,1,1,1,0,0,0,1,1,0,1,1],[0,0,0,0,1,1,1,1,1,1,1,1],[1,1,1,1,0,0,0,0,0,0,0,0],
         [0,0,0,0,1,1,1,1,0,0,0,0],[1,1,0,0,1,1,0,0,1,1,0,0],[1,0,1,0,1,0,1,0,1,0,1,0],
         [0,1,0,1,0,1,0,1,0,1,0,1],[1,0,0,1,1,0,0,1,1,0,0,1],[0,1,1,0,0,1,1,0,0,1,1,0]],
        dtype=np.float32)
    H = np.concatenate([_A.T, np.eye(12, dtype=np.float32)], axis=1)
    HT = np.ascontiguousarray(H.T).astype(BF)
    Ac_lo = (ac_ch - Ac_hi.astype(np.float32)).astype(BF)
    As_lo = (as_ch - As_hi.astype(np.float32)).astype(BF)
    return dict(cAchi=Ac_hi, cAshi=As_hi, cAclo=Ac_lo, cAslo=As_lo,
                cAcThi=AcT_hi, cAcTlo=AcT_lo, cAsThi=AsT_hi, cAsTlo=AsT_lo,
                ccw=cconst, cka=ka_ch, cka2=ka2_ch, cacoef=acoef,
                cident=ident, cHT=HT)


_CONSTS = _build_consts()
_NC_CACHE = {}


def _v3(ap, n=16):
    return ap.rearrange("p (s n) -> p s n", n=n)


def _b16(small_ap, S):
    return small_ap.rearrange("p (s o) -> p s o", o=1).broadcast_to([128, S, 16])


def build_program():
    nc = bass.Bass("TRN2", target_bir_lowering=False, debug=False)

    xb = nc.dram_tensor("xb", [SEQ, E], F32, kind="ExternalInput").ap()
    xfull = nc.dram_tensor("xfull", [ROWS, E], F32, kind="ExternalInput").ap()
    wg = nc.dram_tensor("wg", [I_FFN, E], F32, kind="ExternalInput").ap()
    wu = nc.dram_tensor("wu", [I_FFN, E], F32, kind="ExternalInput").ap()
    wd = nc.dram_tensor("wd", [E, I_FFN], F32, kind="ExternalInput").ap()
    wel = nc.dram_tensor("wel", [E, 24], F32, kind="ExternalInput").ap()
    bel = nc.dram_tensor("bel", [1, 24], F32, kind="ExternalInput").ap()
    wle = nc.dram_tensor("wle", [24, E], F32, kind="ExternalInput").ap()
    ble = nc.dram_tensor("ble", [1, E], F32, kind="ExternalInput").ap()
    yout = nc.dram_tensor("yout", [SEQ, E], F32, kind="ExternalOutput").ap()
    ci = {}
    for name, arr in _CONSTS.items():
        dt = BF16 if arr.dtype == BF else F32
        ci[name] = nc.dram_tensor(name, list(arr.shape), dt, kind="ExternalInput").ap()

    xqT_d = nc.dram_tensor("xqT_d", [TPC, 128, 8, 128], BF16, kind="Internal").ap()
    xhT_d = nc.dram_tensor("xhT_d", [TPC, 128, 8, 128], BF16, kind="Internal").ap()
    xlT_d = nc.dram_tensor("xlT_d", [TPC, 128, 8, 128], BF16, kind="Internal").ap()
    g_d = nc.dram_tensor("g_d", [TPC, 128, E], F32, kind="Internal").ap()
    vec_d = nc.dram_tensor("vec_d", [4, 128], F32, kind="Internal").ap()

    from contextlib import ExitStack
    es = ExitStack()
    with es:
        WGb = es.enter_context(nc.sbuf_tensor("WGb", [128, 8, I_FFN], BF16))
        WUb = es.enter_context(nc.sbuf_tensor("WUb", [128, 8, I_FFN], BF16))
        WDb = es.enter_context(nc.sbuf_tensor("WDb", [128, 22, E], BF16))
        Chi = es.enter_context(nc.sbuf_tensor("Chi", [128, 8, E], BF16))
        Mhi = es.enter_context(nc.sbuf_tensor("Mhi", [128, 8, 24], BF16))
        Mlo = es.enter_context(nc.sbuf_tensor("Mlo", [128, 8, 24], BF16))
        AF_ = es.enter_context(nc.sbuf_tensor("AF_", [128, 6144], F32))
        ABF = es.enter_context(nc.sbuf_tensor("ABF", [128, 13824], BF16))
        ident = es.enter_context(nc.sbuf_tensor("ident", [128, 128], BF16))
        HTs = es.enter_context(nc.sbuf_tensor("HTs", [24, 12], BF16))
        negb = es.enter_context(nc.sbuf_tensor("negb", [128, 64], F32))
        c3buf = es.enter_context(nc.sbuf_tensor("c3buf", [128, 1], F32))
        sc1 = es.enter_context(nc.sbuf_tensor("sc1", [128, 4], F32))
        cstb = es.enter_context(nc.sbuf_tensor("cstb", [128, 8], F32))
        dsm = es.enter_context(nc.sbuf_tensor("dsm", [1, 16], F32))
        dly = es.enter_context(nc.sbuf_tensor("dly", [1, 8], F32))
        dly2 = es.enter_context(nc.sbuf_tensor("dly2", [1, 8], F32))
        dcnt = es.enter_context(nc.sbuf_tensor("dcnt", [1, 8], F32))
        dll = es.enter_context(nc.sbuf_tensor("dll", [1, 8], F32))
        dslp = es.enter_context(nc.sbuf_tensor("dslp", [1, 8], F32))
        acf = es.enter_context(nc.sbuf_tensor("acf", [1, 5], F32))
        Db = es.enter_context(nc.sbuf_tensor("Db", [128, 1], F32))
        alf0 = es.enter_context(nc.sbuf_tensor("alf0", [128, 1], F32))
        gtm = es.enter_context(nc.sbuf_tensor("gtm", [128, 1], F32))
        alf = es.enter_context(nc.sbuf_tensor("alf", [128, 1], F32))
        bet = es.enter_context(nc.sbuf_tensor("bet", [128, 1], F32))
        gam = es.enter_context(nc.sbuf_tensor("gam", [128, 1], F32))
        kas = es.enter_context(nc.sbuf_tensor("kas", [128, 5], F32))
        ka2s = es.enter_context(nc.sbuf_tensor("ka2s", [128, 5], F32))
        cws = es.enter_context(nc.sbuf_tensor("cws", [128, 5], F32))
        tmp5 = es.enter_context(nc.sbuf_tensor("tmp5", [128, 5], F32))
        t5c = es.enter_context(nc.sbuf_tensor("t5c", [128, 5], F32))
        ub5 = es.enter_context(nc.sbuf_tensor("ub5", [128, 5], F32))
        fr1 = es.enter_context(nc.sbuf_tensor("fr1", [128, 5], F32))
        fr2 = es.enter_context(nc.sbuf_tensor("fr2", [128, 5], F32))
        cfr = es.enter_context(nc.sbuf_tensor("cfr", [128, 5], F32))
        WelS = es.enter_context(nc.sbuf_tensor("WelS", [128, 8, 24], F32))
        WelH = es.enter_context(nc.sbuf_tensor("WelH", [128, 8, 24], BF16))
        WelL = es.enter_context(nc.sbuf_tensor("WelL", [128, 8, 24], BF16))
        AWc = es.enter_context(nc.sbuf_tensor("AWc", [128, 5, 64], F32))
        AWs = es.enter_context(nc.sbuf_tensor("AWs", [128, 5, 64], F32))
        Rh = es.enter_context(nc.sbuf_tensor("Rh", [128, 5, 32], BF16))
        Rl = es.enter_context(nc.sbuf_tensor("Rl", [128, 5, 32], BF16))
        Mf = es.enter_context(nc.sbuf_tensor("Mf", [128, 8, 64], F32))
        bits = es.enter_context(nc.sbuf_tensor("bits", [128, 24], BF16))
        corr = es.enter_context(nc.sbuf_tensor("corr", [128, 24], BF16))
        pg0 = es.enter_context(nc.psum_tensor("pg0", [128, 512], F32))
        pg1 = es.enter_context(nc.psum_tensor("pg1", [128, 512], F32))
        py0 = es.enter_context(nc.psum_tensor("py0", [128, 512], F32))
        py1 = es.enter_context(nc.psum_tensor("py1", [128, 512], F32))
        pdn = es.enter_context(nc.psum_tensor("pdn", [128, 1024], F32))
        pT = es.enter_context(nc.psum_tensor("pT", [128, 1024], BF16))
        sLd = es.enter_context(nc.semaphore("sLd"))
        sL2 = es.enter_context(nc.semaphore("sL2"))
        sSt = es.enter_context(nc.semaphore("sSt"))
        sV = es.enter_context(nc.semaphore("sV"))
        sA = es.enter_context(nc.semaphore("sA"))
        sT = es.enter_context(nc.semaphore("sT"))
        sSy = es.enter_context(nc.semaphore("sSy"))
        sG = es.enter_context(nc.semaphore("sG"))
        block = es.enter_context(nc.Block())
        XT0 = AF_[:, 0:1024]
        XT1 = AF_[:, 1024:2048]
        QW = AF_[:, 2048:3072]
        SIL = AF_[:, 3072:3584]
        HB = AF_[:, 3584:4096]
        SMVS = AF_[:, 4096:4352]
        SMVQ = AF_[:, 4352:4608]
        SMN = AF_[:, 4608:4864]
        SMX = AF_[:, 4864:5120]
        XNF = AF_[:, 5120:6144]
        BLEB = AF_[:, 5120:6144]
        SSB = AF_[:, 4096:5120]
        OSB = AF_[:, 3072:4096]
        NEGM = AF_[:, 2048:2304]
        MSQ = AF_[:, 2304:2560]
        VARE = AF_[:, 2560:2816]
        SQV = AF_[:, 2816:3072]
        R0G = AF_[:, 3072:3328]
        RRG = AF_[:, 3328:3584]
        BVG = AF_[:, 3584:3840]
        RSTDG = AF_[:, 3840:4096]
        OWNSUM = AF_[:, 5120:5184]
        OWNSQ = AF_[:, 5184:5248]
        NEGM64 = AF_[:, 3072:3136]
        RSTD64 = AF_[:, 3136:3200]
        MSQ64 = AF_[:, 5248:5312]
        VARE64 = AF_[:, 5312:5376]
        SQ64 = AF_[:, 5376:5440]
        R064 = AF_[:, 5440:5504]
        RR64 = AF_[:, 5504:5568]
        B64 = AF_[:, 5568:5632]
        MfC = AF_[:, 2048:2240].rearrange("p (c n) -> p c n", n=24)
        Rf = AF_[:, 2048:2208].rearrange("p (c n) -> p c n", n=32)
        qm = AF_[:, 4096:4160]
        qsb = AF_[:, 4160:4224]
        qrb = AF_[:, 4224:4288]
        qs2 = AF_[:, 4288:4352]
        XROW = AF_[0:1, 0:1024]
        XNROW = AF_[0:1, 1024:2048]
        ZROW = AF_[0:1, 3072:4096]
        MROW = AF_[0:1, 0:1024]
        MNR = AF_[0:1, 2048:2176]
        MXR = AF_[0:1, 2176:2304]

        S1 = ABF[:, 0:5120].rearrange("p (c n) -> p c n", n=1024)
        S2 = ABF[:, 5120:10240].rearrange("p (c n) -> p c n", n=1024)
        S1f = ABF[:, 0:5120]
        S2f = ABF[:, 5120:10240]
        AcT_hi_v = S1f.rearrange("p (c n) -> p c n", n=640)
        AcT_lo_v = S2f.rearrange("p (c n) -> p c n", n=640)
        QGA = ABF[:, 0:1024]
        QGB = ABF[:, 1024:2048]
        QSG = ABF[:, 2048:3072]
        QOUT = ABF[:, 3072:5888]
        EV1 = ABF[:, 5888:6912]
        EV2 = ABF[:, 6912:7936]
        TSTG = ABF[:, 7936:8960]
        GSTGF = AF_[:, 5120:6144]
        HQT = ABF[:, 8960:11776].rearrange("p (c n) -> p c n", n=128)
        XQT0 = ABF[:, 11776:12800].rearrange("p (c n) -> p c n", n=128)
        XQT1 = ABF[:, 12800:13824].rearrange("p (c n) -> p c n", n=128)
        XHT = ABF[:, 5888:6912].rearrange("p (c n) -> p c n", n=128)
        XLT = ABF[:, 6912:7936].rearrange("p (c n) -> p c n", n=128)
        GTF = AF_[:, 2048:3072]
        GW = AF_[:, 1024:1088]
        GU = AF_[:, 1088:1152]
        GA = AF_[:, 1152:1216]
        GB = AF_[:, 1024:1088]
        GF0 = AF_[:, 1088:1152]
        GF = AF_[:, 1152:1216]
        BITSF = AF_[:, 1216:1280]
        GD = AF_[:, 1024:1088]
        GE = AF_[:, 1088:1152]
        CORRF = AF_[:, 1152:1216]
        synpm = AF_[:, 1280:1344]
        maskb = AF_[:, 1344:1408]
        WleB = ABF[0:24, 8960:9984]
        bitsT = ABF[0:24, 9984:10112]
        corrT = ABF[0:24, 10112:10240]
        NB05 = cstb[:, 0:1]
        NB5 = cstb[:, 1:2]
        HPIN = cstb[:, 2:3]
        HPIP = cstb[:, 3:4]
        oneb = cstb[:, 4:5]

        pAW = pg0[:, 0:24]
        pAW64 = pg0[:, 0:64]
        latP = pg0[:, 0:24]
        synP = py0[:, 0:12]
        psA = pT[0:24, 0:128]
        psB = pT[0:24, 128:256]

        Q = {"sy": [], "v": [], "sa": [], "te": [], "g": []}
        Cn = {"ld": 0, "ld2": 0, "st": 0, "v": 0, "a": 0, "t": 0, "sy": 0, "g": 0}
        SEM = {"ld": sLd, "ld2": sL2, "st": sSt, "v": sV, "a": sA, "t": sT,
               "sy": sSy, "g": sG}

        def em(eng, fn):
            Q[eng].append(fn)

        def dma_in(dst, src):
            Cn["ld"] += 16
            tgt = Cn["ld"]
            em("sy", lambda e, d=dst, s=src: e.dma_start(out=d, in_=s).then_inc(sLd, 16))
            return tgt

        def dma_in2(dst, src):
            Cn["ld2"] += 16
            tgt = Cn["ld2"]
            em("g", lambda e, d=dst, s=src: e.dma_start(out=d, in_=s).then_inc(sL2, 16))
            return tgt

        def wait(eng, sem_key, val):
            em(eng, lambda e, k=sem_key, vv=val: e.wait_ge(SEM[k], vv))

        def op(eng, fn, inc_key=None):
            if inc_key is None:
                em(eng, fn)
                return None
            Cn[inc_key] += 1
            tgt = Cn[inc_key]

            def _f(e, f=fn, k=inc_key):
                f(e).then_inc(SEM[k], 1)

            em(eng, _f)
            return tgt

        def dma_out(eng_key, dst, src):
            Cn["st"] += 16
            tgt = Cn["st"]
            em(eng_key, lambda e, d=dst, s=src: e.dma_start(out=d, in_=s).then_inc(sSt, 16))
            return tgt

        def barrier():
            dp = {
                "sy": lambda e: e.nop(),
                "v": lambda e: e.engine_nop(),
                "sa": lambda e: e.copy(cstb[:, 7:8], c3buf[:]),
                "te": lambda e: e.ldweights(ident[:]),
                "g": lambda e: e.engine_nop(),
            }
            marks = {}
            for ek, sk in (("sy", "sy"), ("v", "v"), ("sa", "a"), ("te", "t"), ("g", "g")):
                Cn[sk] += 1
                marks[ek] = Cn[sk]
                em(ek, lambda e, k=sk, f=dp[ek]: f(e).then_inc(SEM[k], 1))
            for ek in ("sy", "v", "sa", "te", "g"):
                for ok, sk in (("sy", "sy"), ("v", "v"), ("sa", "a"), ("te", "t"), ("g", "g")):
                    if ok != ek:
                        wait(ek, sk, marks[ok])

        def emit_quant(src, dst, W, qsg=None):
            S = W // 16
            if qsg is None:
                qsg = QSG

            def _q(v, s=src, d=dst, W=W, S=S, qsg=qsg):
                v.tensor_reduce(qm[:, 0:S], _v3(s), AX, A.max, apply_absolute_value=True)
                v.tensor_copy(sc1[:, 3:4], c3buf[:])
                v.tensor_scalar(qsb[:, 0:64], qm[:, 0:64], SIXTH, 1e-12, A.mult, A.max)
                v.reciprocal(qrb[:, 0:64], qsb[:, 0:64])
                v.tensor_scalar(qs2[:, 0:64], qsb[:, 0:64], 0.5, None, A.mult)
                v.tensor_tensor(_v3(QW[:, :W]), _v3(s), _b16(qrb[:, 0:S], S), A.mult)
                v._custom_dve(FP4A, out=QGA[:, :W], in0=QW[:, :W], in1=c3buf[:],
                              s0=0.0625, s1=0.5625, imm2=1.5625)
                v._custom_dve(FP4B, out=QGB[:, :W], in0=QW[:, :W],
                              s0=6.25, s1=12.25, imm2=25.0)
                v.tensor_tensor(QGA[:, :W], QGA[:, :W], QGB[:, :W], A.add)
                v.tensor_tensor(QGA[:, :W], QGA[:, :W], qsg[:, :W], A.mult)
                return v.tensor_tensor(_v3(d), _v3(QGA[:, :W]), _b16(qs2[:, 0:S], S), A.mult)

            return _q

        for dst, key in ((ident[:], "cident"), (HTs[:], "cHT"), (acf[:], "cacoef"),
                         (kas[:], "cka"), (ka2s[:], "cka2"), (cws[:], "ccw")):
            dma_in(dst, ci[key])
        dma_in(WelS[:], wel.rearrange("(c p) e -> p c e", p=128))
        t_bel = dma_in(negb[:, 0:24], bel.partition_broadcast(128))
        t_act1 = dma_in(AcT_hi_v, ci["cAcThi"])
        t_act2 = dma_in(AcT_lo_v, ci["cAcTlo"])
        consts_ld = Cn["ld"]

        wait("v", "ld", consts_ld)
        op("v", lambda v: v.memset(c3buf[:], 3.0625))
        wait("g", "ld", t_bel)

        def _gsetup(g):
            g.memset(negb[:, 24:64], 0.0)
            g.tensor_scalar(negb[:, 0:24], negb[:, 0:24], -1.0, None, A.mult)
            g.memset(cstb[:, 0:1], -0.5)
            g.memset(cstb[:, 1:2], -5.0)
            g.memset(cstb[:, 2:3], -HALFPI)
            g.memset(cstb[:, 4:5], 1.0)
            return g.memset(cstb[:, 3:4], HALFPI)

        g_setup = op("g", _gsetup, "g")
        op("v", lambda v: v.tensor_copy(WelH[:], WelS[:]))
        v_wel = op("v", lambda v: v.scalar_tensor_tensor(
            WelL[:].rearrange("p c n -> p (c n)"),
            WelH[:].rearrange("p c n -> p (c n)"), -1.0,
            WelS[:].rearrange("p c n -> p (c n)"), A.mult, A.add), "v")

        wait("te", "v", v_wel)
        wait("te", "ld", t_act2)
        aw_last_t = None
        for nm, AW in (("c", AWc), ("s", AWs)):
            if nm == "s":
                wait("sy", "t", aw_last_t)
                t_ast1 = dma_in(AcT_hi_v, ci["cAsThi"])
                t_ast2 = dma_in(AcT_lo_v, ci["cAsTlo"])
                wait("te", "ld", t_ast2)
            for kc in range(5):
                ksl = slice(kc * 128, (kc + 1) * 128)
                mms = []
                for cc in range(8):
                    mms.append((AcT_hi_v[:, cc, ksl], WelH[:, cc, :]))
                    mms.append((AcT_hi_v[:, cc, ksl], WelL[:, cc, :]))
                    mms.append((AcT_lo_v[:, cc, ksl], WelH[:, cc, :]))
                t_mark = None
                for i, (lh, rh) in enumerate(mms):
                    def _mm(te, lh=lh, rh=rh, st=(i == 0), sp=(i == len(mms) - 1)):
                        return te.matmul(pAW, lh, rh, start=st, stop=sp)
                    if i < len(mms) - 1:
                        op("te", _mm)
                    else:
                        t_mark = op("te", _mm, "t")
                aw_last_t = t_mark
                wait("sa", "t", t_mark)
                a_mark = op("sa", lambda s, d=AW[:, kc, :]: s.copy(d, pAW64), "a")
                wait("te", "a", a_mark)

        op("v", lambda v: v.memset(OWNSUM[:, 0:32], 1.0))
        op("v", lambda v: v.memset(OWNSQ[:, 0:32], 1.0))
        v_stats_base = Cn["v"]
        a_stats_base = Cn["a"]
        NT_ALL = STILES + TPC
        for t in range(NT_ALL):
            buf = XT0 if t % 2 == 0 else XT1
            qk = "sy" if t % 2 == 0 else "g"
            lsem = "ld" if t % 2 == 0 else "ld2"
            if t >= 2:
                wait(qk, "v", v_stats_base + (t - 1))
                wait(qk, "a", a_stats_base + (t - 1))
            if t < STILES:
                srcap = xfull[t * 128:(t + 1) * 128, :]
            else:
                tt = t - STILES
                srcap = xb[tt * 128:(tt + 1) * 128, :]
            t_ld = dma_in(buf, srcap) if t % 2 == 0 else dma_in2(buf, srcap)
            wait("sa", lsem, t_ld)
            if t < STILES:
                def _sq(s, b=buf, t=t):
                    s.activation(QW, b, AF.Square, accum_out=SMVQ[:, t:t + 1])
                    return s.activation(QW, b, AF.Square, bias=oneb,
                                        accum_out=SMVS[:, t:t + 1])
            else:
                def _sq(s, b=buf, tt=t - STILES):
                    s.activation(QW, b, AF.Square, accum_out=OWNSQ[:, tt + 32:tt + 33])
                    return s.activation(QW, b, AF.Square, bias=oneb,
                                        accum_out=OWNSUM[:, tt + 32:tt + 33])
            op("sa", _sq, "a")
            wait("v", lsem, t_ld)

            if t < STILES:
                def _st(v, b=buf, t=t):
                    v.tensor_reduce(SMN[:, t:t + 1], b, AX, A.min)
                    v.reduce_max(SMX[:, t:t + 1], b, AX)
                    return v.tensor_copy(sc1[:, 3:4], c3buf[:])

                op("v", _st, "v")
            else:
                Cn["v"] += 1
                em("v", lambda e: e.engine_nop().then_inc(sV, 1))

        def _wp1(v):
            v.tensor_tensor(NEGM, SMVS, SMVQ, A.subtract)
            v.tensor_scalar(NEGM, NEGM, -1024.0, -1.0 / 2048.0, A.add, A.mult)
            return v.tensor_copy(sc1[:, 3:4], c3buf[:])

        v_wp1 = op("v", _wp1, "v")
        wait("sa", "v", v_wp1)
        a_wp1 = op("sa", lambda s: s.activation(MSQ, NEGM, AF.Square), "a")
        wait("v", "a", a_wp1)

        def _wp2(v):
            v.scalar_tensor_tensor(VARE, SMVQ, 1.0 / 1024.0, MSQ,
                                   A.mult, A.subtract)
            return v.tensor_scalar(VARE, VARE, 1e-5, None, A.add)

        v_wp2 = op("v", _wp2, "v")
        wait("sa", "v", v_wp2)
        a_wp2 = op("sa", lambda s: s.activation(SQV, VARE, AF.Sqrt), "a")
        wait("v", "a", a_wp2)
        v_wp3 = op("v", lambda v: v.reciprocal(R0G, SQV), "v")
        wait("sa", "v", v_wp3)
        a_wp3 = op("sa", lambda s: s.activation(RRG, R0G, AF.Square), "a")
        wait("v", "a", a_wp3)

        def _wp4(v):
            v.tensor_tensor(BVG, RRG, VARE, A.mult)
            v.tensor_scalar(BVG, BVG, -0.5, 1.5, A.mult, A.add)
            v.tensor_tensor(RSTDG, R0G, BVG, A.mult)
            v.tensor_tensor(SMN, SMN, NEGM, A.add)
            v.tensor_tensor(SMN, SMN, RSTDG, A.mult)
            v.tensor_tensor(SMX, SMX, NEGM, A.add)
            v.tensor_tensor(SMX, SMX, RSTDG, A.mult)
            v.tensor_reduce(sc1[:, 0:1], SMN, AX, A.min)
            v.reduce_max(sc1[:, 1:2], SMX, AX)
            return v.tensor_copy(sc1[:, 3:4], c3buf[:])

        v_wp = op("v", _wp4, "v")

        wait("g", "v", v_wp)
        wait("g", "g", g_setup)

        def _dg1(g):
            g.tensor_copy(dsm[0:1, 3:4], NEGM[0:1, 0:1])
            g.tensor_copy(dsm[0:1, 4:5], RSTDG[0:1, 0:1])
            return g.tensor_tensor(dsm[0:1, 7:8], dsm[0:1, 3:4], dsm[0:1, 4:5], A.mult)

        g_d1 = op("g", _dg1, "g")
        dma_out("g", vec_d[0, :], sc1[:, 0:1])
        st_b = dma_out("g", vec_d[1, :], sc1[:, 1:2])
        wait("sy", "st", st_b)
        dma_in(MNR[:, 0:128], vec_d[0:1, :])
        dma_in(MXR[:, 0:128], vec_d[1:2, :])
        t_x0 = dma_in(XROW, xfull[0:1, :])
        wait("v", "ld", t_x0)

        def _dv1(v):
            v.tensor_reduce(dsm[0:1, 0:1], MNR[:, 0:128], AX, A.min)
            v.reduce_max(dsm[0:1, 1:2], MXR[:, 0:128], AX)
            return v.tensor_copy(sc1[:, 3:4], c3buf[:])

        v_d1 = op("v", _dv1, "v")
        wait("g", "v", v_d1)

        def _dg2(g):
            g.tensor_tensor(dsm[0:1, 5:6], dsm[0:1, 1:2], dsm[0:1, 0:1], A.subtract)
            return g.tensor_scalar(dsm[0:1, 5:6], dsm[0:1, 5:6], 1e-10, None, A.add)

        g_d2 = op("g", _dg2, "g")
        wait("v", "g", g_d2)
        v_d2 = op("v", lambda v: v.reciprocal(dsm[0:1, 2:3], dsm[0:1, 5:6]), "v")
        wait("g", "v", v_d2)

        def _dg3(g):
            g.tensor_tensor(dsm[0:1, 6:7], dsm[0:1, 0:1], dsm[0:1, 2:3], A.mult)
            return g.tensor_scalar(dsm[0:1, 6:7], dsm[0:1, 6:7], -1.0, None, A.mult)

        g_d3 = op("g", _dg3, "g")
        wait("sa", "g", g_d3)

        def _da1(s):
            s.activation(XNROW, XROW, AF.Identity,
                         bias=dsm[0:1, 7:8], scale=dsm[0:1, 4:5])
            return s.activation(ZROW, XNROW, AF.Identity,
                                bias=dsm[0:1, 6:7], scale=dsm[0:1, 2:3])

        a_d1 = op("sa", _da1, "a")
        wait("v", "a", a_d1)
        NNJ = (4, 16, 64, 256, 1024)

        def _dv3(v):
            for j, nn in enumerate(NNJ):
                v.tensor_reduce(dly[0:1, j:j + 1], ZROW[:, 0:nn], AX, A.add)
            return v.tensor_copy(sc1[:, 3:4], c3buf[:])

        v_d3 = op("v", _dv3, "v")
        wait("g", "v", v_d3)

        def _dg4(g):
            last = None
            for j, nn in enumerate(NNJ):
                last = g.tensor_scalar(dly2[0:1, j:j + 1], dly[0:1, j:j + 1],
                                       1.0 / nn, None, A.mult)
            return last

        g_d4 = op("g", _dg4, "g")
        wait("v", "g", g_d4)

        def _dv4(v):
            for j, nn in enumerate(NNJ):
                v.tensor_scalar(MROW, ZROW, dly2[0:1, j:j + 1], None, A.is_gt)
                v.tensor_reduce(dcnt[0:1, j:j + 1], MROW[:, 0:nn], AX, A.add)
            return v.tensor_copy(sc1[:, 3:4], c3buf[:])

        v_d4 = op("v", _dv4, "v")
        wait("g", "v", v_d4)
        g_d5 = op("g", lambda g: g.tensor_scalar(dcnt[0:1, 0:5], dcnt[0:1, 0:5],
                                                 1.0, None, A.max), "g")
        wait("sa", "g", g_d5)
        a_d2 = op("sa", lambda s: s.activation(dll[0:1, 0:5], dcnt[0:1, 0:5], AF.Ln), "a")
        wait("g", "a", a_d2)
        g_d6 = op("g", lambda g: g.tensor_tensor(dslp[0:1, 0:5], dll[0:1, 0:5],
                                                 acf[0:1, :], A.mult), "g")
        wait("v", "g", g_d6)

        def _dv5(v):
            v.tensor_reduce(dsm[0:1, 8:9], dslp[0:1, 0:5], AX, A.add)
            return v.tensor_copy(sc1[:, 3:4], c3buf[:])

        v_d5 = op("v", _dv5, "v")
        wait("g", "v", v_d5)
        g_d7 = op("g", lambda g: g.tensor_scalar(dsm[0:1, 9:10], dsm[0:1, 8:9],
                                                 1.0, 2.0, A.max, A.min), "g")
        wait("g", "g", g_d7)
        st_d = dma_out("g", vec_d[2, 0:1], dsm[0:1, 9:10])
        wait("sy", "st", st_d)
        t_db = dma_in(Db[:], vec_d[2:3, 0:1].partition_broadcast(128))

        wait("sa", "ld", t_db)
        a_f0 = op("sa", lambda s: s.activation(alf0[:], Db[:], AF.Square), "a")
        wait("g", "a", a_f0)

        def _fg1(g):
            g.tensor_scalar(gtm[:], alf0[:], 0.2, None, A.mult)
            g.tensor_tensor(alf[:], gtm[:], Db[:], A.add)
            g.tensor_scalar(bet[:], Db[:], 0.5, 1.0, A.mult, A.add)
            return g.tensor_scalar(gam[:], Db[:], 0.1, None, A.mult)

        g_f1 = op("g", _fg1, "g")
        wait("sa", "g", g_f1)

        def _fa1(s):
            s.activation(tmp5[:], kas[:], AF.Identity, bias=alf[:], scale=bet[:])
            s.activation(ub5[:], tmp5[:], AF.Abs, bias=HPIN)
            s.activation(fr1[:], ub5[:], AF.Sin, bias=HPIP, scale=-1.0)
            return s.activation(t5c[:], ka2s[:], AF.Identity, bias=alf[:], scale=gam[:])

        a_f1 = op("sa", _fa1, "a")
        wait("g", "a", a_f1)
        g_f2 = op("g", lambda g: g.tensor_tensor(t5c[:], t5c[:], kas[:], A.subtract), "g")
        wait("sa", "g", g_f2)
        a_f2 = op("sa", lambda s: s.activation(fr2[:], t5c[:], AF.Sin,
                                               bias=HPIP, scale=-1.0), "a")
        wait("g", "a", a_f2)

        def _fg3(g):
            g.tensor_tensor(cfr[:], fr1[:], fr2[:], A.mult)
            return g.tensor_tensor(cfr[:], cfr[:], cws[:], A.mult)

        g_cfr = op("g", _fg3, "g")

        wait("sy", "t", aw_last_t)
        t_m2 = dma_in(S1, ci["cAchi"])
        t_m2b = dma_in(S2, ci["cAclo"])
        m_last_t = None
        m_last_v = None
        for nm in ("c", "s"):
            AW = AWc if nm == "c" else AWs
            if nm == "s":
                wait("sy", "t", m_last_t)
                t_m2 = dma_in(S1, ci["cAshi"])
                t_m2b = dma_in(S2, ci["cAslo"])
            cfrb32 = cfr[:].rearrange("p (c o) -> p c o", o=1).broadcast_to([128, 5, 32])

            def _rbuild(v, AW=AW, cfrb32=cfrb32):
                return v.tensor_tensor(Rf, AW[:, :, 0:32], cfrb32, A.mult)

            if nm == "c":
                wait("v", "g", g_cfr)
            v_r = op("v", _rbuild, "v")
            op("v", lambda v: v.tensor_copy(Rh[:], Rf))
            v_rl = op("v", lambda v: v.scalar_tensor_tensor(
                Rl[:].rearrange("p c n -> p (c n)"),
                Rh[:].rearrange("p c n -> p (c n)"), -1.0,
                Rf.rearrange("p c n -> p (c n)"), A.mult, A.add), "v")
            wait("te", "v", v_rl)
            wait("te", "ld", t_m2b)
            for jc in range(8):
                jsl = slice(jc * 128, (jc + 1) * 128)
                mms = []
                for kc in range(5):
                    mms.append((S1[:, kc, jsl], Rh[:, kc, 0:24]))
                    mms.append((S1[:, kc, jsl], Rl[:, kc, 0:24]))
                    mms.append((S2[:, kc, jsl], Rh[:, kc, 0:24]))
                t_mk = None
                for i, (lh, rh) in enumerate(mms):
                    def _mm(te, lh=lh, rh=rh, st=(i == 0), sp=(i == len(mms) - 1)):
                        return te.matmul(pAW, lh, rh, start=st, stop=sp)
                    if i < len(mms) - 1:
                        op("te", _mm)
                    else:
                        t_mk = op("te", _mm, "t")
                m_last_t = t_mk
                wait("v", "t", t_mk)
                if nm == "c":
                    m_last_v = op("v", lambda v, jc=jc: v.tensor_copy(Mf[:, jc, :], pAW64), "v")
                else:
                    m_last_v = op("v", lambda v, jc=jc: v.tensor_tensor(
                        Mf[:, jc, :], Mf[:, jc, :], pAW64, A.add), "v")
                wait("te", "v", m_last_v)

        def _msplit(v):
            v.tensor_copy(MfC, Mf[:, :, 0:24])
            v.tensor_copy(Mhi[:], MfC)
            return v.scalar_tensor_tensor(
                Mlo[:].rearrange("p c n -> p (c n)"),
                Mhi[:].rearrange("p c n -> p (c n)"), -1.0,
                MfC.rearrange("p c n -> p (c n)"), A.mult, A.add)

        v_msp = op("v", _msplit, "v")

        c_last_v = v_msp
        for nm in ("c", "s"):
            wait("sy", "t", m_last_t)
            wait("sy", "v", c_last_v)
            t_c1 = dma_in(S1, ci["cAchi"] if nm == "c" else ci["cAshi"])
            t_c2 = dma_in(S2, ci["cAchi"] if nm == "c" else ci["cAshi"])
            wait("v", "ld", t_c2)
            cfrb1k = cfr[:].rearrange("p (c o) -> p c o", o=1).broadcast_to([128, 5, 1024])

            def _pbuild(v, cfrb1k=cfrb1k):
                return v.tensor_tensor(S2, S2, cfrb1k, A.mult)

            v_p = op("v", _pbuild, "v")
            wait("te", "v", v_p)
            wait("te", "ld", t_c1)
            for jc in range(8):
                jsl = slice(jc * 128, (jc + 1) * 128)
                t_ck = None
                for half in range(2):
                    hsl = slice(half * 512, (half + 1) * 512)
                    for kc in range(5):
                        def _mm(te, kc=kc, jsl=jsl, hsl=hsl, st=(kc == 0), sp=(kc == 4)):
                            return te.matmul(pdn[:, hsl], S1[:, kc, jsl],
                                             S2[:, kc, hsl], start=st, stop=sp)
                        if kc < 4 or half == 0:
                            op("te", _mm)
                        else:
                            t_ck = op("te", _mm, "t")
                wait("v", "t", t_ck)
                if nm == "c":
                    c_last_v = op("v", lambda v, jc=jc: v.tensor_copy(Chi[:, jc, :], pdn[:]), "v")
                else:
                    c_last_v = op("v", lambda v, jc=jc: v.tensor_add(Chi[:, jc, :], Chi[:, jc, :], pdn[:]), "v")
                wait("te", "v", c_last_v)

        barrier()

        wq_jobs = [(wg[m * 128:(m + 1) * 128, :],
                    (lambda m: lambda g0, gn: WGb[:, g0:g0 + gn, m * 128:(m + 1) * 128])(m))
                   for m in range(22)]
        wq_jobs += [(wu[m * 128:(m + 1) * 128, :],
                     (lambda m: lambda g0, gn: WUb[:, g0:g0 + gn, m * 128:(m + 1) * 128])(m))
                    for m in range(22)]
        vqm, trqm = [], []
        WQSG = (QSG, EV1)

        def emit_wq_tr(k):
            dstf = wq_jobs[k][1]
            off = 1024 * (k % 2)
            wait("te", "v", vqm[k])
            t_tr = None
            for b in range(8):
                def _tr(te, b=b, off=off):
                    return te.transpose(pT[:, b * 128:(b + 1) * 128],
                                        QOUT[:, off + b * 128:off + (b + 1) * 128],
                                        ident[:])
                if b < 7:
                    op("te", _tr)
                else:
                    t_tr = op("te", _tr, "t")
            trqm.append(t_tr)
            wait("sa", "t", t_tr)
            a_ev = op("sa", lambda s, d=dstf(0, 8):
                      s.copy(d, pT[:].rearrange("p (c n) -> p c n", n=128)), "a")
            wait("te", "a", a_ev)

        for k, (rows, dstf) in enumerate(wq_jobs):
            p = k % 2
            xbuf = XT0 if p == 0 else XT1
            qsg = WQSG[p]
            if k >= 2:
                wait("sy", "v", vqm[k - 2])
            t_ld = dma_in(xbuf, rows)
            wait("sa", "ld", t_ld)
            if k >= 2:
                wait("sa", "v", vqm[k - 2])
            a_sg = op("sa", lambda s, b=xbuf, q=qsg:
                      s.activation(q[:, 0:1024], b, AF.Sign), "a")
            wait("v", "a", a_sg)
            if k >= 2:
                wait("v", "t", trqm[k - 2])
            vqm.append(op("v", emit_quant(xbuf, QOUT[:, 1024 * p:1024 * p + 1024],
                                          1024, qsg=qsg), "v"))
            if k >= 1:
                emit_wq_tr(k - 1)
        emit_wq_tr(len(wq_jobs) - 1)

        prev_v = vqm[-1]
        prev_tr = trqm[-1]

        def wq_tile(rows_src, m, dst_fn, subw):
            nonlocal prev_v, prev_tr
            marks = []
            for s_i, w in enumerate(subw):
                o = sum(subw[:s_i])
                if prev_v is not None:
                    wait("sy", "v", prev_v)
                t_ld = dma_in(XT0[:, 0:w], rows_src[:, o:o + w])
                wait("sa", "ld", t_ld)
                if prev_v is not None:
                    wait("sa", "v", prev_v)
                a_sg = op("sa", lambda s, w=w: s.activation(QSG[:, 0:w], XT0[:, 0:w], AF.Sign), "a")
                wait("v", "ld", t_ld)
                wait("v", "a", a_sg)
                if prev_tr is not None:
                    wait("v", "t", prev_tr)
                v_mk = op("v", emit_quant(XT0[:, 0:w], QOUT[:, o:o + w], w), "v")
                marks.append(v_mk)
                prev_v = v_mk
            nblk = sum(subw) // 128
            wait("te", "v", marks[-1])
            for g0 in range(0, nblk, 8):
                gn = min(8, nblk - g0)
                t_tr = None
                for b in range(gn):
                    def _tr(te, b=b, g0=g0):
                        return te.transpose(pT[:, b * 128:(b + 1) * 128],
                                            QOUT[:, (g0 + b) * 128:(g0 + b + 1) * 128],
                                            ident[:])
                    if b < gn - 1:
                        op("te", _tr)
                    else:
                        t_tr = op("te", _tr, "t")
                prev_tr = t_tr
                wait("sa", "t", t_tr)
                a_ev = op("sa", lambda s, d=dst_fn(m, g0, gn), gn=gn:
                          s.copy(d, pT[:, 0:gn * 128].rearrange("p (c n) -> p c n", n=128)), "a")
                wait("te", "a", a_ev)

        for m in range(8):
            wq_tile(wd[m * 128:(m + 1) * 128, :], m,
                    lambda m, g0, gn: WDb[:, g0:g0 + gn, m * 128:(m + 1) * 128],
                    [1024, 1024, 768])

        barrier()

        def _owp1(v):
            v.tensor_tensor(NEGM64, OWNSUM, OWNSQ, A.subtract)
            v.tensor_scalar(NEGM64, NEGM64, -1024.0, -1.0 / 2048.0, A.add, A.mult)
            return v.tensor_copy(sc1[:, 3:4], c3buf[:])

        v_o1 = op("v", _owp1, "v")
        wait("sa", "v", v_o1)
        a_o1 = op("sa", lambda s: s.activation(MSQ64, NEGM64, AF.Square), "a")
        wait("v", "a", a_o1)

        def _owp2(v):
            v.scalar_tensor_tensor(VARE64, OWNSQ, 1.0 / 1024.0, MSQ64,
                                   A.mult, A.subtract)
            return v.tensor_scalar(VARE64, VARE64, 1e-5, None, A.add)

        v_o2 = op("v", _owp2, "v")
        wait("sa", "v", v_o2)
        a_o2 = op("sa", lambda s: s.activation(SQ64, VARE64, AF.Sqrt), "a")
        wait("v", "a", a_o2)
        v_o3 = op("v", lambda v: v.reciprocal(R064, SQ64), "v")
        wait("sa", "v", v_o3)
        a_o3 = op("sa", lambda s: s.activation(RR64, R064, AF.Square), "a")
        wait("v", "a", a_o3)

        def _owp4(v):
            v.tensor_tensor(B64, RR64, VARE64, A.mult)
            v.tensor_scalar(B64, B64, -0.5, 1.5, A.mult, A.add)
            return v.tensor_tensor(RSTD64, R064, B64, A.mult)

        v_own = op("v", _owp4, "v")

        ln_marks = []
        tr_last = []
        last_st = [None, None]
        PB_QO = (ABF[:, 3072:4096], ABF[:, 8960:9984])
        PB_E1 = (EV1, ABF[:, 9984:11008])
        PB_E2 = (EV2, ABF[:, 11008:12032])
        PB_TS = (TSTG, ABF[:, 12032:13056])
        for t in range(TPC):
            buf = XT0 if t % 2 == 0 else XT1
            p = t % 2
            qo, e1, e2, ts = PB_QO[p], PB_E1[p], PB_E2[p], PB_TS[p]
            if t >= 2:
                wait("sy", "v", ln_marks[t - 2])
            t_ld = dma_in(buf, xb[t * 128:(t + 1) * 128, :])
            wait("v", "ld", t_ld)
            v_ln = op("v", lambda v, b=buf, t=t: v.tensor_scalar(
                XNF, b, NEGM64[:, 32 + t:33 + t], RSTD64[:, 32 + t:33 + t],
                A.add, A.mult), "v")
            ln_marks.append(v_ln)
            wait("sa", "v", v_ln)
            if t >= 2:
                wait("sa", "t", tr_last[t - 2])

            def _ahi(s, e1=e1):
                s.activation(QSG[:, 0:1024], XNF, AF.Sign)
                return s.activation(e1, XNF, AF.Copy)

            a_hi = op("sa", _ahi, "a")
            wait("v", "a", a_hi)
            if t >= 2:
                wait("v", "t", tr_last[t - 2])
            op("v", emit_quant(XNF, qo, 1024))
            v_a = op("v", lambda v, e1=e1, e2=e2: v.scalar_tensor_tensor(
                e2, e1, -1.0, XNF, A.mult, A.add), "v")

            wait("te", "v", v_a)
            first_ev = True
            for src, dstd in ((qo, xqT_d), (e1, xhT_d), (e2, xlT_d)):
                t_tr = None
                for b in range(8):
                    def _tr(te, src=src, b=b):
                        return te.transpose(pT[:, b * 128:(b + 1) * 128],
                                            src[:, b * 128:(b + 1) * 128], ident[:])
                    if b < 7:
                        op("te", _tr)
                    else:
                        t_tr = op("te", _tr, "t")
                wait("sa", "t", t_tr)
                if first_ev and last_st[p] is not None:
                    wait("sa", "st", last_st[p])
                first_ev = False
                a_ev = op("sa", lambda s, ts=ts: s.copy(ts, pT[:]), "a")
                wait("te", "a", a_ev)
                wait("g", "a", a_ev)
                st_mk = dma_out("g", dstd[t], ts.rearrange("p (c n) -> p c n", n=128))
                if dstd is not xlT_d:
                    wait("sa", "st", st_mk)
                else:
                    last_st[p] = st_mk
            tr_last.append(t_tr)

        wait("sy", "st", max(m for m in last_st if m is not None))

        barrier()

        chunks = [(0, 512), (512, 512), (1024, 512), (1536, 512), (2048, 512), (2560, 256)]
        sil_marks, h_marks, hq_marks, tr_marks, gy_marks = [], [], [], [], []
        tile_last_gy = []
        prev_g_st = None
        pending_tr = None
        pending_ag = None

        def emit_tr(gk, io, iw):
            wait("te", "v", hq_marks[gk])
            nb = iw // 128
            tr_mk = None
            for b in range(nb):
                def _tr(te, b=b, io=io):
                    return te.transpose(pT[:, b * 128:(b + 1) * 128],
                                        QOUT[:, io + b * 128: io + (b + 1) * 128],
                                        ident[:])
                if b < nb - 1:
                    op("te", _tr)
                else:
                    tr_mk = op("te", _tr, "t")
            tr_marks.append(tr_mk)
            wait("sa", "t", tr_mk)
            a_ev = op("sa", lambda s, io=io, nb=nb:
                      s.copy(HQT[:, io // 128: io // 128 + nb, :],
                             pT[:, 0:nb * 128].rearrange("p (c n) -> p c n", n=128)), "a")
            wait("te", "a", a_ev)

        for t in range(TPC):
            xq = XQT0 if t % 2 == 0 else XQT1
            if t >= 2:
                wait("sy", "t", tile_last_gy[t - 2])
            t_ld = dma_in(xq, xqT_d[t])
            wait("te", "ld", t_ld)
            for ci_, (io, iw) in enumerate(chunks):
                gk = t * 6 + ci_
                pg = pg0 if gk % 2 == 0 else pg1
                py = py0 if gk % 2 == 0 else py1
                if gk >= 2:
                    wait("te", "a", sil_marks[gk - 2])
                    wait("te", "v", h_marks[gk - 2])
                for cc in range(8):
                    def _mmg(te, cc=cc, io=io, iw=iw, pg=pg, xq=xq,
                             st=(cc == 0), sp=(cc == 7)):
                        return te.matmul(pg[:, 0:iw], xq[:, cc, :],
                                         WGb[:, cc, io:io + iw], start=st, stop=sp)
                    op("te", _mmg)
                gy_mk = None
                for cc in range(8):
                    def _mmy(te, cc=cc, io=io, iw=iw, py=py, xq=xq,
                             st=(cc == 0), sp=(cc == 7)):
                        return te.matmul(py[:, 0:iw], xq[:, cc, :],
                                         WUb[:, cc, io:io + iw], start=st, stop=sp)
                    if cc < 7:
                        op("te", _mmy)
                    else:
                        gy_mk = op("te", _mmy, "t")
                gy_marks.append(gy_mk)
                if pending_tr is not None:
                    emit_tr(*pending_tr)
                    pending_tr = None
                wait("sa", "t", gy_mk)
                if gk >= 1:
                    wait("sa", "v", h_marks[gk - 1])
                sil_marks.append(op("sa", lambda s, pg=pg, iw=iw:
                                    s.activation(SIL[:, 0:iw], pg[:, 0:iw], AF.Silu), "a"))
                wait("v", "a", sil_marks[gk])
                h_marks.append(op("v", lambda v, py=py, iw=iw:
                                  v.tensor_mul(HB[:, 0:iw], SIL[:, 0:iw], py[:, 0:iw]), "v"))
                wait("sa", "v", h_marks[gk])
                if gk >= 1:
                    wait("sa", "v", hq_marks[gk - 1])
                a_sgn = op("sa", lambda s, iw=iw:
                           s.activation(QSG[:, 0:iw], HB[:, 0:iw], AF.Sign), "a")
                wait("v", "a", a_sgn)
                if gk >= 6:
                    wait("v", "t", tr_marks[gk - 6])
                hq_marks.append(op("v", emit_quant(HB[:, 0:iw], QOUT[:, io:io + iw], iw), "v"))
                pending_tr = (gk, io, iw)
            emit_tr(*pending_tr)
            pending_tr = None
            tile_last_gy.append(gy_marks[-1])
            if pending_ag is not None:
                wait("te", "a", pending_ag)
            t_dn = None
            for ic in range(22):
                for half in range(2):
                    def _mmd(te, ic=ic, half=half, st=(ic == 0), sp=(ic == 21)):
                        return te.matmul(pdn[:, half * 512:(half + 1) * 512],
                                         HQT[:, ic, :], WDb[:, ic, half * 512:(half + 1) * 512],
                                         start=st, stop=sp)
                    if ic == 21 and half == 1:
                        t_dn = op("te", _mmd, "t")
                    else:
                        op("te", _mmd)
            wait("sa", "t", t_dn)
            if prev_g_st is not None:
                wait("sa", "st", prev_g_st)
            a_g = op("sa", lambda s: s.activation(GSTGF, pdn[:], AF.Silu), "a")
            wait("g", "a", a_g)
            prev_g_st = dma_out("g", g_d[t], GSTGF)
            pending_ag = a_g

        barrier()

        wait("sy", "st", prev_g_st)
        t_ble = dma_in(BLEB, ble.partition_broadcast(128))
        wait("v", "ld", t_ble)
        Cn["st"] += 16
        _wle_st = Cn["st"]
        em("g", lambda e: e.dma_start(out=WleB[:], in_=wle).then_inc(sSt, 16))
        wait("te", "st", _wle_st)

        def _gmask(g):
            g.memset(maskb[:], 0.0)
            return g.memset(maskb[:, 0:12], 1.0)

        g_mask = op("g", _gmask, "g")
        wait("v", "g", g_mask)
        prev_cb = None
        prev_sty = None
        for t in range(TPC):
            if prev_cb is not None:
                wait("sy", "v", prev_cb)
                wait("sy", "st", prev_sty)
            dma_in(XHT, xhT_d[t])
            t2 = dma_in(XLT, xlT_d[t])
            t3 = dma_in2(XT0, xb[t * 128:(t + 1) * 128, :])
            t4 = dma_in2(GTF, g_d[t])
            wait("te", "ld", t2)
            for half in range(2):
                hsl = slice(half * 512, (half + 1) * 512)
                for cc in range(8):
                    def _mmx(te, cc=cc, hsl=hsl, st=(cc == 0)):
                        return te.matmul(pdn[:, hsl], XHT[:, cc, :],
                                         Chi[:, cc, hsl], start=st, stop=False)
                    op("te", _mmx)
            lat_mms = ([(XHT[:, cc, :], Mhi[:, cc, :]) for cc in range(8)]
                       + [(XHT[:, cc, :], Mlo[:, cc, :]) for cc in range(8)]
                       + [(XLT[:, cc, :], Mhi[:, cc, :]) for cc in range(8)])
            t_lat = None
            for i, (lh, rh) in enumerate(lat_mms):
                def _mml(te, lh=lh, rh=rh, st=(i == 0), sp=(i == 23)):
                    return te.matmul(latP, lh, rh, start=st, stop=sp)
                if i < 23:
                    op("te", _mml)
                else:
                    t_lat = op("te", _mml, "t")
            wait("v", "t", t_lat)
            v_bits = op("v", lambda v: v.tensor_tensor(
                BITSF, pg0[:, 0:64], negb[:], A.is_gt), "v")
            wait("sa", "v", v_bits)
            a_bc = op("sa", lambda s: s.copy(bits[:], BITSF[:, 0:24]), "a")
            wait("te", "a", a_bc)
            t_bt = op("te", lambda te: te.transpose(psA, bits[:], ident[:]), "t")
            wait("sa", "t", t_bt)
            a_bt = op("sa", lambda s: s.copy(bitsT[:], psA), "a")
            wait("te", "a", a_bt)
            t_syn = op("te", lambda te: te.matmul(synP, bitsT[:], HTs[:], start=True, stop=True), "t")
            wait("v", "t", t_syn)

            def _golay(v):
                v.tensor_scalar(synpm, py0[:, 0:64], 0.5, None, A.is_gt)
                v.tensor_reduce(sc1[:, 2:3], synpm[:, 0:12], AX, A.add)
                v.tensor_copy(sc1[:, 3:4], c3buf[:])
                v.tensor_copy(GW, sc1[:, 2:3].broadcast_to([128, 64]))
                v.tensor_scalar(GU, GW, 3.5, None, A.is_lt)
                v.tensor_tensor(GA, synpm, GU, A.mult)
                v.tensor_tensor(GF, GA, maskb, A.mult)
                v.tensor_tensor(GD, BITSF, GF, A.mult)
                v.tensor_tensor(GE, BITSF, GF, A.add)
                return v.scalar_tensor_tensor(CORRF, GD, -2.0, GE, A.mult, A.add)

            v_go = op("v", _golay, "v")
            wait("sa", "v", v_go)

            def _acorr(s):
                s.copy(corr[:, 0:12], CORRF[:, 0:12])
                return s.copy(corr[:, 12:24], BITSF[:, 12:24])

            a_co = op("sa", _acorr, "a")
            wait("te", "a", a_co)
            t_ct = op("te", lambda te: te.transpose(psB, corr[:], ident[:]), "t")
            wait("sa", "t", t_ct)
            a_ct = op("sa", lambda s: s.copy(corrT[:], psB), "a")
            wait("te", "a", a_ct)
            t_wle2 = None
            for half in range(2):
                def _mmw(te, half=half):
                    return te.matmul(pdn[:, half * 512:(half + 1) * 512], corrT[:],
                                     WleB[:, half * 512:(half + 1) * 512],
                                     start=False, stop=True)
                if half == 0:
                    op("te", _mmw)
                else:
                    t_wle2 = op("te", _mmw, "t")
            wait("v", "t", t_wle2)
            wait("v", "ld2", t4)

            def _comb(v):
                v.tensor_add(SSB, pdn[:], BLEB)
                v.tensor_mul(SSB, SSB, GTF)
                return v.tensor_add(OSB, SSB, XT0)

            prev_cb = op("v", _comb, "v")
            wait("te", "v", prev_cb)
            wait("g", "v", prev_cb)
            prev_sty = dma_out("g", yout[t * 128:(t + 1) * 128, :], OSB)

        wait("g", "st", Cn["st"])

        @block.sync
        def _(e):
            for fn in Q["sy"]:
                fn(e)

        @block.vector
        def _(e):
            for fn in Q["v"]:
                fn(e)

        @block.scalar
        def _(e):
            for fn in Q["sa"]:
                fn(e)

        @block.tensor
        def _(e):
            for fn in Q["te"]:
                fn(e)

        @block.gpsimd
        def _(e):
            for fn in Q["g"]:
                fn(e)

    mybir.codegen_inst_isa_subclasses(nc)
    return nc


def kernel(L0_attn_x, L0_attn_W_el, L0_attn_b_el, L0_attn_W_le, L0_attn_b_le,
           L0_ffn_w_gate, L0_ffn_w_up, L0_ffn_w_down):
    x = np.asarray(L0_attn_x)
    xf = np.ascontiguousarray(x.reshape(ROWS, E))
    if "nc" not in _NC_CACHE:
        _NC_CACHE["nc"] = build_program()
    nc = _NC_CACHE["nc"]
    base = {
        "xfull": xf,
        "wg": np.ascontiguousarray(np.asarray(L0_ffn_w_gate)),
        "wu": np.ascontiguousarray(np.asarray(L0_ffn_w_up)),
        "wd": np.ascontiguousarray(np.asarray(L0_ffn_w_down)),
        "wel": np.ascontiguousarray(np.asarray(L0_attn_W_el)),
        "bel": np.ascontiguousarray(np.asarray(L0_attn_b_el).reshape(1, 24)),
        "wle": np.ascontiguousarray(np.asarray(L0_attn_W_le)),
        "ble": np.ascontiguousarray(np.asarray(L0_attn_b_le).reshape(1, E)),
    }
    for name, arr in _CONSTS.items():
        base[name] = arr
    in_maps = []
    for c in range(N_CORES):
        m = dict(base)
        m["xb"] = np.ascontiguousarray(x[c])
        in_maps.append(m)
    res = run_bass_kernel_spmd(nc, in_maps, core_ids=list(range(N_CORES)))
    out = np.stack([res.results[c]["yout"] for c in range(N_CORES)], axis=0)
    return out.astype(np.float32, copy=False)
